# version 17
# baseline (speedup 1.0000x reference)
import numpy as np
from contextlib import ExitStack

import concourse.bass as bass
import concourse.mybir as mybir
from concourse.bass_utils import run_bass_kernel_spmd

F32 = mybir.dt.float32
BF16 = mybir.dt.bfloat16
U8 = mybir.dt.uint8
F32R = mybir.dt.float32r
AX = mybir.AxisListType.X
ALU = mybir.AluOpType
ACT = mybir.ActivationFunctionType

D = 1024
TC = 2048
S = 4096
NH = 16
HD = 64
NC = 8
NT = TC // 512
EPS = 1e-8
NEG = -1.0e9
HIDDEN = 4096
RANK = 16
PREP_N = D * 512 + HIDDEN * 128

N_BLOCKS = 4
DEV_DUMP = False

ENGINES = ("sync", "tensor", "scalar", "vector", "gpsimd")


class Bld:
    def __init__(self, nc, stack):
        self.nc = nc
        self.stack = stack
        self.ops = {e: [] for e in ENGINES}
        self.sem = {}
        self.cnt = {}
        self.nbar = 0

    def new_sem(self, name):
        self.sem[name] = self.stack.enter_context(self.nc.semaphore(name))
        self.cnt[name] = 0

    def add(self, eng, fn):
        self.ops[eng].append(fn)

    def bump(self, name, k=1):
        self.cnt[name] += k
        return self.cnt[name]

    def val(self, name):
        return self.cnt[name]

    def dma(self, eng, out_ap, in_ap, sem):
        tgt = self.bump(sem, 16)
        h = self.sem[sem]
        self.add(eng, lambda e, o=out_ap, i=in_ap: e.dma_start(out=o, in_=i).then_inc(h, 16))
        return tgt

    def wait(self, eng, sem, tgt):
        if tgt <= 0:
            return
        h = self.sem[sem]
        self.add(eng, lambda e, t=tgt: e.wait_ge(h, t))

    def run(self, eng, fn, inc=None, k=1):
        if inc is None:
            self.add(eng, fn)
            return None
        tgt = self.bump(inc, k)
        h = self.sem[inc]
        self.add(eng, lambda e, f=fn, kk=k: f(e).then_inc(h, kk))
        return tgt

    def barrier(self):
        self.nbar += 1
        tgt = self.bump("ph", 5)
        h = self.sem["ph"]
        for eng in ENGINES:
            if eng in ("vector", "gpsimd"):
                self.add(eng, lambda e: e.engine_nop().then_inc(h, 1))
            elif eng == "scalar":
                d = self.dummy
                self.add(eng, lambda e: e.copy(d[0:1, 0:1], d[0:1, 1:2]).then_inc(h, 1))
            elif eng == "tensor":
                self.add(eng, lambda e: e.nop().then_inc(h, 1))
            else:
                self.add(eng, lambda e: e.sem_inc(h, 1))
            self.add(eng, lambda e, t=tgt: e.wait_ge(h, t))


class Ring:
    def __init__(self, bld, n):
        self.b = bld
        self.n = n
        self.last = [None] * n

    def acquire(self, eng, i):
        lu = self.last[i % self.n]
        if lu is not None:
            self.b.wait(eng, lu[0], lu[1])

    def release(self, i, sem, tgt):
        self.last[i % self.n] = (sem, tgt)


class DmaRing:

    _uid = [0]

    def __init__(self, bld, name, depth):
        self.b = bld
        self.depth = depth
        self.names = []
        DmaRing._uid[0] += 1
        for i in range(depth):
            nm = f"{name}{DmaRing._uid[0]}_{i}"
            bld.new_sem(nm)
            self.names.append(nm)
        self.last = [None] * depth
        self.uses = [0] * depth

    def load(self, eng, out_ap, in_ap, i):
        s = i % self.depth
        lu = self.last[s]
        if lu is not None:
            self.b.wait(eng, lu[0], lu[1])
        tgt = self.b.dma(eng, out_ap, in_ap, self.names[s])
        return (self.names[s], tgt)

    def consumed(self, i, sem, tgt):
        self.last[i % self.depth] = (sem, tgt)


class View:

    def __init__(self, pool, th, off, p, w):
        self.pool = pool
        self.th = th
        self.off = off
        self.p = p
        self.w = w

    def __getitem__(self, idx):
        if idx == slice(None):
            return self.th[0:self.p, self.off:self.off + self.w]
        if isinstance(idx, tuple):
            ps, cs = idx
        else:
            ps, cs = idx, slice(None)
        p0 = 0 if ps.start is None else ps.start
        p1 = self.p if ps.stop is None else ps.stop
        c0 = 0 if cs.start is None else cs.start
        c1 = self.w if cs.stop is None else cs.stop
        return self.th[p0:p1, self.off + c0:self.off + c1]

    def free(self):
        self.pool.free(self)


class Pool:

    def __init__(self, nc, nbytes):
        self.arena = nc.alloc_sbuf_tensor("arena", [128, nbytes], U8)
        self.views = {F32: self.arena.bitcast(F32), BF16: self.arena.bitcast(BF16),
                      U8: self.arena, F32R: self.arena.bitcast(F32R)}
        self.free_list = [(0, nbytes)]
        self.nbytes = nbytes
        self.hwm = 0

    verbose = False

    def alloc(self, name, shape, dt=F32):
        p, w = shape
        esz = {F32: 4, F32R: 4, BF16: 2, U8: 1}[dt]
        need = w * esz
        align = 32
        for i, (st, sz) in enumerate(self.free_list):
            a = (st + align - 1) // align * align
            if a + need <= st + sz:
                pre = a - st
                post = (st + sz) - (a + need)
                repl = []
                if pre:
                    repl.append((st, pre))
                if post:
                    repl.append((a + need, post))
                self.free_list[i:i + 1] = repl
                self.hwm = max(self.hwm, a + need)
                if Pool.verbose:
                    print(f"POOL {name}: [{a}, {a+need})")
                v = View(self, self.views[dt], a // esz, p, w)
                v._range = (a, need)
                return v
        raise RuntimeError(f"SBUF pool exhausted allocating {name} "
                           f"({need}B): free={self.free_list}")

    def free(self, v):
        st, sz = v._range
        self.free_list.append((st, sz))
        self.free_list.sort()
        merged = []
        for rng in self.free_list:
            if merged and merged[-1][0] + merged[-1][1] == rng[0]:
                merged[-1] = (merged[-1][0], merged[-1][1] + rng[1])
            else:
                merged.append(list(rng) if isinstance(rng, tuple) else rng)
        self.free_list = [tuple(m) for m in merged]

    def free_all(self, vs):
        for v in vs:
            self.free(v)



def _rope_tables():
    j = np.arange(0, HD, 2, dtype=np.float64)
    inv = 1.0 / (10000.0 ** (j / HD))
    pos = np.arange(S, dtype=np.float64)
    fr = pos[:, None] * inv[None, :]
    cos = np.cos(fr).T
    sin = np.sin(fr).T
    cosf = np.empty((HD, S), np.float64)
    sinS = np.empty((HD, S), np.float64)
    cosf[0::2] = cos
    cosf[1::2] = cos
    sinS[0::2] = -sin
    sinS[1::2] = sin
    return (np.ascontiguousarray(np.tile(cosf, (2, 1)).astype(np.float32)),
            np.ascontiguousarray(np.tile(sinS, (2, 1)).astype(np.float32)))


def _swap_pairs(n):
    idx = np.arange(n)
    idx[0::2] += 1
    idx[1::2] -= 1
    return idx


def _blockdiag(W):
    out = np.zeros((8, 128, 128), np.float32)
    for g in range(128):
        c, loc = g // 16, (g % 16) * 8
        out[c, loc:loc + 8, loc:loc + 8] = W[g].T
    return out


def _cmask4():
    cm = np.zeros((4, 128, 512), np.float32)
    i = np.arange(128)[:, None]
    jj = np.arange(512)[None, :]
    for r in range(4):
        cm[r] = np.where(i + r * 128 <= jj, 0.0, NEG)
    return cm


def stage_inputs(inp):
    x = np.asarray(inp["L0_attn_x"], np.float32)
    cos2, sin2 = _rope_tables()
    sw = _swap_pairs(D)

    sh = {}
    sh["wqT"] = np.ascontiguousarray(np.asarray(inp["L0_attn_wq"], np.float32).T)
    sh["wkT"] = np.ascontiguousarray(np.asarray(inp["L0_attn_wk"], np.float32).T)
    sh["wqswT"] = np.ascontiguousarray(sh["wqT"][:, sw])
    sh["wkswT"] = np.ascontiguousarray(sh["wkT"][:, sw])
    sh["wvT"] = np.ascontiguousarray(np.asarray(inp["L0_attn_wv"], np.float32).T)
    sh["woT"] = np.ascontiguousarray(np.asarray(inp["L0_attn_wo"], np.float32).T)
    sh["WT"] = np.ascontiguousarray(np.asarray(inp["L0_ffn_W"], np.float32).T)
    sh["bcol"] = np.ascontiguousarray(np.asarray(inp["L0_ffn_b"], np.float32).reshape(D, 1))
    sh["Vr"] = np.ascontiguousarray(np.asarray(inp["L0_ffn_V_r"], np.float32))
    sh["CT"] = np.ascontiguousarray(np.asarray(inp["L0_ffn_C"], np.float32).T)
    sh["bdf"] = _blockdiag(np.asarray(inp["L1_attn_Wf"], np.float32))
    sh["bdi"] = _blockdiag(np.asarray(inp["L1_attn_Wi"], np.float32))
    sh["bdh"] = _blockdiag(np.asarray(inp["L1_attn_Wh"], np.float32))
    sh["uT"] = np.ascontiguousarray(np.asarray(inp["L1_ffn_u"], np.float32).T)
    sh["vT"] = np.ascontiguousarray(np.asarray(inp["L1_ffn_v"], np.float32).T)
    sh["fbT"] = np.ascontiguousarray(np.asarray(inp["L1_ffn_fast_b"], np.float32).T)
    sh["gate16"] = np.full((16, 1), np.float32(np.asarray(inp["L1_ffn_gate"])), np.float32)
    sh["eye"] = np.eye(128, dtype=np.float32)
    sh["cmask4"] = _cmask4()
    sh["onescol"] = np.ones((128, 1), np.float32)

    wfc = np.asarray(inp["L1_ffn_w_fc"], np.float32)
    wproj = np.asarray(inp["L1_ffn_w_proj"], np.float32)

    per = []
    for c in range(8):
        seq, half = c // 2, c % 2
        own = slice(half * TC, half * TC + TC)
        par = slice((1 - half) * TC, (1 - half) * TC + TC)
        m = {}
        m["x_own"] = np.ascontiguousarray(x[seq, own].T)
        m["x_par"] = np.ascontiguousarray(x[seq, par].T)
        m["cos_own"] = np.ascontiguousarray(cos2[:, own])
        m["sin_own"] = np.ascontiguousarray(sin2[:, own])
        m["cos_par"] = np.ascontiguousarray(cos2[:, par])
        m["sin_par"] = np.ascontiguousarray(sin2[:, par])
        m["rmask"] = np.full((128, 1), 0.0 if half == 1 else NEG, np.float32)
        cs = np.zeros((8, 1), np.float32)
        if half == 1:
            cs[c - 1, 0] = 1.0
        m["carrysel"] = cs
        m["wfc_sl"] = np.ascontiguousarray(wfc[c * 512:(c + 1) * 512])
        m["wproj_sl"] = np.ascontiguousarray(wproj[c * 128:(c + 1) * 128])
        per.append(m)
    return sh, per



def build_program():
    nc = bass.Bass("TRN2", target_bir_lowering=False)
    dr = {}

    def din(name, shape):
        dr[name] = nc.dram_tensor(name, list(shape), F32, kind="ExternalInput")

    din("x_own", (D, TC)); din("x_par", (D, TC))
    din("cos_own", (128, TC)); din("sin_own", (128, TC))
    din("cos_par", (128, TC)); din("sin_par", (128, TC))
    din("rmask", (128, 1)); din("carrysel", (8, 1))
    for n in ("wqT", "wkT", "wqswT", "wkswT", "wvT", "woT", "WT"):
        din(n, (D, D))
    din("bcol", (D, 1)); din("Vr", (D, 32)); din("CT", (32, D))
    din("bdf", (8, 128, 128)); din("bdi", (8, 128, 128)); din("bdh", (8, 128, 128))
    din("uT", (RANK, HIDDEN)); din("vT", (D, RANK)); din("fbT", (RANK, RANK))
    din("gate16", (RANK, 1))
    din("eye", (128, 128)); din("cmask4", (4, 128, 512))
    din("onescol", (128, 1))
    din("wfc_sl", (512, D)); din("wproj_sl", (128, HIDDEN))

    out = nc.dram_tensor("y", [D, TC], F32, kind="ExternalOutput")
    dbg = {}
    if DEV_DUMP:
        dbg["xp"] = nc.dram_tensor("dbg_xp", [D, TC], F32, kind="ExternalOutput")
        dbg["xp2"] = nc.dram_tensor("dbg_xp2", [D, TC], F32, kind="ExternalOutput")
        dbg["xo"] = nc.dram_tensor("dbg_xo", [D, TC], F32, kind="ExternalOutput")
        dbg["K"] = nc.dram_tensor("dbg_K", [D, S], F32, kind="ExternalOutput")
        dbg["Q"] = nc.dram_tensor("dbg_Q", [D, TC], F32, kind="ExternalOutput")
        dbg["V"] = nc.dram_tensor("dbg_V", [32 * 128, NH * (HD + 1)], F32, kind="ExternalOutput")
        dbg["attn"] = nc.dram_tensor("dbg_attn", [D, TC], F32, kind="ExternalOutput")
        dbg["r2"] = nc.dram_tensor("dbg_r2", [D, TC], F32, kind="ExternalOutput")
        dbg["r3"] = nc.dram_tensor("dbg_r3", [D, TC], F32, kind="ExternalOutput")
        dbg["F"] = nc.dram_tensor("dbg_F", [D, TC], F32, kind="ExternalOutput")
        dbg["x3"] = nc.dram_tensor("dbg_x3", [D, TC], F32, kind="ExternalOutput")
        dbg["r4"] = nc.dram_tensor("dbg_r4", [D, TC], F32, kind="ExternalOutput")
        dbg["prep"] = nc.dram_tensor("dbg_prep", [8, PREP_N], F32, kind="ExternalOutput")
        dbg["prepb"] = nc.dram_tensor("dbg_prepb", [8, PREP_N], BF16, kind="ExternalOutput")
        dbg["x4"] = nc.dram_tensor("dbg_x4", [D, TC], F32, kind="ExternalOutput")
    prep_part = nc.dram_tensor("prep_part", [PREP_N], BF16)
    prep_all = nc.dram_tensor("prep_all", [8, PREP_N], BF16, addr_space="Shared")
    carry_d = nc.dram_tensor("carry_d", [D], F32)
    carry_all = nc.dram_tensor("carry_all", [8, D], F32, addr_space="Shared")
    ropew = nc.dram_tensor("ropew", [4, D, D], BF16)

    with ExitStack() as st0:
        b = Bld(nc, st0)
        for s in ("ph", "lds", "ldg", "lda", "tc", "sc", "vc", "cc", "sts"):
            b.new_sem(s)

        pairs = [st0.enter_context(nc.psum_tensor(f"pp{i}", [128, 1024], F32))
                 for i in range(4)]
        bank = [View(None, pairs[i // 2], (i % 2) * 512, 128, 512) for i in range(8)]
        ones_row_r = nc.alloc_sbuf_tensor("ones_row_r", [1, 128], F32R)
        drow_r = nc.alloc_sbuf_tensor("drow_r", [1, 512], F32R)
        pool = Pool(nc, (nc.sbuf_bytes_remaining - 64) // 32 * 32)

        b.dummy = pool.alloc("dummy", [1, 2], F32)
        I_bf = pool.alloc("I_bf", [128, 128], BF16)
        I_f32 = pool.alloc("I_f32", [128, 128], F32)
        ones_row = pool.alloc("ones_row", [128, 128], F32)
        smalls = pool.alloc("smalls", [128, 8], F32)
        csel_sb = pool.alloc("csel", [8, 1], F32)
        rmask_sb = View(pool, smalls.th, smalls.off + 0, 128, 1)
        ones_col = View(pool, smalls.th, smalls.off + 1, 128, 1)
        b.c_eps1e5 = View(pool, smalls.th, smalls.off + 2, 128, 1)
        b.c_eps = View(pool, smalls.th, smalls.off + 3, 128, 1)
        b.run("vector", lambda e: e.memset(b.c_eps1e5[:], 1e-5))
        b.run("vector", lambda e: e.memset(b.c_eps[:], EPS))


        b.dma("sync", rmask_sb[:], dr["rmask"][:], "lds")
        b.dma("sync", ones_col[:], dr["onescol"][:], "lds")
        b.dma("sync", csel_sb[:], dr["carrysel"][:], "lds")
        b.run("vector", lambda e: e.memset(ones_row[:], 1.0))
        b.wait("vector", "lds", b.val("lds"))
        b.run("vector", lambda e: e.tensor_copy(ones_row_r[:], ones_row[0:1, :]))
        b.ones_row_r = ones_row_r
        b.drow_r = drow_r
        b.dma("gpsimd", I_bf[:], dr["eye"][:], "ldg")
        b.dma("sync", I_f32[:], dr["eye"][:], "lds")
        b.wait("vector", "lds", b.val("lds"))
        b.wait("vector", "ldg", b.val("ldg"))

        _emit_precast(b, dr, ropew, pool)
        if N_BLOCKS >= 4:
            _emit_prep(b, dr, prep_part, prep_all, bank, I_f32, pool)
        I_f32.free()
        b.barrier()

        K_sb = pool.alloc("K", [128, NC * S], BF16)
        V_sb = pool.alloc("V", [128, 32 * NH * (HD + 1)], BF16)

        def dump(view, dst, rows, width):
            if not DEV_DUMP:
                return
            for cc in range(rows // 128):
                b.dma("gpsimd", dst[cc * 128:(cc + 1) * 128, :],
                      view[:, cc * width:(cc + 1) * width], "ldg")
            b.wait("gpsimd", "ldg", b.val("ldg"))
            b.barrier()

        xp = pool.alloc("xp", [128, NC * TC], BF16)
        _emit_ln(b, bank, ones_col, ones_row, pool,
                 src=("hbm", dr["x_par"]), dst=xp, tag="ln0p")
        b.barrier()
        dump(xp, dbg.get("xp2"), D, TC)
        _emit_vproj(b, dr, bank, V_sb, xp, 16, pool)
        b.barrier()
        _emit_rope_proj(b, dr, bank, K_sb, xp, S, TC, pool, ropew, 2, 3,
                        dr["cos_par"], dr["sin_par"], wring=2)
        b.barrier()
        dump(xp, dbg.get("xp"), D, TC)
        xp.free()

        xo = pool.alloc("xo", [128, NC * TC], BF16)
        _emit_ln(b, bank, ones_col, ones_row, pool,
                 src=("hbm", dr["x_own"]), dst=xo, tag="ln0o")
        b.barrier()
        _emit_vproj(b, dr, bank, V_sb, xo, 0, pool)
        b.barrier()
        _emit_rope_proj(b, dr, bank, K_sb, xo, S, 0, pool, ropew, 2, 3,
                        dr["cos_own"], dr["sin_own"], wring=2)
        b.barrier()
        Q_sb = pool.alloc("Q", [128, NC * TC], BF16)
        _emit_rope_proj(b, dr, bank, Q_sb, xo, TC, 0, pool, ropew, 0, 1,
                        dr["cos_own"], dr["sin_own"], wring=1, paired=True)
        b.barrier()
        dump(xo, dbg.get("xo"), D, TC)
        dump(K_sb, dbg.get("K"), D, S)
        dump(Q_sb, dbg.get("Q"), D, TC)
        dump(V_sb, dbg.get("V"), 32 * 128, NH * (HD + 1))
        xo.free()

        attn_sb = pool.alloc("attn", [128, NC * TC], BF16)
        _emit_attention(b, dr, bank, K_sb, V_sb, Q_sb, attn_sb, I_bf,
                        rmask_sb, ones_row, pool)
        b.barrier()
        dump(attn_sb, dbg.get("attn"), D, TC)
        pool.free_all([K_sb, V_sb, Q_sb])

        resid = pool.alloc("resid", [128, NC * TC], F32)
        _emit_oproj(b, dr, bank, attn_sb, resid, pool)
        b.barrier()
        attn_sb.free()

        if N_BLOCKS >= 2:
            _emit_ffn0(b, dr, bank, resid, ones_col, ones_row, pool)
            b.barrier()
        if N_BLOCKS >= 3:
            if DEV_DUMP:
                for cc in range(NC):
                    b.dma("sync", dbg["r2"][cc * 128:(cc + 1) * 128, :],
                          resid[:, cc * TC:(cc + 1) * TC], "sts")
                b.wait("sync", "sts", b.val("sts"))
                b.barrier()
            _emit_l1attn(b, dr, bank, pairs, resid, ones_col, ones_row, csel_sb,
                         carry_d, carry_all, pool, dbg)
            b.barrier()
            if DEV_DUMP:
                for cc in range(NC):
                    b.dma("sync", dbg["r3"][cc * 128:(cc + 1) * 128, :],
                          resid[:, cc * TC:(cc + 1) * TC], "sts")
                b.wait("sync", "sts", b.val("sts"))
                b.barrier()
        if N_BLOCKS >= 4:
            _emit_ffn1(b, dr, bank, resid, ones_col, ones_row, prep_all, pool)
            b.barrier()

        for c in range(NC):
            b.dma("sync", out[c * 128:(c + 1) * 128, :],
                  resid[:, c * TC:(c + 1) * TC], "sts")
        b.wait("sync", "sts", b.val("sts"))

        with nc.Block() as block:
            @block.sync
            def _(e):
                for f in b.ops["sync"]:
                    f(e)

            @block.tensor
            def _(e):
                for f in b.ops["tensor"]:
                    f(e)

            @block.scalar
            def _(e):
                for f in b.ops["scalar"]:
                    f(e)

            @block.vector
            def _(e):
                for f in b.ops["vector"]:
                    f(e)

            @block.gpsimd
            def _(e):
                for f in b.ops["gpsimd"]:
                    f(e)

    return nc



def _emit_precast(b, dr, ropew, pool):
    stg = pool.alloc("pc_stg", [128, 8 * D], BF16)
    names = ("wqT", "wqswT", "wkT", "wkswT")
    for i, nm in enumerate(names):
        lg = b.dma("gpsimd", stg[:].rearrange("p (c n) -> p c n", c=8),
                   dr[nm][:].rearrange("(c p) n -> p c n", p=128), "ldg")
        b.wait("sync", "ldg", lg)
        st = b.dma("sync", ropew[i].rearrange("(c p) n -> p c n", p=128),
                   stg[:].rearrange("p (c n) -> p c n", c=8), "lds")
        b.wait("gpsimd", "lds", st)
    b.wait("sync", "lds", b.val("lds"))
    stg.free()



def _emit_prep(b, dr, prep_part, prep_all, bank, I_f32, pool):
    wraw = pool.alloc("pr_raw", [128, HIDDEN], F32)
    scth = pool.alloc("pr_scth", [128, 3], F32)
    gt = pool.alloc("pr_gt", [128, HIDDEN], F32)
    lt = pool.alloc("pr_lt", [128, HIDDEN], F32)
    stg = [pool.alloc(f"pr_stg{i}", [128, 128], BF16) for i in range(2)]
    sc_ = View(pool, scth.th, scth.off + 0, 128, 1)
    th = View(pool, scth.th, scth.off + 1, 128, 1)
    thn = View(pool, scth.th, scth.off + 2, 128, 1)
    rawd = DmaRing(b, "prraw", 1)
    stgd = DmaRing(b, "prstg", 2)
    bank_ring = Ring(b, 2)

    fcv = prep_part[:].rearrange("(d n) -> d n", n=512)
    woff = D * 512
    step = [0]
    prev_tt = [0]

    def prep_tile(src_ap, width, dst_fn, nblk):
        rsem, t_ld = rawd.load("sync", wraw[:, 0:width], src_ap, 0)
        b.wait("vector", rsem, t_ld)
        b.wait("vector", "tc", prev_tt[0])
        tr_ = b.run("vector", lambda e, w=width: e.reduce_sum(
            sc_[:], wraw[:, 0:w], axis=AX, apply_absolute_value=True), inc="vc")
        b.wait("scalar", "vc", tr_)
        ts_ = b.run("scalar", lambda e, w=width: e.mul(sc_[:], sc_[:], 1.0 / w),
                    inc="sc")
        b.wait("vector", "sc", ts_)
        tm_ = b.run("vector", lambda e: e.tensor_max(sc_[:], sc_[:], b.c_eps1e5[:]),
                    inc="vc")
        b.wait("scalar", "vc", tm_)
        b.run("scalar", lambda e: e.mul(th[:], sc_[:], 0.5))
        tg_ = b.run("scalar", lambda e: e.mul(thn[:], sc_[:], -0.5), inc="sc")
        b.wait("vector", "sc", tg_)
        b.run("vector", lambda e, w=width: e.tensor_scalar(
            gt[:, 0:w], wraw[:, 0:w], th[:], None, op0=ALU.is_gt))
        b.run("vector", lambda e, w=width: e.tensor_scalar(
            lt[:, 0:w], wraw[:, 0:w], thn[:], None, op0=ALU.is_lt))
        b.run("vector", lambda e, w=width: e.tensor_sub(gt[:, 0:w], gt[:, 0:w], lt[:, 0:w]))
        tv = b.run("vector", lambda e, w=width: e.tensor_scalar(
            gt[:, 0:w], gt[:, 0:w], sc_[:], None, op0=ALU.mult), inc="vc")
        rawd.consumed(0, "vc", tv)
        b.wait("tensor", "vc", tv)
        for j in range(nblk):
            i = step[0]; step[0] += 1
            bank_ring.acquire("tensor", i)
            tt = b.run("tensor", lambda e, jj=j, bi=i % 2: e.transpose(
                bank[bi][:, 0:128], gt[:, jj * 128:(jj + 1) * 128], I_f32[:]),
                inc="tc")
            b.wait("scalar", "tc", tt)
            lu = stgd.last[i % 2]
            if lu is not None:
                b.wait("scalar", lu[0], lu[1])
            ts = b.run("scalar", lambda e, bi=i % 2, si=i % 2: e.copy(
                stg[si][:], bank[bi][:, 0:128]), inc="sc")
            bank_ring.release(i, "sc", ts)
            b.wait("sync", "sc", ts)
            t_st = b.dma("sync", dst_fn(j), stg[i % 2][:], stgd.names[i % 2])
            stgd.uses[i % 2] += 1
            stgd.last[i % 2] = (stgd.names[i % 2], t_st)
            prev_tt[0] = tt

    for t in range(4):
        def dst_fc(j, tt=t):
            return fcv[j * 128:(j + 1) * 128, tt * 128:(tt + 1) * 128]
        prep_tile(dr["wfc_sl"][t * 128:(t + 1) * 128, :], D, dst_fc, 8)

    def dst_pr(j):
        return prep_part[woff + j * 128 * 128: woff + (j + 1) * 128 * 128].rearrange(
            "(p n) -> p n", p=128)
    prep_tile(dr["wproj_sl"][:], HIDDEN, dst_pr, 32)

    b.wait("gpsimd", "lds", b.val("lds"))
    for nm in stgd.names:
        b.wait("gpsimd", nm, b.val(nm))
    b.run("gpsimd", lambda e: e.collective_compute(
        "AllGather", ALU.bypass, replica_groups=[list(range(8))],
        ins=[prep_part[:]], outs=[prep_all[:]]), inc="cc")
    pool.free_all([wraw, scth, gt, lt, stg[0], stg[1]])



def _emit_ln(b, bank, ones_col, ones_row, pool, src, dst, tag):
    kind, sh = src
    scratch = []

    def al(n, shp, dt=F32):
        v = pool.alloc(f"{tag}_{n}", shp, dt)
        scratch.append(v)
        return v

    xin = [al(f"x{i}", [128, 512]) for i in range(2)] if kind == "hbm" else None
    xdr = DmaRing(b, f"{tag}xd", 2) if kind == "hbm" else None
    xsq = al("q0", [128, 512])
    fin_m = al("m", [1, 512])
    fin_ms = al("ms", [1, 512])
    fin_v = al("v", [1, 512])
    fin_r = al("r", [1, 512])
    fin_mr = al("mr", [1, 512])
    tmp = al("t", [128, 512])
    qring = Ring(b, 1)

    SB, QB, RB, MB = bank[4], bank[5], bank[6], bank[7]

    def src_tile(c, t):
        if kind == "hbm":
            return None
        return sh[:, c * TC + t * 512: c * TC + (t + 1) * 512]

    for t in range(NT):
        for c in range(NC):
            if kind == "hbm":
                lsem, tl = xdr.load("sync", xin[c % 2][:],
                                    sh[c * 128:(c + 1) * 128, t * 512:(t + 1) * 512], c)
                b.wait("scalar", lsem, tl)
                b.wait("tensor", lsem, tl)
                src_ap = xin[c % 2][:]
            else:
                src_ap = src_tile(c, t)
            qring.acquire("scalar", c)
            ts = b.run("scalar", lambda e, a=src_ap: e.activation(
                xsq[:], a, ACT.Square), inc="sc")
            b.run("tensor", lambda e, a=src_ap, first=(c == 0): e.matmul(
                SB[0:1, :], ones_col[:], a, start=first, stop=(c == NC - 1)))
            b.wait("tensor", "sc", ts)
            tq = b.run("tensor", lambda e, first=(c == 0): e.matmul(
                QB[0:1, :], ones_col[:], xsq[:], start=first, stop=(c == NC - 1)),
                inc="tc")
            qring.release(c, "tc", tq)
            if kind == "hbm":
                xdr.consumed(c, "tc", tq)
        b.wait("vector", "tc", b.val("tc"))
        b.run("vector", lambda e: e.tensor_scalar(fin_m[:], SB[0:1, :], 1.0 / D,
                                                  None, op0=ALU.mult))
        b.run("vector", lambda e: e.tensor_mul(fin_ms[:], fin_m[:], fin_m[:]))
        tv = b.run("vector", lambda e: e.scalar_tensor_tensor(
            fin_v[:], QB[0:1, :], 1.0 / D, fin_ms[:], op0=ALU.mult,
            op1=ALU.subtract), inc="vc")
        b.wait("scalar", "vc", tv)
        b.run("scalar", lambda e: e.activation(fin_v[:], fin_v[:], ACT.Ln,
                                               bias=b.c_eps1e5[0:1, :]))
        ts2 = b.run("scalar", lambda e: e.activation(fin_r[:], fin_v[:], ACT.Exp,
                                                     scale=-0.5), inc="sc")
        b.wait("vector", "sc", ts2)
        tv2 = b.run("vector", lambda e: e.tensor_mul(fin_mr[:], fin_m[:], fin_r[:]),
                    inc="vc")
        b.wait("tensor", "vc", tv2)
        b.run("tensor", lambda e: e.matmul(RB[:, :], ones_row[0:1, :], fin_r[:],
                                           start=True, stop=True))
        tb = b.run("tensor", lambda e: e.matmul(MB[:, :], ones_row[0:1, :], fin_mr[:],
                                                start=True, stop=True), inc="tc")
        b.wait("vector", "tc", tb)
        for c in range(NC):
            if kind == "hbm":
                lsem, tl = xdr.load("sync", xin[c % 2][:],
                                    sh[c * 128:(c + 1) * 128, t * 512:(t + 1) * 512], c)
                b.wait("vector", lsem, tl)
                src_ap = xin[c % 2][:]
            else:
                src_ap = src_tile(c, t)
            b.run("vector", lambda e, a=src_ap: e.tensor_mul(tmp[:], a, RB[:, :]))
            tv3 = b.run("vector", lambda e, c2=c, t2=t: e.tensor_sub(
                dst[:, c2 * TC + t2 * 512: c2 * TC + (t2 + 1) * 512],
                tmp[:], MB[:, :]), inc="vc")
            if kind == "hbm":
                xdr.consumed(c, "vc", tv3)
        b.wait("tensor", "vc", b.val("vc"))
    pool.free_all(scratch)



def _emit_vproj(b, dr, bank, V_sb, xln, kt0, pool):
    wv = pool.alloc("wv_sb", [128, 8 * D], BF16)
    tl = b.dma("gpsimd", wv[:].rearrange("p (c n) -> p c n", c=8),
               dr["wvT"][:].rearrange("(c p) n -> p c n", p=128), "ldg")
    b.wait("tensor", "ldg", tl)
    for kt in range(kt0, kt0 + 16):
        b.run("vector", lambda e, t=kt: e.memset(
            V_sb[:, t * 1040:(t + 1) * 1040].rearrange(
                "p (h n) -> p h n", h=NH)[:, :, HD:HD + 1], 1.0))
    vdone = b.run("vector", lambda e: e.engine_nop(), inc="vc")
    b.wait("scalar", "vc", vdone)

    bring = Ring(b, 4)
    for ktl in range(16):
        kt = kt0 + ktl
        for eh in range(2):
            i = ktl * 2 + eh
            bk = bank[i % 4]
            bring.acquire("tensor", i)
            for dc in range(NC):
                b.run("tensor", lambda e, t=ktl, h=eh, d=dc, bb=bk: e.matmul(
                    bb[:, :], xln[:, d * TC + t * 128: d * TC + (t + 1) * 128],
                    wv[:, d * D + h * 512: d * D + (h + 1) * 512],
                    start=(d == 0), stop=(d == NC - 1)),
                    inc=("tc" if dc == NC - 1 else None))
            tt = b.val("tc")
            b.wait("scalar", "tc", tt)
            ts = b.run("scalar", lambda e, k=kt, h=eh, bb=bk: e.copy(
                V_sb[:, k * 1040:(k + 1) * 1040].rearrange(
                    "p (hh n) -> p hh n", hh=NH)[:, h * 8:(h + 1) * 8, 0:HD],
                bb[:].rearrange("p (hh n) -> p hh n", hh=8)), inc="sc")
            bring.release(i, "sc", ts)
    wv.free()



def _emit_rope_proj(b, dr, bank, dst_sb, xln, dst_span, coff, pool, ropew,
                    wi_a, wi_b, cosd, sind, wring, paired=None):
    if wring < 2 and not paired:
        _emit_rope_proj_single(b, dr, bank, dst_sb, xln, dst_span, coff, pool,
                               ropew, wi_a, wi_b, cosd, sind)
        return
    tdt = F32 if wring >= 2 else BF16
    teng = "sync" if wring >= 2 else "gpsimd"
    tab = pool.alloc("rp_tab", [128, 4 * 512], tdt)
    wsl = [pool.alloc(f"rp_w{i}", [128, 2 * 8 * 128], BF16) for i in range(wring)]
    t1 = pool.alloc("rp_t1", [128, 512], BF16)

    tdr = DmaRing(b, f"rt{wi_a}", 1)
    wdr = DmaRing(b, f"rw{wi_a}", wring)
    pair_ring = Ring(b, 2)
    step = [0]

    for tp in range(NT // 2):
        t0 = 2 * tp
        tdr.load(teng, tab[:, 0:512], cosd[:, t0 * 512:(t0 + 1) * 512], tp)
        ls = None
        for k, srcap in enumerate((sind[:, t0 * 512:(t0 + 1) * 512],
                                   cosd[:, (t0 + 1) * 512:(t0 + 2) * 512],
                                   sind[:, (t0 + 1) * 512:(t0 + 2) * 512])):
            ls = b.dma(teng, tab[:, (k + 1) * 512:(k + 2) * 512], srcap,
                       tdr.names[0])
        b.wait("vector", tdr.names[0], ls)
        last_tv = None
        for ec in range(NC):
            i = step[0]; step[0] += 1
            wv = wsl[i % wring]
            wdr.load("sync", wv[:, 0:1024].rearrange("p (c n) -> p c n", c=8),
                     ropew[wi_a][:, ec * 128:(ec + 1) * 128].rearrange(
                         "(c p) n -> p c n", p=128), i)
            lw = b.dma("sync", wv[:, 1024:2048].rearrange("p (c n) -> p c n", c=8),
                       ropew[wi_b][:, ec * 128:(ec + 1) * 128].rearrange(
                           "(c p) n -> p c n", p=128), wdr.names[i % wring])
            b.wait("tensor", wdr.names[i % wring], lw)
            bs = [bank[(i % 2) * 4 + j] for j in range(4)]
            pair_ring.acquire("tensor", i)
            for dc in range(NC):
                for q in range(2):
                    b.run("tensor", lambda e, d=dc, bb=bs[q], w=wv, tt=t0 + q:
                          e.matmul(
                        bb[:, :], w[:, d * 128:(d + 1) * 128],
                        xln[:, d * TC + tt * 512: d * TC + (tt + 1) * 512],
                        start=(d == 0), stop=(d == NC - 1)))
            tcb = []
            for dc in range(NC):
                for q in range(2):
                    last = (dc == NC - 1)
                    tgt = b.run("tensor", lambda e, d=dc, bb=bs[2 + q], w=wv,
                                tt=t0 + q: e.matmul(
                        bb[:, :], w[:, 1024 + d * 128: 1024 + (d + 1) * 128],
                        xln[:, d * TC + tt * 512: d * TC + (tt + 1) * 512],
                        start=(d == 0), stop=last),
                        inc=("tc" if last else None))
                    if last:
                        tcb.append(tgt)
            for q in range(2):
                b.wait("vector", "tc", tcb[q])
                b.run("vector", lambda e, bb=bs[2 + q], q2=q: e.tensor_mul(
                    t1[:], bb[:, :], tab[:, (2 * q2 + 1) * 512:(2 * q2 + 2) * 512]))
                b.run("vector", lambda e, bb=bs[q], q2=q: e.tensor_mul(
                    bb[:, :], bb[:, :], tab[:, 2 * q2 * 512:(2 * q2 + 1) * 512]))
                tv = b.run("vector", lambda e, c=ec, tt3=t0 + q, bb=bs[q],
                           co=coff: e.tensor_add(
                    dst_sb[:, c * dst_span + co + tt3 * 512:
                           c * dst_span + co + (tt3 + 1) * 512],
                    bb[:, :], t1[:]), inc="vc")
            pair_ring.release(i, "vc", tv)
            wdr.consumed(i, "vc", tv)
            last_tv = tv
        tdr.consumed(tp, "vc", last_tv)
    pool.free_all([tab, t1] + wsl)


def _emit_rope_proj_single(b, dr, bank, dst_sb, xln, dst_span, coff, pool,
                           ropew, wi_a, wi_b, cosd, sind):
    tab = pool.alloc("rp_tab0", [128, 1024], F32)
    wsl = [pool.alloc("rp_w0", [128, 2 * 8 * 128], BF16)]
    t1 = pool.alloc("rp_t1", [128, 512], F32)

    tdr = DmaRing(b, f"rt{wi_a}", 1)
    wdr = DmaRing(b, f"rw{wi_a}", 1)
    pair_ring = Ring(b, 2)
    step = [0]

    for t in range(NT):
        tdr.load("sync", tab[:, 0:512], cosd[:, t * 512:(t + 1) * 512], t)
        ls = b.dma("sync", tab[:, 512:1024], sind[:, t * 512:(t + 1) * 512],
                   tdr.names[0])
        b.wait("vector", tdr.names[0], ls)
        last_tv = None
        for ec in range(NC):
            i = step[0]; step[0] += 1
            wv = wsl[0]
            wdr.load("sync", wv[:, 0:1024].rearrange("p (c n) -> p c n", c=8),
                     ropew[wi_a][:, ec * 128:(ec + 1) * 128].rearrange(
                         "(c p) n -> p c n", p=128), i)
            lw = b.dma("sync", wv[:, 1024:2048].rearrange("p (c n) -> p c n", c=8),
                       ropew[wi_b][:, ec * 128:(ec + 1) * 128].rearrange(
                           "(c p) n -> p c n", p=128), wdr.names[0])
            b.wait("tensor", wdr.names[0], lw)
            bA = bank[(i % 2) * 2]
            bB = bank[(i % 2) * 2 + 1]
            pair_ring.acquire("tensor", i)
            for dc in range(NC):
                b.run("tensor", lambda e, d=dc, bb=bA, w=wv, tt=t: e.matmul(
                    bb[:, :], w[:, d * 128:(d + 1) * 128],
                    xln[:, d * TC + tt * 512: d * TC + (tt + 1) * 512],
                    start=(d == 0), stop=(d == NC - 1)))
            for dc in range(NC):
                b.run("tensor", lambda e, d=dc, bb=bB, w=wv, tt=t: e.matmul(
                    bb[:, :], w[:, 1024 + d * 128: 1024 + (d + 1) * 128],
                    xln[:, d * TC + tt * 512: d * TC + (tt + 1) * 512],
                    start=(d == 0), stop=(d == NC - 1)),
                    inc=("tc" if dc == NC - 1 else None))
            tt2 = b.val("tc")
            b.wait("vector", "tc", tt2)
            b.run("vector", lambda e, bb=bB: e.tensor_mul(
                t1[:], bb[:, :], tab[:, 512:1024]))
            b.run("vector", lambda e, bb=bA: e.tensor_mul(
                bb[:, :], bb[:, :], tab[:, 0:512]))
            tv = b.run("vector", lambda e, c=ec, tt3=t, bb=bA, co=coff: e.tensor_add(
                dst_sb[:, c * dst_span + co + tt3 * 512:
                       c * dst_span + co + (tt3 + 1) * 512],
                bb[:, :], t1[:]), inc="vc")
            pair_ring.release(i, "vc", tv)
            wdr.consumed(i, "vc", tv)
            last_tv = tv
        tdr.consumed(t, "vc", last_tv)
    pool.free_all([tab, t1] + wsl)



def _emit_attention(b, dr, bank, K_sb, V_sb, Q_sb, attn_sb, I_bf,
                    rmask_sb, ones_row, pool):
    cmask_bf = pool.alloc("cmask", [128, 4 * 512], BF16)
    expP = [pool.alloc(f"at_e{i}", [128, 512], BF16) for i in range(4)]
    drow = b.drow_r
    bc_sb = [pool.alloc("at_bc0", [64, 512], F32)]
    lg = b.dma("gpsimd", cmask_bf[:].rearrange("p (r n) -> p r n", r=4),
               dr["cmask4"][:].rearrange("r p n -> p r n"), "ldg")
    b.wait("tensor", "ldg", lg)

    sco = [bank[0], bank[1], bank[2], bank[3]]
    for i in range(120):
        b.run("tensor", lambda e, bb=sco[i % 4]: e.matmul(
            bb[:, :], I_bf[:], cmask_bf[:, 0:512], start=True, stop=True))

    avb = [bank[4], bank[6]]
    bcb = [bank[5], bank[7]]
    sco_ring = Ring(b, 4)
    exp_ring = Ring(b, 4)
    av_ring = Ring(b, 2)
    bc_ring = Ring(b, 2)
    bcs_ring = Ring(b, 1)

    hq = [0]
    tb_prev = [0]
    pending = [None]
    for qt in range(NT):
        for h in range(NH):
            ec, rh = h // 2, (h % 2) * 64
            i_h = hq[0]; hq[0] += 1
            av = avb[i_h % 2]
            kts = list(range(0, 4 * (qt + 1))) + list(range(16, 32))
            nk = len(kts)
            exp_t = {}

            def emit_score(j):
                kt = kts[j]
                bk = sco[j % 4]
                diag = (kt < 16 and kt >= 4 * qt)
                sco_ring.acquire("tensor", j)
                if diag:
                    rel = kt - 4 * qt
                    b.run("tensor", lambda e, bb=bk, r=rel: e.matmul(
                        bb[:, :], I_bf[:], cmask_bf[:, r * 512:(r + 1) * 512],
                        start=True, stop=False))
                tsc = b.run("tensor", lambda e, bb=bk, k=kt, c=ec, r=rh, q=qt,
                            dg=diag: e.matmul(
                    bb[:, :],
                    K_sb[r:r + 64, c * S + k * 128: c * S + (k + 1) * 128],
                    Q_sb[r:r + 64, c * TC + q * 512: c * TC + (q + 1) * 512],
                    start=(not dg), stop=True), inc="tc")
                b.wait("scalar", "tc", tsc)
                exp_ring.acquire("scalar", j)
                rect = kt >= 16
                te = b.run("scalar", lambda e, bb=bk, sl=j % 4, rc=rect: e.activation(
                    expP[sl][:], bb[:, :], ACT.Exp,
                    bias=(rmask_sb[:] if rc else 0.0), scale=0.125), inc="sc")
                exp_t[j] = te
                sco_ring.release(j, "sc", te)

            def emit_av(j):
                b.wait("tensor", "sc", exp_t[j])
                if j == 0:
                    av_ring.acquire("tensor", i_h)
                ta = b.run("tensor", lambda e, k=kts[j], hh=h, sl=j % 4,
                           aa=av: e.matmul(
                    aa[0:65, :],
                    V_sb[:, k * 1040 + hh * 65: k * 1040 + (hh + 1) * 65],
                    expP[sl][:],
                    start=(j == 0), stop=(j == nk - 1)), inc="tc")
                exp_ring.release(j, "tc", ta)
                return ta

            for j in range(4):
                emit_score(j)
            for j in range(4, nk):
                emit_av(j - 4)
                if j == 10 and pending[0] is not None:
                    pending[0]()
                    pending[0] = None
                emit_score(j)
            for j in range(nk - 4, nk - 1):
                emit_av(j)
            ta = emit_av(nk - 1)

            b.wait("vector", "tc", max(ta, tb_prev[0]))

            def _recip(e, aa):
                with b.nc.allow_low_precision(reason="float32r is fp32"):
                    return e.reciprocal(drow[0:1, :], aa[64:65, :])
            trc = b.run("vector", lambda e, aa=av, f=_recip: f(e, aa), inc="vc")

            def tail(ih=i_h, aa=av, c=ec, r=rh, q=qt, trc=trc):
                bc_ring.acquire("tensor", ih)
                b.wait("tensor", "vc", trc)
                tb = b.run("tensor", lambda e, bb=bcb[ih % 2]: e.matmul(
                    bb[0:64, :], b.ones_row_r[0:1, 0:64], drow[0:1, :],
                    start=True, stop=True), inc="tc")
                tb_prev[0] = tb
                bcs_ring.acquire("vector", ih)
                b.wait("vector", "tc", tb)
                tcp = b.run("vector", lambda e, bb=bcb[ih % 2], s2=bc_sb[0]:
                            e.tensor_copy(s2[:], bb[0:64, :]), inc="vc")
                bc_ring.release(ih, "vc", tcp)
                tn = b.run("vector", lambda e, s2=bc_sb[0]: e.tensor_mul(
                    attn_sb[r:r + 64, c * TC + q * 512: c * TC + (q + 1) * 512],
                    aa[0:64, :], s2[:]), inc="vc")
                av_ring.release(ih, "vc", tn)
                bcs_ring.release(ih, "vc", tn)

            pending[0] = tail
    pending[0]()
    pending[0] = None
    pool.free_all([cmask_bf] + expP + bc_sb)



def _emit_oproj(b, dr, bank, attn_sb, resid, pool):
    wo = pool.alloc("wo_sb", [128, 8 * D], BF16)
    xbuf = [pool.alloc(f"ox{i}", [128, 512], F32) for i in range(4)]
    tl = b.dma("gpsimd", wo[:].rearrange("p (c n) -> p c n", c=8),
               dr["woT"][:].rearrange("(c p) n -> p c n", p=128), "ldg")
    b.wait("tensor", "ldg", tl)
    bring = Ring(b, 2)
    xdr = DmaRing(b, "oprx", 4)
    for ec in range(NC):
        bs = [bank[(ec % 2) * 4 + j] for j in range(4)]
        xs = []
        for t in range(NT):
            xs.append(xdr.load("sync", xbuf[t][:],
                               dr["x_own"][ec * 128:(ec + 1) * 128,
                                           t * 512:(t + 1) * 512], ec * NT + t))
        bring.acquire("tensor", ec)
        tcb = []
        for dc in range(NC):
            for t in range(NT):
                last = (dc == NC - 1)
                tgt = b.run("tensor", lambda e, d=dc, c=ec, tt=t, bb=bs[t]:
                            e.matmul(
                    bb[:, :], wo[:, d * D + c * 128: d * D + (c + 1) * 128],
                    attn_sb[:, d * TC + tt * 512: d * TC + (tt + 1) * 512],
                    start=(d == 0), stop=(d == NC - 1)),
                    inc=("tc" if last else None))
                if last:
                    tcb.append(tgt)
        tv = None
        for t in range(NT):
            b.wait("vector", "tc", tcb[t])
            b.wait("vector", xs[t][0], xs[t][1])
            tv = b.run("vector", lambda e, c=ec, tt3=t, bb=bs[t], xb=xbuf[t]:
                       e.tensor_add(resid[:, c * TC + tt3 * 512:
                                          c * TC + (tt3 + 1) * 512],
                                    xb[:], bb[:, :]), inc="vc")
            xdr.consumed(ec * NT + t, "vc", tv)
        bring.release(ec, "vc", tv)
    pool.free_all([wo] + xbuf)



def _emit_ffn0(b, dr, bank, resid, ones_col, ones_row, pool):
    xln2 = pool.alloc("xln2", [128, NC * TC], BF16)
    _emit_ln(b, bank, ones_col, ones_row, pool,
             src=("sbuf", resid), dst=xln2, tag="ln1")
    b.barrier()

    wsb = pool.alloc("W_sb", [128, 8 * D], BF16)
    vr = pool.alloc("vr_sb", [128, 8 * 32], BF16)
    ct = pool.alloc("ct_sb", [32, D], BF16)
    bsb = pool.alloc("b_sb", [128, 8], F32)
    r1 = pool.alloc("r1_sb", [32, 4 * 512], BF16)
    b.dma("gpsimd", wsb[:].rearrange("p (c n) -> p c n", c=8),
          dr["WT"][:].rearrange("(c p) n -> p c n", p=128), "ldg")
    b.dma("gpsimd", vr[:].rearrange("p (c n) -> p c n", c=8),
          dr["Vr"][:].rearrange("(c p) n -> p c n", p=128), "ldg")
    b.dma("gpsimd", ct[:], dr["CT"][:], "ldg")
    for c in range(NC):
        b.dma("sync", bsb[:, c:c + 1], dr["bcol"][c * 128:(c + 1) * 128, :], "lds")
    b.wait("tensor", "ldg", b.val("ldg"))
    b.wait("vector", "lds", b.val("lds"))

    ts = None
    for t in range(NT):
        rb = bank[6]
        for dc in range(NC):
            b.run("tensor", lambda e, d=dc, tt=t: e.matmul(
                rb[0:32, :], vr[:, d * 32:(d + 1) * 32],
                xln2[:, d * TC + tt * 512: d * TC + (tt + 1) * 512],
                start=(d == 0), stop=(d == NC - 1)),
                inc=("tc" if dc == NC - 1 else None))
        tt1 = b.val("tc")
        b.wait("scalar", "tc", tt1)
        ts = b.run("scalar", lambda e, tt=t: e.copy(
            r1[:, tt * 512:(tt + 1) * 512], rb[0:32, :]), inc="sc")
        b.wait("tensor", "sc", ts)

    bring = Ring(b, 2)
    for ec in range(NC):
        bs = [bank[(ec % 2) * 4 + j] for j in range(4)]
        bring.acquire("tensor", ec)
        for t in range(NT):
            b.run("tensor", lambda e, c=ec, tt=t, bb=bs[t]: e.matmul(
                bb[:, :], ct[:, c * 128:(c + 1) * 128],
                r1[:, tt * 512:(tt + 1) * 512],
                start=True, stop=False))
        tcb = []
        for dc in range(NC):
            for t in range(NT):
                last = (dc == NC - 1)
                tgt = b.run("tensor", lambda e, d=dc, c=ec, tt2=t, bb=bs[t]:
                            e.matmul(
                    bb[:, :], wsb[:, d * D + c * 128: d * D + (c + 1) * 128],
                    xln2[:, d * TC + tt2 * 512: d * TC + (tt2 + 1) * 512],
                    start=False, stop=(d == NC - 1)),
                    inc=("tc" if last else None))
                if last:
                    tcb.append(tgt)
        tv = None
        for t in range(NT):
            b.wait("vector", "tc", tcb[t])
            tv = b.run("vector", lambda e, c=ec, tt4=t, bb=bs[t]:
                       e.scalar_tensor_tensor(
                resid[:, c * TC + tt4 * 512: c * TC + (tt4 + 1) * 512], bb[:, :],
                bsb[:, c:c + 1],
                resid[:, c * TC + tt4 * 512: c * TC + (tt4 + 1) * 512],
                op0=ALU.add, op1=ALU.add), inc="vc")
        bring.release(ec, "vc", tv)
    pool.free_all([xln2, wsb, vr, ct, bsb, r1])



def _emit_l1attn(b, dr, bank, pairs, resid, ones_col, ones_row, csel_sb,
                 carry_d, carry_all, pool, dbg=None):
    xln3 = pool.alloc("xln3", [128, NC * TC], BF16)
    _emit_ln(b, bank, ones_col, ones_row, pool,
             src=("sbuf", resid), dst=xln3, tag="ln2")
    b.barrier()

    bd = {n: pool.alloc(f"{n}_sb", [128, 8 * 128], BF16)
          for n in ("bdf", "bdi", "bdh")}
    F_sb = pool.alloc("F_sb", [128, NC * TC], F32)
    v_sb = pool.alloc("v_sb", [128, TC], F32)
    h_sb = pool.alloc("h_sb", [128, TC], F32)
    zero_sb = pool.alloc("zero_sb", [128, 1], F32)
    t0 = pool.alloc("g_t0", [128, 1024], F32)
    t1 = pool.alloc("g_t1", [128, 1024], F32)
    t2_ = pool.alloc("g_t2", [128, 1024], F32)
    t3 = pool.alloc("g_t3", [128, 1024], F32)
    mk = pool.alloc("g_mk", [128, 1024], U8)
    carry_sb = pool.alloc("carry_sb", [128, 8], F32)
    cg_sb = pool.alloc("cg_sb", [8, D], F32)
    hin = pool.alloc("hin_sb", [128, 8], F32)

    for n in ("bdf", "bdi", "bdh"):
        b.dma("gpsimd", bd[n][:].rearrange("p (c n2) -> p c n2", c=8),
              dr[n][:].rearrange("c p n2 -> p c n2"), "ldg")
    b.run("vector", lambda e: e.memset(zero_sb[:], 0.0))
    zero_bc = zero_sb[:].broadcast_to([128, TC])
    b.wait("tensor", "ldg", b.val("ldg"))

    gate_ring = Ring(b, 1)
    for c in range(NC):
        for t2 in range(2):
            gate_ring.acquire("tensor", 0)
            for gi, gname in enumerate(("bdf", "bdi", "bdh")):
                for sub in range(2):
                    b.run("tensor", lambda e, cc=c, tt=t2 * 2 + sub, gb=bd[gname],
                          pi_=pairs[gi], s2=sub: e.matmul(
                        pi_[:, s2 * 512:(s2 + 1) * 512],
                        gb[:, cc * 128:(cc + 1) * 128],
                        xln3[:, cc * TC + tt * 512: cc * TC + (tt + 1) * 512],
                        start=True, stop=True),
                        inc=("tc" if (gi == 2 and sub == 1) else None))
            tg = b.val("tc")
            b.wait("scalar", "tc", tg)
            b.wait("vector", "tc", tg)
            pf = pairs[0][:, 0:1024]
            pi = pairs[1][:, 0:1024]
            ph_ = pairs[2][:, 0:1024]
            sl = slice(t2 * 1024, (t2 + 1) * 1024)
            b.run("scalar", lambda e, p=pf: e.activation(t0[:], p, ACT.Exp, scale=-1.0))
            b.run("scalar", lambda e: e.activation(t0[:], t0[:], ACT.Ln, bias=1.0))
            b.run("scalar", lambda e, p=pi: e.activation(t1[:], p, ACT.Exp, scale=-1.0))
            ts1 = b.run("scalar", lambda e: e.activation(t1[:], t1[:], ACT.Ln, bias=1.0),
                        inc="sc")
            b.wait("vector", "sc", ts1)
            tv1 = b.run("vector", lambda e: e.tensor_sub(t2_[:], t0[:], t1[:]), inc="vc")
            b.wait("scalar", "vc", tv1)
            b.run("scalar", lambda e: e.activation(t3[:], t2_[:], ACT.Exp))
            b.run("scalar", lambda e: e.activation(t3[:], t3[:], ACT.Ln, bias=1.0))
            ts2 = b.run("scalar", lambda e, cc=c, ss=sl: e.activation(
                F_sb[:, cc * TC + ss.start: cc * TC + ss.stop], t3[:],
                ACT.Exp, scale=-1.0, bias=b.c_eps[:]), inc="sc")
            b.wait("vector", "sc", ts2)
            tv2 = b.run("vector", lambda e: e.tensor_sub(t2_[:], t2_[:], t3[:]), inc="vc")
            b.wait("scalar", "vc", tv2)
            b.run("scalar", lambda e: e.activation(t2_[:], t2_[:], ACT.Exp))
            b.run("scalar", lambda e, p=ph_: e.activation(t0[:], p, ACT.Exp, scale=-1.0))
            b.run("scalar", lambda e: e.activation(t0[:], t0[:], ACT.Ln, bias=1.0))
            ts3 = b.run("scalar", lambda e: e.activation(t0[:], t0[:], ACT.Exp,
                                                         scale=-1.0, bias=b.c_eps[:]), inc="sc")
            b.wait("vector", "sc", ts3)
            b.run("vector", lambda e, p=ph_: e.tensor_scalar(mk[:], p, 0.0, None,
                                                             op0=ALU.is_ge))
            b.run("vector", lambda e, p=ph_: e.tensor_scalar(t1[:], p, 0.5 + EPS,
                                                             None, op0=ALU.add))
            b.run("vector", lambda e: e.select(t3[:], mk[:], t1[:], t0[:]))
            tv3 = b.run("vector", lambda e, ss=sl: e.tensor_mul(
                v_sb[:, ss], t2_[:], t3[:]), inc="vc")
            gate_ring.release(0, "vc", tv3)
        tv4 = b.run("vector", lambda e: e.engine_nop(), inc="vc")
        b.wait("vector", "vc", tv4)
        b.run("vector", lambda e, cc=c: e.tensor_tensor_scan(
            h_sb[:], F_sb[:, cc * TC:(cc + 1) * TC], v_sb[:], 0.0,
            op0=ALU.mult, op1=ALU.add))
        b.run("vector", lambda e, cc=c: e.tensor_add(
            resid[:, cc * TC:(cc + 1) * TC], resid[:, cc * TC:(cc + 1) * TC], h_sb[:]))
        b.run("vector", lambda e, cc=c: e.tensor_copy(
            carry_sb[:, cc:cc + 1], h_sb[:, TC - 1:TC]), inc="vc")
    b.wait("sync", "vc", b.val("vc"))
    if dbg is not None and "F" in dbg:
        import kernel as _k
        if _k.DEV_DUMP:
            for cc in range(NC):
                b.dma("sync", dbg["F"][cc * 128:(cc + 1) * 128, :],
                      F_sb[:, cc * TC:(cc + 1) * TC], "sts")
                b.dma("gpsimd", dbg["x3"][cc * 128:(cc + 1) * 128, :],
                      xln3[:, cc * TC:(cc + 1) * TC], "ldg")
            b.wait("sync", "sts", b.val("sts"))
            b.wait("gpsimd", "ldg", b.val("ldg"))
    for c in range(NC):
        b.dma("sync", carry_d[c * 128:(c + 1) * 128], carry_sb[:, c:c + 1], "lds")
    b.wait("gpsimd", "lds", b.val("lds"))
    tcc = b.run("gpsimd", lambda e: e.collective_compute(
        "AllGather", ALU.bypass, replica_groups=[list(range(8))],
        ins=[carry_d[:]], outs=[carry_all[:]]), inc="cc")
    b.wait("sync", "cc", tcc)
    tcg = b.dma("sync", cg_sb[:], carry_all[:], "lds")
    b.wait("tensor", "lds", tcg)
    for c in range(NC):
        b.run("tensor", lambda e, cc=c: e.matmul(
            bank[5][:, cc:cc + 1], cg_sb[:, cc * 128:(cc + 1) * 128], csel_sb[:],
            start=True, stop=True), inc=("tc" if c == NC - 1 else None))
    tm = b.val("tc")
    b.wait("vector", "tc", tm)
    tvh = b.run("vector", lambda e: e.tensor_copy(hin[:], bank[5][:, 0:8]), inc="vc")
    b.wait("vector", "vc", tvh)
    for c in range(NC):
        b.run("vector", lambda e, cc=c: e.tensor_tensor_scan(
            h_sb[:], F_sb[:, cc * TC:(cc + 1) * TC], zero_bc, 1.0,
            op0=ALU.mult, op1=ALU.add))
        b.run("vector", lambda e, cc=c: e.scalar_tensor_tensor(
            resid[:, cc * TC:(cc + 1) * TC], h_sb[:], hin[:, cc:cc + 1],
            resid[:, cc * TC:(cc + 1) * TC], op0=ALU.mult, op1=ALU.add))
    pool.free_all([xln3, bd["bdf"], bd["bdi"], bd["bdh"], F_sb, v_sb, h_sb,
                   zero_sb, t0, t1, t2_, t3, mk, carry_sb, cg_sb, hin])



def _emit_ffn1(b, dr, bank, resid, ones_col, ones_row, prep_all, pool, dbg=None):
    xln4 = pool.alloc("xln4", [128, NC * TC], BF16)
    _emit_ln(b, bank, ones_col, ones_row, pool,
             src=("sbuf", resid), dst=xln4, tag="ln3")
    b.barrier()

    vt = pool.alloc("vt_sb", [128, 8 * RANK], BF16)
    fb = pool.alloc("fb_sb", [RANK, RANK], BF16)
    ut = pool.alloc("ut_sb", [RANK, HIDDEN], BF16)
    g16 = pool.alloc("g16_sb", [RANK, 1], F32)
    s1 = pool.alloc("s1_sb", [RANK, 512], BF16)
    s2 = pool.alloc("s2_sb", [RANK, 512], BF16)
    s1x = pool.alloc("s1x_sb", [RANK, 512], BF16)
    rsq = pool.alloc("rsq_sb", [128, 32 * 1024], BF16)
    relu_t = pool.alloc("relu_t", [128, 512], F32)
    wcol = [pool.alloc(f"wcol{i}", [128, 8 * 128], BF16) for i in range(2)]
    wpb = [pool.alloc(f"wpb{i}", [128, 32 * 128], BF16) for i in range(2)]

    b.dma("gpsimd", vt[:].rearrange("p (c n) -> p c n", c=8),
          dr["vT"][:].rearrange("(c p) n -> p c n", p=128), "ldg")
    b.dma("gpsimd", fb[:], dr["fbT"][:], "ldg")
    b.dma("gpsimd", ut[:], dr["uT"][:], "ldg")
    tg = b.dma("sync", g16[:], dr["gate16"][:], "lds")
    b.wait("tensor", "ldg", b.val("ldg"))
    b.wait("vector", "lds", tg)

    import kernel as _k
    if dbg is not None and _k.DEV_DUMP:
        vscr = pool.alloc("vscr", [128, TC], F32)
        for cc in range(NC):
            b.dma("gpsimd", dbg["x4"][cc * 128:(cc + 1) * 128, :],
                  xln4[:, cc * TC:(cc + 1) * TC], "ldg")
            tvv = b.run("vector", lambda e, c2=cc: e.tensor_copy(
                vscr[:], xln4[:, c2 * TC:(c2 + 1) * TC]), inc="vc")
            b.wait("sync", "vc", tvv)
            b.dma("sync", dbg["x3"][cc * 128:(cc + 1) * 128, :], vscr[:], "sts")
            b.wait("vector", "sts", b.val("sts"))
        b.wait("gpsimd", "ldg", b.val("ldg"))
        b.wait("sync", "sts", b.val("sts"))
        vscr.free()
    wcd = DmaRing(b, "f1wc", 2)
    wpd = DmaRing(b, "f1wp", 2)
    fc_ring = Ring(b, 2)
    pj_ring = Ring(b, 2)

    s2ab = [s1, s2]
    for qp in range(2):
        q0 = 2 * qp
        rb = bank[0]
        fc_ring.acquire("tensor", 0)
        fc_ring.acquire("tensor", 1)
        for qq in range(2):
            for dc in range(NC):
                b.run("tensor", lambda e, d=dc, q=q0 + qq: e.matmul(
                    rb[0:RANK, :], vt[:, d * RANK:(d + 1) * RANK],
                    xln4[:, d * TC + q * 512: d * TC + (q + 1) * 512],
                    start=(d == 0), stop=(d == NC - 1)),
                    inc=("tc" if dc == NC - 1 else None))
            tt = b.val("tc")
            b.wait("scalar", "tc", tt)
            ts = b.run("scalar", lambda e: e.copy(s1x[:], rb[0:RANK, :]), inc="sc")
            b.wait("tensor", "sc", ts)
            b.run("tensor", lambda e: e.matmul(rb[0:RANK, :], fb[:], s1x[:],
                                               start=True, stop=True), inc="tc")
            tt2 = b.val("tc")
            b.wait("vector", "tc", tt2)
            tv = b.run("vector", lambda e, s_=s2ab[qq]: e.tensor_scalar(
                s_[:], rb[0:RANK, :], g16[:], None, op0=ALU.mult), inc="vc")
            b.wait("tensor", "vc", tv)

        for hc in range(32):
            i = qp * 32 + hc
            r_ = (hc * 128) // 512
            lc = (hc * 128) % 512
            wsem, lw = wcd.load(
                "sync", wcol[i % 2][:].rearrange("p (c n) -> p c n", c=8),
                prep_all[r_, 0:D * 512].rearrange("(d n) -> d n", n=512)[
                    :, lc:lc + 128].rearrange("(c p) n -> p c n", p=128), i)
            fc_ring.acquire("tensor", hc)
            bkA = bank[(hc % 2) * 2]
            bkB = bank[(hc % 2) * 2 + 1]
            b.run("tensor", lambda e, h2=hc, bb=bkA: e.matmul(
                bb[:, :], ut[:, h2 * 128:(h2 + 1) * 128], s2ab[0][:],
                start=True, stop=False))
            b.run("tensor", lambda e, h2=hc, bb=bkB: e.matmul(
                bb[:, :], ut[:, h2 * 128:(h2 + 1) * 128], s2ab[1][:],
                start=True, stop=False))
            b.wait("tensor", wsem, lw)
            for dc in range(NC):
                for qq in range(2):
                    b.run("tensor",
                          lambda e, d=dc, q=q0 + qq, wi=i % 2,
                          bb=(bkA if qq == 0 else bkB): e.matmul(
                              bb[:, :], wcol[wi][:, d * 128:(d + 1) * 128],
                              xln4[:, d * TC + q * 512: d * TC + (q + 1) * 512],
                              start=False, stop=(d == NC - 1)),
                          inc=("tc" if dc == NC - 1 and qq == 1 else None))
            tt3 = b.val("tc")
            b.wait("scalar", "tc", tt3)
            for qq, bb in ((0, bkA), (1, bkB)):
                b.run("scalar", lambda e, bb=bb: e.activation(
                    relu_t[:], bb[:, :], ACT.Relu))
                tv2 = b.run("scalar", lambda e, h2=hc, q2=qq: e.activation(
                    rsq[:, h2 * 1024 + q2 * 512: h2 * 1024 + (q2 + 1) * 512],
                    relu_t[:], ACT.Square), inc="sc")
            fc_ring.release(hc, "sc", tv2)
            wcd.consumed(i, "sc", tv2)
        rsq_done = b.val("sc")

        for dco in range(NC):
            i = qp * NC + dco
            woff = D * 512
            psem, lw = wpd.load(
                "sync", wpb[i % 2][:].rearrange("p (c n) -> p c n", c=32),
                prep_all[dco, woff:woff + HIDDEN * 128].rearrange(
                    "(c p n) -> p c n", p=128, n=128), i)
            b.wait("tensor", psem, lw)
            b.wait("tensor", "sc", rsq_done)
            pj_ring.acquire("tensor", dco)
            bkA = bank[4 + (dco % 2) * 2]
            bkB = bank[5 + (dco % 2) * 2]
            for hc in range(32):
                for qq in range(2):
                    b.run("tensor",
                          lambda e, h2=hc, wi=i % 2, q2=qq,
                          bb=(bkA if qq == 0 else bkB): e.matmul(
                              bb[:, :], wpb[wi][:, h2 * 128:(h2 + 1) * 128],
                              rsq[:, h2 * 1024 + q2 * 512:
                                  h2 * 1024 + (q2 + 1) * 512],
                              start=(h2 == 0), stop=(h2 == 31)),
                          inc=("tc" if hc == 31 and qq == 1 else None))
            tt4 = b.val("tc")
            b.wait("vector", "tc", tt4)
            for qq, bb in ((0, bkA), (1, bkB)):
                tv3 = b.run("vector", lambda e, d=dco, q=q0 + qq, bb=bb:
                            e.tensor_add(
                    resid[:, d * TC + q * 512: d * TC + (q + 1) * 512],
                    resid[:, d * TC + q * 512: d * TC + (q + 1) * 512],
                    bb[:, :]), inc="vc")
            pj_ring.release(dco, "vc", tv3)
            wpd.consumed(i, "vc", tv3)
    pool.free_all([xln4, vt, fb, ut, g16, s1x, s1, s2, rsq, relu_t]
                  + wcol + wpb)



def kernel(**inputs):
    sh, per = stage_inputs(inputs)
    nc = build_program()
    in_maps = [{**sh, **per[c]} for c in range(8)]
    res = run_bass_kernel_spmd(nc, in_maps, core_ids=list(range(8)))
    out = np.empty((4, S, D), np.float32)
    for c in range(8):
        seq, half = c // 2, c % 2
        out[seq, half * TC:(half + 1) * TC, :] = res.results[c]["y"].T
    return out


# version 18 (final)
# speedup vs baseline: 1.0143x; 1.0143x over previous
import numpy as np
from contextlib import ExitStack

import concourse.bass as bass
import concourse.mybir as mybir
from concourse.bass_utils import run_bass_kernel_spmd

F32 = mybir.dt.float32
BF16 = mybir.dt.bfloat16
U8 = mybir.dt.uint8
F32R = mybir.dt.float32r
AX = mybir.AxisListType.X
ALU = mybir.AluOpType
ACT = mybir.ActivationFunctionType

D = 1024
TC = 2048
S = 4096
NH = 16
HD = 64
NC = 8
NT = TC // 512
EPS = 1e-8
NEG = -1.0e9
HIDDEN = 4096
RANK = 16
PREP_N = D * 512 + HIDDEN * 128

N_BLOCKS = 4
DEV_DUMP = False

ENGINES = ("sync", "tensor", "scalar", "vector", "gpsimd")


class Bld:
    def __init__(self, nc, stack):
        self.nc = nc
        self.stack = stack
        self.ops = {e: [] for e in ENGINES}
        self.sem = {}
        self.cnt = {}
        self.nbar = 0

    def new_sem(self, name):
        self.sem[name] = self.stack.enter_context(self.nc.semaphore(name))
        self.cnt[name] = 0

    def add(self, eng, fn):
        self.ops[eng].append(fn)

    def bump(self, name, k=1):
        self.cnt[name] += k
        return self.cnt[name]

    def val(self, name):
        return self.cnt[name]

    def dma(self, eng, out_ap, in_ap, sem):
        tgt = self.bump(sem, 16)
        h = self.sem[sem]
        self.add(eng, lambda e, o=out_ap, i=in_ap: e.dma_start(out=o, in_=i).then_inc(h, 16))
        return tgt

    def wait(self, eng, sem, tgt):
        if tgt <= 0:
            return
        h = self.sem[sem]
        self.add(eng, lambda e, t=tgt: e.wait_ge(h, t))

    def run(self, eng, fn, inc=None, k=1):
        if inc is None:
            self.add(eng, fn)
            return None
        tgt = self.bump(inc, k)
        h = self.sem[inc]
        self.add(eng, lambda e, f=fn, kk=k: f(e).then_inc(h, kk))
        return tgt

    def barrier(self):
        self.nbar += 1
        tgt = self.bump("ph", 5)
        h = self.sem["ph"]
        for eng in ENGINES:
            if eng in ("vector", "gpsimd"):
                self.add(eng, lambda e: e.engine_nop().then_inc(h, 1))
            elif eng == "scalar":
                d = self.dummy
                self.add(eng, lambda e: e.copy(d[0:1, 0:1], d[0:1, 1:2]).then_inc(h, 1))
            elif eng == "tensor":
                self.add(eng, lambda e: e.nop().then_inc(h, 1))
            else:
                self.add(eng, lambda e: e.sem_inc(h, 1))
            self.add(eng, lambda e, t=tgt: e.wait_ge(h, t))


class Ring:
    def __init__(self, bld, n):
        self.b = bld
        self.n = n
        self.last = [None] * n

    def acquire(self, eng, i):
        lu = self.last[i % self.n]
        if lu is not None:
            self.b.wait(eng, lu[0], lu[1])

    def release(self, i, sem, tgt):
        self.last[i % self.n] = (sem, tgt)


class DmaRing:

    _uid = [0]

    def __init__(self, bld, name, depth):
        self.b = bld
        self.depth = depth
        self.names = []
        DmaRing._uid[0] += 1
        for i in range(depth):
            nm = f"{name}{DmaRing._uid[0]}_{i}"
            bld.new_sem(nm)
            self.names.append(nm)
        self.last = [None] * depth
        self.uses = [0] * depth

    def load(self, eng, out_ap, in_ap, i):
        s = i % self.depth
        lu = self.last[s]
        if lu is not None:
            self.b.wait(eng, lu[0], lu[1])
        tgt = self.b.dma(eng, out_ap, in_ap, self.names[s])
        return (self.names[s], tgt)

    def consumed(self, i, sem, tgt):
        self.last[i % self.depth] = (sem, tgt)


class View:

    def __init__(self, pool, th, off, p, w):
        self.pool = pool
        self.th = th
        self.off = off
        self.p = p
        self.w = w

    def __getitem__(self, idx):
        if idx == slice(None):
            return self.th[0:self.p, self.off:self.off + self.w]
        if isinstance(idx, tuple):
            ps, cs = idx
        else:
            ps, cs = idx, slice(None)
        p0 = 0 if ps.start is None else ps.start
        p1 = self.p if ps.stop is None else ps.stop
        c0 = 0 if cs.start is None else cs.start
        c1 = self.w if cs.stop is None else cs.stop
        return self.th[p0:p1, self.off + c0:self.off + c1]

    def free(self):
        self.pool.free(self)


class Pool:

    def __init__(self, nc, nbytes):
        self.arena = nc.alloc_sbuf_tensor("arena", [128, nbytes], U8)
        self.views = {F32: self.arena.bitcast(F32), BF16: self.arena.bitcast(BF16),
                      U8: self.arena, F32R: self.arena.bitcast(F32R)}
        self.free_list = [(0, nbytes)]
        self.nbytes = nbytes
        self.hwm = 0

    verbose = False

    def alloc(self, name, shape, dt=F32):
        p, w = shape
        esz = {F32: 4, F32R: 4, BF16: 2, U8: 1}[dt]
        need = w * esz
        align = 32
        for i, (st, sz) in enumerate(self.free_list):
            a = (st + align - 1) // align * align
            if a + need <= st + sz:
                pre = a - st
                post = (st + sz) - (a + need)
                repl = []
                if pre:
                    repl.append((st, pre))
                if post:
                    repl.append((a + need, post))
                self.free_list[i:i + 1] = repl
                self.hwm = max(self.hwm, a + need)
                if Pool.verbose:
                    print(f"POOL {name}: [{a}, {a+need})")
                v = View(self, self.views[dt], a // esz, p, w)
                v._range = (a, need)
                return v
        raise RuntimeError(f"SBUF pool exhausted allocating {name} "
                           f"({need}B): free={self.free_list}")

    def free(self, v):
        st, sz = v._range
        self.free_list.append((st, sz))
        self.free_list.sort()
        merged = []
        for rng in self.free_list:
            if merged and merged[-1][0] + merged[-1][1] == rng[0]:
                merged[-1] = (merged[-1][0], merged[-1][1] + rng[1])
            else:
                merged.append(list(rng) if isinstance(rng, tuple) else rng)
        self.free_list = [tuple(m) for m in merged]

    def free_all(self, vs):
        for v in vs:
            self.free(v)



def _rope_tables():
    j = np.arange(0, HD, 2, dtype=np.float64)
    inv = 1.0 / (10000.0 ** (j / HD))
    pos = np.arange(S, dtype=np.float64)
    fr = pos[:, None] * inv[None, :]
    cos = np.cos(fr).T
    sin = np.sin(fr).T
    cosf = np.empty((HD, S), np.float64)
    sinS = np.empty((HD, S), np.float64)
    cosf[0::2] = cos
    cosf[1::2] = cos
    sinS[0::2] = -sin
    sinS[1::2] = sin
    return (np.ascontiguousarray(np.tile(cosf, (2, 1)).astype(np.float32)),
            np.ascontiguousarray(np.tile(sinS, (2, 1)).astype(np.float32)))


def _swap_pairs(n):
    idx = np.arange(n)
    idx[0::2] += 1
    idx[1::2] -= 1
    return idx


def _blockdiag(W):
    out = np.zeros((8, 128, 128), np.float32)
    for g in range(128):
        c, loc = g // 16, (g % 16) * 8
        out[c, loc:loc + 8, loc:loc + 8] = W[g].T
    return out


def _cmask4():
    cm = np.zeros((4, 128, 512), np.float32)
    i = np.arange(128)[:, None]
    jj = np.arange(512)[None, :]
    for r in range(4):
        cm[r] = np.where(i + r * 128 <= jj, 0.0, NEG)
    return cm


def stage_inputs(inp):
    x = np.asarray(inp["L0_attn_x"], np.float32)
    cos2, sin2 = _rope_tables()
    sw = _swap_pairs(D)

    sh = {}
    sh["wqT"] = np.ascontiguousarray(np.asarray(inp["L0_attn_wq"], np.float32).T)
    sh["wkT"] = np.ascontiguousarray(np.asarray(inp["L0_attn_wk"], np.float32).T)
    sh["wqswT"] = np.ascontiguousarray(sh["wqT"][:, sw])
    sh["wkswT"] = np.ascontiguousarray(sh["wkT"][:, sw])
    sh["wvT"] = np.ascontiguousarray(np.asarray(inp["L0_attn_wv"], np.float32).T)
    sh["woT"] = np.ascontiguousarray(np.asarray(inp["L0_attn_wo"], np.float32).T)
    sh["WT"] = np.ascontiguousarray(np.asarray(inp["L0_ffn_W"], np.float32).T)
    sh["bcol"] = np.ascontiguousarray(np.asarray(inp["L0_ffn_b"], np.float32).reshape(D, 1))
    sh["Vr"] = np.ascontiguousarray(np.asarray(inp["L0_ffn_V_r"], np.float32))
    sh["CT"] = np.ascontiguousarray(np.asarray(inp["L0_ffn_C"], np.float32).T)
    sh["bdf"] = _blockdiag(np.asarray(inp["L1_attn_Wf"], np.float32))
    sh["bdi"] = _blockdiag(np.asarray(inp["L1_attn_Wi"], np.float32))
    sh["bdh"] = _blockdiag(np.asarray(inp["L1_attn_Wh"], np.float32))
    sh["uT"] = np.ascontiguousarray(np.asarray(inp["L1_ffn_u"], np.float32).T)
    sh["vT"] = np.ascontiguousarray(np.asarray(inp["L1_ffn_v"], np.float32).T)
    sh["fbT"] = np.ascontiguousarray(np.asarray(inp["L1_ffn_fast_b"], np.float32).T)
    sh["gate16"] = np.full((16, 1), np.float32(np.asarray(inp["L1_ffn_gate"])), np.float32)
    sh["eye"] = np.eye(128, dtype=np.float32)
    sh["cmask4"] = _cmask4()
    sh["onescol"] = np.ones((128, 1), np.float32)

    wfc = np.asarray(inp["L1_ffn_w_fc"], np.float32)
    wproj = np.asarray(inp["L1_ffn_w_proj"], np.float32)

    per = []
    for c in range(8):
        seq, half = c // 2, c % 2
        own = slice(half * TC, half * TC + TC)
        par = slice((1 - half) * TC, (1 - half) * TC + TC)
        m = {}
        m["x_own"] = np.ascontiguousarray(x[seq, own].T)
        m["x_par"] = np.ascontiguousarray(x[seq, par].T)
        m["cos_own"] = np.ascontiguousarray(cos2[:, own])
        m["sin_own"] = np.ascontiguousarray(sin2[:, own])
        m["cos_par"] = np.ascontiguousarray(cos2[:, par])
        m["sin_par"] = np.ascontiguousarray(sin2[:, par])
        m["rmask"] = np.full((128, 1), 0.0 if half == 1 else NEG, np.float32)
        cs = np.zeros((8, 1), np.float32)
        if half == 1:
            cs[c - 1, 0] = 1.0
        m["carrysel"] = cs
        m["wfc_sl"] = np.ascontiguousarray(wfc[c * 512:(c + 1) * 512])
        m["wproj_sl"] = np.ascontiguousarray(wproj[c * 128:(c + 1) * 128])
        per.append(m)
    return sh, per



def build_program():
    nc = bass.Bass("TRN2", target_bir_lowering=False)
    dr = {}

    def din(name, shape):
        dr[name] = nc.dram_tensor(name, list(shape), F32, kind="ExternalInput")

    din("x_own", (D, TC)); din("x_par", (D, TC))
    din("cos_own", (128, TC)); din("sin_own", (128, TC))
    din("cos_par", (128, TC)); din("sin_par", (128, TC))
    din("rmask", (128, 1)); din("carrysel", (8, 1))
    for n in ("wqT", "wkT", "wqswT", "wkswT", "wvT", "woT", "WT"):
        din(n, (D, D))
    din("bcol", (D, 1)); din("Vr", (D, 32)); din("CT", (32, D))
    din("bdf", (8, 128, 128)); din("bdi", (8, 128, 128)); din("bdh", (8, 128, 128))
    din("uT", (RANK, HIDDEN)); din("vT", (D, RANK)); din("fbT", (RANK, RANK))
    din("gate16", (RANK, 1))
    din("eye", (128, 128)); din("cmask4", (4, 128, 512))
    din("onescol", (128, 1))
    din("wfc_sl", (512, D)); din("wproj_sl", (128, HIDDEN))

    out = nc.dram_tensor("y", [D, TC], F32, kind="ExternalOutput")
    dbg = {}
    if DEV_DUMP:
        dbg["xp"] = nc.dram_tensor("dbg_xp", [D, TC], F32, kind="ExternalOutput")
        dbg["xp2"] = nc.dram_tensor("dbg_xp2", [D, TC], F32, kind="ExternalOutput")
        dbg["xo"] = nc.dram_tensor("dbg_xo", [D, TC], F32, kind="ExternalOutput")
        dbg["K"] = nc.dram_tensor("dbg_K", [D, S], F32, kind="ExternalOutput")
        dbg["Q"] = nc.dram_tensor("dbg_Q", [D, TC], F32, kind="ExternalOutput")
        dbg["V"] = nc.dram_tensor("dbg_V", [32 * 128, NH * (HD + 1)], F32, kind="ExternalOutput")
        dbg["attn"] = nc.dram_tensor("dbg_attn", [D, TC], F32, kind="ExternalOutput")
        dbg["r2"] = nc.dram_tensor("dbg_r2", [D, TC], F32, kind="ExternalOutput")
        dbg["r3"] = nc.dram_tensor("dbg_r3", [D, TC], F32, kind="ExternalOutput")
        dbg["F"] = nc.dram_tensor("dbg_F", [D, TC], F32, kind="ExternalOutput")
        dbg["x3"] = nc.dram_tensor("dbg_x3", [D, TC], F32, kind="ExternalOutput")
        dbg["r4"] = nc.dram_tensor("dbg_r4", [D, TC], F32, kind="ExternalOutput")
        dbg["prep"] = nc.dram_tensor("dbg_prep", [8, PREP_N], F32, kind="ExternalOutput")
        dbg["prepb"] = nc.dram_tensor("dbg_prepb", [8, PREP_N], BF16, kind="ExternalOutput")
        dbg["x4"] = nc.dram_tensor("dbg_x4", [D, TC], F32, kind="ExternalOutput")
    prep_part = nc.dram_tensor("prep_part", [PREP_N], BF16)
    prep_all = nc.dram_tensor("prep_all", [8, PREP_N], BF16, addr_space="Shared")
    carry_d = nc.dram_tensor("carry_d", [D], F32)
    carry_all = nc.dram_tensor("carry_all", [8, D], F32, addr_space="Shared")
    ropew = nc.dram_tensor("ropew", [4, D, D], BF16)

    with ExitStack() as st0:
        b = Bld(nc, st0)
        for s in ("ph", "lds", "ldg", "lda", "tc", "sc", "vc", "cc", "sts"):
            b.new_sem(s)

        pairs = [st0.enter_context(nc.psum_tensor(f"pp{i}", [128, 1024], F32))
                 for i in range(4)]
        bank = [View(None, pairs[i // 2], (i % 2) * 512, 128, 512) for i in range(8)]
        ones_row_r = nc.alloc_sbuf_tensor("ones_row_r", [1, 128], F32R)
        drow_r = nc.alloc_sbuf_tensor("drow_r", [1, 512], F32R)
        pool = Pool(nc, (nc.sbuf_bytes_remaining - 64) // 32 * 32)

        b.dummy = pool.alloc("dummy", [1, 2], F32)
        I_bf = pool.alloc("I_bf", [128, 128], BF16)
        I_f32 = pool.alloc("I_f32", [128, 128], F32)
        ones_row = pool.alloc("ones_row", [128, 128], F32)
        smalls = pool.alloc("smalls", [128, 8], F32)
        csel_sb = pool.alloc("csel", [8, 1], F32)
        rmask_sb = View(pool, smalls.th, smalls.off + 0, 128, 1)
        ones_col = View(pool, smalls.th, smalls.off + 1, 128, 1)
        b.c_eps1e5 = View(pool, smalls.th, smalls.off + 2, 128, 1)
        b.c_eps = View(pool, smalls.th, smalls.off + 3, 128, 1)
        b.run("vector", lambda e: e.memset(b.c_eps1e5[:], 1e-5))
        b.run("vector", lambda e: e.memset(b.c_eps[:], EPS))


        b.dma("sync", rmask_sb[:], dr["rmask"][:], "lds")
        b.dma("sync", ones_col[:], dr["onescol"][:], "lds")
        b.dma("sync", csel_sb[:], dr["carrysel"][:], "lds")
        b.run("vector", lambda e: e.memset(ones_row[:], 1.0))
        b.wait("vector", "lds", b.val("lds"))
        b.run("vector", lambda e: e.tensor_copy(ones_row_r[:], ones_row[0:1, :]))
        b.ones_row_r = ones_row_r
        b.drow_r = drow_r
        b.dma("gpsimd", I_bf[:], dr["eye"][:], "ldg")
        b.dma("sync", I_f32[:], dr["eye"][:], "lds")
        b.wait("vector", "lds", b.val("lds"))
        b.wait("vector", "ldg", b.val("ldg"))

        _emit_precast(b, dr, ropew, pool)
        if N_BLOCKS >= 4:
            _emit_prep(b, dr, prep_part, prep_all, bank, I_f32, pool)
        I_f32.free()
        b.barrier()

        K_sb = pool.alloc("K", [128, NC * S], BF16)
        V_sb = pool.alloc("V", [128, 32 * NH * (HD + 1)], BF16)

        def dump(view, dst, rows, width):
            if not DEV_DUMP:
                return
            for cc in range(rows // 128):
                b.dma("gpsimd", dst[cc * 128:(cc + 1) * 128, :],
                      view[:, cc * width:(cc + 1) * width], "ldg")
            b.wait("gpsimd", "ldg", b.val("ldg"))
            b.barrier()

        xp = pool.alloc("xp", [128, NC * TC], BF16)
        _emit_ln(b, bank, ones_col, ones_row, pool,
                 src=("hbm", dr["x_par"]), dst=xp, tag="ln0p")
        b.barrier()
        dump(xp, dbg.get("xp2"), D, TC)
        _emit_vproj(b, dr, bank, V_sb, xp, 16, pool)
        b.barrier()
        _emit_rope_proj(b, dr, bank, K_sb, xp, S, TC, pool, ropew, 2, 3,
                        dr["cos_par"], dr["sin_par"], wring=2)
        b.barrier()
        dump(xp, dbg.get("xp"), D, TC)
        xp.free()

        xo = pool.alloc("xo", [128, NC * TC], BF16)
        _emit_ln(b, bank, ones_col, ones_row, pool,
                 src=("hbm", dr["x_own"]), dst=xo, tag="ln0o")
        b.barrier()
        _emit_vproj(b, dr, bank, V_sb, xo, 0, pool)
        b.barrier()
        _emit_rope_proj(b, dr, bank, K_sb, xo, S, 0, pool, ropew, 2, 3,
                        dr["cos_own"], dr["sin_own"], wring=2)
        b.barrier()
        Q_sb = pool.alloc("Q", [128, NC * TC], BF16)
        _emit_rope_proj(b, dr, bank, Q_sb, xo, TC, 0, pool, ropew, 0, 1,
                        dr["cos_own"], dr["sin_own"], wring=1, paired=True)
        b.barrier()
        dump(xo, dbg.get("xo"), D, TC)
        dump(K_sb, dbg.get("K"), D, S)
        dump(Q_sb, dbg.get("Q"), D, TC)
        dump(V_sb, dbg.get("V"), 32 * 128, NH * (HD + 1))
        xo.free()

        attn_sb = pool.alloc("attn", [128, NC * TC], BF16)
        _emit_attention(b, dr, bank, K_sb, V_sb, Q_sb, attn_sb, I_bf,
                        rmask_sb, ones_row, pool)
        b.barrier()
        dump(attn_sb, dbg.get("attn"), D, TC)
        pool.free_all([K_sb, V_sb, Q_sb])

        resid = pool.alloc("resid", [128, NC * TC], F32)
        _emit_oproj(b, dr, bank, attn_sb, resid, pool)
        b.barrier()
        attn_sb.free()

        if N_BLOCKS >= 2:
            _emit_ffn0(b, dr, bank, resid, ones_col, ones_row, pool)
            b.barrier()
        if N_BLOCKS >= 3:
            if DEV_DUMP:
                for cc in range(NC):
                    b.dma("sync", dbg["r2"][cc * 128:(cc + 1) * 128, :],
                          resid[:, cc * TC:(cc + 1) * TC], "sts")
                b.wait("sync", "sts", b.val("sts"))
                b.barrier()
            _emit_l1attn(b, dr, bank, pairs, resid, ones_col, ones_row, csel_sb,
                         carry_d, carry_all, pool, dbg)
            b.barrier()
            if DEV_DUMP:
                for cc in range(NC):
                    b.dma("sync", dbg["r3"][cc * 128:(cc + 1) * 128, :],
                          resid[:, cc * TC:(cc + 1) * TC], "sts")
                b.wait("sync", "sts", b.val("sts"))
                b.barrier()
        if N_BLOCKS >= 4:
            _emit_ffn1(b, dr, bank, resid, ones_col, ones_row, prep_all, pool)
            b.barrier()

        for c in range(NC):
            b.dma("sync", out[c * 128:(c + 1) * 128, :],
                  resid[:, c * TC:(c + 1) * TC], "sts")
        b.wait("sync", "sts", b.val("sts"))

        with nc.Block() as block:
            @block.sync
            def _(e):
                for f in b.ops["sync"]:
                    f(e)

            @block.tensor
            def _(e):
                for f in b.ops["tensor"]:
                    f(e)

            @block.scalar
            def _(e):
                for f in b.ops["scalar"]:
                    f(e)

            @block.vector
            def _(e):
                for f in b.ops["vector"]:
                    f(e)

            @block.gpsimd
            def _(e):
                for f in b.ops["gpsimd"]:
                    f(e)

    return nc



def _emit_precast(b, dr, ropew, pool):
    stg = pool.alloc("pc_stg", [128, 8 * D], BF16)
    names = ("wqT", "wqswT", "wkT", "wkswT")
    for i, nm in enumerate(names):
        lg = b.dma("gpsimd", stg[:].rearrange("p (c n) -> p c n", c=8),
                   dr[nm][:].rearrange("(c p) n -> p c n", p=128), "ldg")
        b.wait("sync", "ldg", lg)
        st = b.dma("sync", ropew[i].rearrange("(c p) n -> p c n", p=128),
                   stg[:].rearrange("p (c n) -> p c n", c=8), "lds")
        b.wait("gpsimd", "lds", st)
    b.wait("sync", "lds", b.val("lds"))
    stg.free()



def _emit_prep(b, dr, prep_part, prep_all, bank, I_f32, pool):
    wraw = pool.alloc("pr_raw", [128, HIDDEN], F32)
    scth = pool.alloc("pr_scth", [128, 3], F32)
    gt = pool.alloc("pr_gt", [128, HIDDEN], F32)
    lt = pool.alloc("pr_lt", [128, HIDDEN], F32)
    stg = [pool.alloc(f"pr_stg{i}", [128, 128], BF16) for i in range(2)]
    sc_ = View(pool, scth.th, scth.off + 0, 128, 1)
    th = View(pool, scth.th, scth.off + 1, 128, 1)
    thn = View(pool, scth.th, scth.off + 2, 128, 1)
    rawd = DmaRing(b, "prraw", 1)
    stgd = DmaRing(b, "prstg", 2)
    bank_ring = Ring(b, 2)

    fcv = prep_part[:].rearrange("(d n) -> d n", n=512)
    woff = D * 512
    step = [0]
    prev_tt = [0]

    def prep_tile(src_ap, width, dst_fn, nblk):
        rsem, t_ld = rawd.load("sync", wraw[:, 0:width], src_ap, 0)
        b.wait("vector", rsem, t_ld)
        b.wait("vector", "tc", prev_tt[0])
        tr_ = b.run("vector", lambda e, w=width: e.reduce_sum(
            sc_[:], wraw[:, 0:w], axis=AX, apply_absolute_value=True), inc="vc")
        b.wait("scalar", "vc", tr_)
        ts_ = b.run("scalar", lambda e, w=width: e.mul(sc_[:], sc_[:], 1.0 / w),
                    inc="sc")
        b.wait("vector", "sc", ts_)
        tm_ = b.run("vector", lambda e: e.tensor_max(sc_[:], sc_[:], b.c_eps1e5[:]),
                    inc="vc")
        b.wait("scalar", "vc", tm_)
        b.run("scalar", lambda e: e.mul(th[:], sc_[:], 0.5))
        tg_ = b.run("scalar", lambda e: e.mul(thn[:], sc_[:], -0.5), inc="sc")
        b.wait("vector", "sc", tg_)
        b.run("vector", lambda e, w=width: e.tensor_scalar(
            gt[:, 0:w], wraw[:, 0:w], th[:], None, op0=ALU.is_gt))
        b.run("vector", lambda e, w=width: e.tensor_scalar(
            lt[:, 0:w], wraw[:, 0:w], thn[:], None, op0=ALU.is_lt))
        b.run("vector", lambda e, w=width: e.tensor_sub(gt[:, 0:w], gt[:, 0:w], lt[:, 0:w]))
        tv = b.run("vector", lambda e, w=width: e.tensor_scalar(
            gt[:, 0:w], gt[:, 0:w], sc_[:], None, op0=ALU.mult), inc="vc")
        rawd.consumed(0, "vc", tv)
        b.wait("tensor", "vc", tv)
        for j in range(nblk):
            i = step[0]; step[0] += 1
            bank_ring.acquire("tensor", i)
            tt = b.run("tensor", lambda e, jj=j, bi=i % 2: e.transpose(
                bank[bi][:, 0:128], gt[:, jj * 128:(jj + 1) * 128], I_f32[:]),
                inc="tc")
            b.wait("scalar", "tc", tt)
            lu = stgd.last[i % 2]
            if lu is not None:
                b.wait("scalar", lu[0], lu[1])
            ts = b.run("scalar", lambda e, bi=i % 2, si=i % 2: e.copy(
                stg[si][:], bank[bi][:, 0:128]), inc="sc")
            bank_ring.release(i, "sc", ts)
            b.wait("sync", "sc", ts)
            t_st = b.dma("sync", dst_fn(j), stg[i % 2][:], stgd.names[i % 2])
            stgd.uses[i % 2] += 1
            stgd.last[i % 2] = (stgd.names[i % 2], t_st)
            prev_tt[0] = tt

    for t in range(4):
        def dst_fc(j, tt=t):
            return fcv[j * 128:(j + 1) * 128, tt * 128:(tt + 1) * 128]
        prep_tile(dr["wfc_sl"][t * 128:(t + 1) * 128, :], D, dst_fc, 8)

    def dst_pr(j):
        return prep_part[woff + j * 128 * 128: woff + (j + 1) * 128 * 128].rearrange(
            "(p n) -> p n", p=128)
    prep_tile(dr["wproj_sl"][:], HIDDEN, dst_pr, 32)

    b.wait("gpsimd", "lds", b.val("lds"))
    for nm in stgd.names:
        b.wait("gpsimd", nm, b.val(nm))
    b.run("gpsimd", lambda e: e.collective_compute(
        "AllGather", ALU.bypass, replica_groups=[list(range(8))],
        ins=[prep_part[:]], outs=[prep_all[:]]), inc="cc")
    pool.free_all([wraw, scth, gt, lt, stg[0], stg[1]])



def _emit_ln(b, bank, ones_col, ones_row, pool, src, dst, tag):
    kind, sh = src
    scratch = []

    def al(n, shp, dt=F32):
        v = pool.alloc(f"{tag}_{n}", shp, dt)
        scratch.append(v)
        return v

    xin = [al(f"x{i}", [128, 512]) for i in range(2)] if kind == "hbm" else None
    xdr = DmaRing(b, f"{tag}xd", 2) if kind == "hbm" else None
    xsq = al("q0", [128, 512])
    fin_m = al("m", [1, 512])
    fin_ms = al("ms", [1, 512])
    fin_v = al("v", [1, 512])
    fin_r = al("r", [1, 512])
    fin_mr = al("mr", [1, 512])
    tmp = al("t", [128, 512])
    qring = Ring(b, 1)

    SB, QB, RB, MB = bank[4], bank[5], bank[6], bank[7]

    def src_tile(c, t):
        if kind == "hbm":
            return None
        return sh[:, c * TC + t * 512: c * TC + (t + 1) * 512]

    for t in range(NT):
        for c in range(NC):
            if kind == "hbm":
                lsem, tl = xdr.load("sync", xin[c % 2][:],
                                    sh[c * 128:(c + 1) * 128, t * 512:(t + 1) * 512], c)
                b.wait("scalar", lsem, tl)
                b.wait("tensor", lsem, tl)
                src_ap = xin[c % 2][:]
            else:
                src_ap = src_tile(c, t)
            qring.acquire("scalar", c)
            ts = b.run("scalar", lambda e, a=src_ap: e.activation(
                xsq[:], a, ACT.Square), inc="sc")
            b.run("tensor", lambda e, a=src_ap, first=(c == 0): e.matmul(
                SB[0:1, :], ones_col[:], a, start=first, stop=(c == NC - 1)))
            b.wait("tensor", "sc", ts)
            tq = b.run("tensor", lambda e, first=(c == 0): e.matmul(
                QB[0:1, :], ones_col[:], xsq[:], start=first, stop=(c == NC - 1)),
                inc="tc")
            qring.release(c, "tc", tq)
            if kind == "hbm":
                xdr.consumed(c, "tc", tq)
        b.wait("vector", "tc", b.val("tc"))
        b.run("vector", lambda e: e.tensor_scalar(fin_m[:], SB[0:1, :], 1.0 / D,
                                                  None, op0=ALU.mult))
        b.run("vector", lambda e: e.tensor_mul(fin_ms[:], fin_m[:], fin_m[:]))
        tv = b.run("vector", lambda e: e.scalar_tensor_tensor(
            fin_v[:], QB[0:1, :], 1.0 / D, fin_ms[:], op0=ALU.mult,
            op1=ALU.subtract), inc="vc")
        b.wait("scalar", "vc", tv)
        b.run("scalar", lambda e: e.activation(fin_v[:], fin_v[:], ACT.Ln,
                                               bias=b.c_eps1e5[0:1, :]))
        ts2 = b.run("scalar", lambda e: e.activation(fin_r[:], fin_v[:], ACT.Exp,
                                                     scale=-0.5), inc="sc")
        b.wait("vector", "sc", ts2)
        tv2 = b.run("vector", lambda e: e.tensor_mul(fin_mr[:], fin_m[:], fin_r[:]),
                    inc="vc")
        b.wait("tensor", "vc", tv2)
        b.run("tensor", lambda e: e.matmul(RB[:, :], ones_row[0:1, :], fin_r[:],
                                           start=True, stop=True))
        tb = b.run("tensor", lambda e: e.matmul(MB[:, :], ones_row[0:1, :], fin_mr[:],
                                                start=True, stop=True), inc="tc")
        b.wait("vector", "tc", tb)
        for c in range(NC):
            if kind == "hbm":
                lsem, tl = xdr.load("sync", xin[c % 2][:],
                                    sh[c * 128:(c + 1) * 128, t * 512:(t + 1) * 512], c)
                b.wait("vector", lsem, tl)
                src_ap = xin[c % 2][:]
            else:
                src_ap = src_tile(c, t)
            b.run("vector", lambda e, a=src_ap: e.tensor_mul(tmp[:], a, RB[:, :]))
            tv3 = b.run("vector", lambda e, c2=c, t2=t: e.tensor_sub(
                dst[:, c2 * TC + t2 * 512: c2 * TC + (t2 + 1) * 512],
                tmp[:], MB[:, :]), inc="vc")
            if kind == "hbm":
                xdr.consumed(c, "vc", tv3)
        b.wait("tensor", "vc", b.val("vc"))
    pool.free_all(scratch)



def _emit_vproj(b, dr, bank, V_sb, xln, kt0, pool):
    wv = pool.alloc("wv_sb", [128, 8 * D], BF16)
    tl = b.dma("gpsimd", wv[:].rearrange("p (c n) -> p c n", c=8),
               dr["wvT"][:].rearrange("(c p) n -> p c n", p=128), "ldg")
    b.wait("tensor", "ldg", tl)
    for kt in range(kt0, kt0 + 16):
        b.run("vector", lambda e, t=kt: e.memset(
            V_sb[:, t * 1040:(t + 1) * 1040].rearrange(
                "p (h n) -> p h n", h=NH)[:, :, HD:HD + 1], 1.0))
    vdone = b.run("vector", lambda e: e.engine_nop(), inc="vc")
    b.wait("scalar", "vc", vdone)

    bring = Ring(b, 4)
    for ktl in range(16):
        kt = kt0 + ktl
        for eh in range(2):
            i = ktl * 2 + eh
            bk = bank[i % 4]
            bring.acquire("tensor", i)
            for dc in range(NC):
                b.run("tensor", lambda e, t=ktl, h=eh, d=dc, bb=bk: e.matmul(
                    bb[:, :], xln[:, d * TC + t * 128: d * TC + (t + 1) * 128],
                    wv[:, d * D + h * 512: d * D + (h + 1) * 512],
                    start=(d == 0), stop=(d == NC - 1)),
                    inc=("tc" if dc == NC - 1 else None))
            tt = b.val("tc")
            b.wait("scalar", "tc", tt)
            ts = b.run("scalar", lambda e, k=kt, h=eh, bb=bk: e.copy(
                V_sb[:, k * 1040:(k + 1) * 1040].rearrange(
                    "p (hh n) -> p hh n", hh=NH)[:, h * 8:(h + 1) * 8, 0:HD],
                bb[:].rearrange("p (hh n) -> p hh n", hh=8)), inc="sc")
            bring.release(i, "sc", ts)
    wv.free()



def _emit_rope_proj(b, dr, bank, dst_sb, xln, dst_span, coff, pool, ropew,
                    wi_a, wi_b, cosd, sind, wring, paired=None):
    if wring < 2 and not paired:
        _emit_rope_proj_single(b, dr, bank, dst_sb, xln, dst_span, coff, pool,
                               ropew, wi_a, wi_b, cosd, sind)
        return
    tdt = F32 if wring >= 2 else BF16
    teng = "sync" if wring >= 2 else "gpsimd"
    tab = pool.alloc("rp_tab", [128, 4 * 512], tdt)
    wsl = [pool.alloc(f"rp_w{i}", [128, 2 * 8 * 128], BF16) for i in range(wring)]
    t1 = pool.alloc("rp_t1", [128, 512], BF16)

    tdr = DmaRing(b, f"rt{wi_a}", 1)
    wdr = DmaRing(b, f"rw{wi_a}", wring)
    pair_ring = Ring(b, 2)
    step = [0]

    for tp in range(NT // 2):
        t0 = 2 * tp
        tdr.load(teng, tab[:, 0:512], cosd[:, t0 * 512:(t0 + 1) * 512], tp)
        ls = None
        for k, srcap in enumerate((sind[:, t0 * 512:(t0 + 1) * 512],
                                   cosd[:, (t0 + 1) * 512:(t0 + 2) * 512],
                                   sind[:, (t0 + 1) * 512:(t0 + 2) * 512])):
            ls = b.dma(teng, tab[:, (k + 1) * 512:(k + 2) * 512], srcap,
                       tdr.names[0])
        b.wait("vector", tdr.names[0], ls)
        last_tv = None
        for ec in range(NC):
            i = step[0]; step[0] += 1
            wv = wsl[i % wring]
            wdr.load("sync", wv[:, 0:1024].rearrange("p (c n) -> p c n", c=8),
                     ropew[wi_a][:, ec * 128:(ec + 1) * 128].rearrange(
                         "(c p) n -> p c n", p=128), i)
            lw = b.dma("sync", wv[:, 1024:2048].rearrange("p (c n) -> p c n", c=8),
                       ropew[wi_b][:, ec * 128:(ec + 1) * 128].rearrange(
                           "(c p) n -> p c n", p=128), wdr.names[i % wring])
            b.wait("tensor", wdr.names[i % wring], lw)
            bs = [bank[(i % 2) * 4 + j] for j in range(4)]
            pair_ring.acquire("tensor", i)
            for dc in range(NC):
                for q in range(2):
                    b.run("tensor", lambda e, d=dc, bb=bs[q], w=wv, tt=t0 + q:
                          e.matmul(
                        bb[:, :], w[:, d * 128:(d + 1) * 128],
                        xln[:, d * TC + tt * 512: d * TC + (tt + 1) * 512],
                        start=(d == 0), stop=(d == NC - 1)))
            tcb = []
            for dc in range(NC):
                for q in range(2):
                    last = (dc == NC - 1)
                    tgt = b.run("tensor", lambda e, d=dc, bb=bs[2 + q], w=wv,
                                tt=t0 + q: e.matmul(
                        bb[:, :], w[:, 1024 + d * 128: 1024 + (d + 1) * 128],
                        xln[:, d * TC + tt * 512: d * TC + (tt + 1) * 512],
                        start=(d == 0), stop=last),
                        inc=("tc" if last else None))
                    if last:
                        tcb.append(tgt)
            for q in range(2):
                b.wait("vector", "tc", tcb[q])
                b.run("vector", lambda e, bb=bs[2 + q], q2=q: e.tensor_mul(
                    t1[:], bb[:, :], tab[:, (2 * q2 + 1) * 512:(2 * q2 + 2) * 512]))
                b.run("vector", lambda e, bb=bs[q], q2=q: e.tensor_mul(
                    bb[:, :], bb[:, :], tab[:, 2 * q2 * 512:(2 * q2 + 1) * 512]))
                tv = b.run("vector", lambda e, c=ec, tt3=t0 + q, bb=bs[q],
                           co=coff: e.tensor_add(
                    dst_sb[:, c * dst_span + co + tt3 * 512:
                           c * dst_span + co + (tt3 + 1) * 512],
                    bb[:, :], t1[:]), inc="vc")
            pair_ring.release(i, "vc", tv)
            wdr.consumed(i, "vc", tv)
            last_tv = tv
        tdr.consumed(tp, "vc", last_tv)
    pool.free_all([tab, t1] + wsl)


def _emit_rope_proj_single(b, dr, bank, dst_sb, xln, dst_span, coff, pool,
                           ropew, wi_a, wi_b, cosd, sind):
    tab = pool.alloc("rp_tab0", [128, 1024], F32)
    wsl = [pool.alloc("rp_w0", [128, 2 * 8 * 128], BF16)]
    t1 = pool.alloc("rp_t1", [128, 512], F32)

    tdr = DmaRing(b, f"rt{wi_a}", 1)
    wdr = DmaRing(b, f"rw{wi_a}", 1)
    pair_ring = Ring(b, 2)
    step = [0]

    for t in range(NT):
        tdr.load("sync", tab[:, 0:512], cosd[:, t * 512:(t + 1) * 512], t)
        ls = b.dma("sync", tab[:, 512:1024], sind[:, t * 512:(t + 1) * 512],
                   tdr.names[0])
        b.wait("vector", tdr.names[0], ls)
        last_tv = None
        for ec in range(NC):
            i = step[0]; step[0] += 1
            wv = wsl[0]
            wdr.load("sync", wv[:, 0:1024].rearrange("p (c n) -> p c n", c=8),
                     ropew[wi_a][:, ec * 128:(ec + 1) * 128].rearrange(
                         "(c p) n -> p c n", p=128), i)
            lw = b.dma("sync", wv[:, 1024:2048].rearrange("p (c n) -> p c n", c=8),
                       ropew[wi_b][:, ec * 128:(ec + 1) * 128].rearrange(
                           "(c p) n -> p c n", p=128), wdr.names[0])
            b.wait("tensor", wdr.names[0], lw)
            bA = bank[(i % 2) * 2]
            bB = bank[(i % 2) * 2 + 1]
            pair_ring.acquire("tensor", i)
            for dc in range(NC):
                b.run("tensor", lambda e, d=dc, bb=bA, w=wv, tt=t: e.matmul(
                    bb[:, :], w[:, d * 128:(d + 1) * 128],
                    xln[:, d * TC + tt * 512: d * TC + (tt + 1) * 512],
                    start=(d == 0), stop=(d == NC - 1)))
            for dc in range(NC):
                b.run("tensor", lambda e, d=dc, bb=bB, w=wv, tt=t: e.matmul(
                    bb[:, :], w[:, 1024 + d * 128: 1024 + (d + 1) * 128],
                    xln[:, d * TC + tt * 512: d * TC + (tt + 1) * 512],
                    start=(d == 0), stop=(d == NC - 1)),
                    inc=("tc" if dc == NC - 1 else None))
            tt2 = b.val("tc")
            b.wait("vector", "tc", tt2)
            b.run("vector", lambda e, bb=bB: e.tensor_mul(
                t1[:], bb[:, :], tab[:, 512:1024]))
            b.run("vector", lambda e, bb=bA: e.tensor_mul(
                bb[:, :], bb[:, :], tab[:, 0:512]))
            tv = b.run("vector", lambda e, c=ec, tt3=t, bb=bA, co=coff: e.tensor_add(
                dst_sb[:, c * dst_span + co + tt3 * 512:
                       c * dst_span + co + (tt3 + 1) * 512],
                bb[:, :], t1[:]), inc="vc")
            pair_ring.release(i, "vc", tv)
            wdr.consumed(i, "vc", tv)
            last_tv = tv
        tdr.consumed(t, "vc", last_tv)
    pool.free_all([tab, t1] + wsl)



def _emit_attention(b, dr, bank, K_sb, V_sb, Q_sb, attn_sb, I_bf,
                    rmask_sb, ones_row, pool):
    cmask_bf = pool.alloc("cmask", [128, 4 * 512], BF16)
    expP = [pool.alloc(f"at_e{i}", [128, 512], BF16) for i in range(4)]
    drow = b.drow_r
    bc_sb = [pool.alloc("at_bc0", [64, 512], F32)]
    lg = b.dma("gpsimd", cmask_bf[:].rearrange("p (r n) -> p r n", r=4),
               dr["cmask4"][:].rearrange("r p n -> p r n"), "ldg")
    b.wait("tensor", "ldg", lg)

    sco = [bank[0], bank[1], bank[2], bank[3]]
    for i in range(120):
        b.run("tensor", lambda e, bb=sco[i % 4]: e.matmul(
            bb[:, :], I_bf[:], cmask_bf[:, 0:512], start=True, stop=True))

    avb = [bank[4], bank[6]]
    bcb = [bank[5], bank[7]]
    sco_ring = Ring(b, 4)
    exp_ring = Ring(b, 4)
    av_ring = Ring(b, 2)
    bc_ring = Ring(b, 2)
    bcs_ring = Ring(b, 1)

    hq = [0]
    tb_prev = [0]
    pending = [None]
    for qt in range(NT):
        for h in range(NH):
            ec, rh = h // 2, (h % 2) * 64
            i_h = hq[0]; hq[0] += 1
            av = avb[i_h % 2]
            kts = list(range(0, 4 * (qt + 1))) + list(range(16, 32))
            nk = len(kts)
            exp_t = {}

            def emit_score(j):
                kt = kts[j]
                bk = sco[j % 4]
                diag = (kt < 16 and kt >= 4 * qt)
                sco_ring.acquire("tensor", j)
                if diag:
                    rel = kt - 4 * qt
                    b.run("tensor", lambda e, bb=bk, r=rel: e.matmul(
                        bb[:, :], I_bf[:], cmask_bf[:, r * 512:(r + 1) * 512],
                        start=True, stop=False))
                tsc = b.run("tensor", lambda e, bb=bk, k=kt, c=ec, r=rh, q=qt,
                            dg=diag: e.matmul(
                    bb[:, :],
                    K_sb[r:r + 64, c * S + k * 128: c * S + (k + 1) * 128],
                    Q_sb[r:r + 64, c * TC + q * 512: c * TC + (q + 1) * 512],
                    start=(not dg), stop=True), inc="tc")
                b.wait("scalar", "tc", tsc)
                exp_ring.acquire("scalar", j)
                rect = kt >= 16
                te = b.run("scalar", lambda e, bb=bk, sl=j % 4, rc=rect: e.activation(
                    expP[sl][:], bb[:, :], ACT.Exp,
                    bias=(rmask_sb[:] if rc else 0.0), scale=0.125), inc="sc")
                exp_t[j] = te
                sco_ring.release(j, "sc", te)

            def emit_av(j):
                b.wait("tensor", "sc", exp_t[j])
                if j == 0:
                    av_ring.acquire("tensor", i_h)
                ta = b.run("tensor", lambda e, k=kts[j], hh=h, sl=j % 4,
                           aa=av: e.matmul(
                    aa[0:65, :],
                    V_sb[:, k * 1040 + hh * 65: k * 1040 + (hh + 1) * 65],
                    expP[sl][:],
                    start=(j == 0), stop=(j == nk - 1)), inc="tc")
                exp_ring.release(j, "tc", ta)
                return ta

            for j in range(4):
                emit_score(j)
            for j in range(4, nk):
                emit_av(j - 4)
                if j == 10 and pending[0] is not None:
                    pending[0]()
                    pending[0] = None
                emit_score(j)
            for j in range(nk - 4, nk - 1):
                emit_av(j)
            ta = emit_av(nk - 1)

            b.wait("vector", "tc", max(ta, tb_prev[0]))

            def _recip(e, aa):
                with b.nc.allow_low_precision(reason="float32r is fp32"):
                    return e.reciprocal(drow[0:1, :], aa[64:65, :])
            trc = b.run("vector", lambda e, aa=av, f=_recip: f(e, aa), inc="vc")

            def tail(ih=i_h, aa=av, c=ec, r=rh, q=qt, trc=trc):
                bc_ring.acquire("tensor", ih)
                b.wait("tensor", "vc", trc)
                tb = b.run("tensor", lambda e, bb=bcb[ih % 2]: e.matmul(
                    bb[0:64, :], b.ones_row_r[0:1, 0:64], drow[0:1, :],
                    start=True, stop=True), inc="tc")
                tb_prev[0] = tb
                bcs_ring.acquire("vector", ih)
                b.wait("vector", "tc", tb)
                tcp = b.run("vector", lambda e, bb=bcb[ih % 2], s2=bc_sb[0]:
                            e.tensor_copy(s2[:], bb[0:64, :]), inc="vc")
                bc_ring.release(ih, "vc", tcp)
                tn = b.run("vector", lambda e, s2=bc_sb[0]: e.tensor_mul(
                    attn_sb[r:r + 64, c * TC + q * 512: c * TC + (q + 1) * 512],
                    aa[0:64, :], s2[:]), inc="vc")
                av_ring.release(ih, "vc", tn)
                bcs_ring.release(ih, "vc", tn)

            pending[0] = tail
    pending[0]()
    pending[0] = None
    pool.free_all([cmask_bf] + expP + bc_sb)



def _emit_oproj(b, dr, bank, attn_sb, resid, pool):
    wo = pool.alloc("wo_sb", [128, 8 * D], BF16)
    xbuf = [pool.alloc(f"ox{i}", [128, 512], F32) for i in range(4)]
    tl = b.dma("gpsimd", wo[:].rearrange("p (c n) -> p c n", c=8),
               dr["woT"][:].rearrange("(c p) n -> p c n", p=128), "ldg")
    b.wait("tensor", "ldg", tl)
    bring = Ring(b, 2)
    xdr = DmaRing(b, "oprx", 4)
    for ec in range(NC):
        bs = [bank[(ec % 2) * 4 + j] for j in range(4)]
        xs = []
        for t in range(NT):
            xs.append(xdr.load("sync", xbuf[t][:],
                               dr["x_own"][ec * 128:(ec + 1) * 128,
                                           t * 512:(t + 1) * 512], ec * NT + t))
        bring.acquire("tensor", ec)
        tcb = []
        for dc in range(NC):
            for t in range(NT):
                last = (dc == NC - 1)
                tgt = b.run("tensor", lambda e, d=dc, c=ec, tt=t, bb=bs[t]:
                            e.matmul(
                    bb[:, :], wo[:, d * D + c * 128: d * D + (c + 1) * 128],
                    attn_sb[:, d * TC + tt * 512: d * TC + (tt + 1) * 512],
                    start=(d == 0), stop=(d == NC - 1)),
                    inc=("tc" if last else None))
                if last:
                    tcb.append(tgt)
        tv = None
        for t in range(NT):
            b.wait("vector", "tc", tcb[t])
            b.wait("vector", xs[t][0], xs[t][1])
            tv = b.run("vector", lambda e, c=ec, tt3=t, bb=bs[t], xb=xbuf[t]:
                       e.tensor_add(resid[:, c * TC + tt3 * 512:
                                          c * TC + (tt3 + 1) * 512],
                                    xb[:], bb[:, :]), inc="vc")
            xdr.consumed(ec * NT + t, "vc", tv)
        bring.release(ec, "vc", tv)
    pool.free_all([wo] + xbuf)



def _emit_ffn0(b, dr, bank, resid, ones_col, ones_row, pool):
    xln2 = pool.alloc("xln2", [128, NC * TC], BF16)
    _emit_ln(b, bank, ones_col, ones_row, pool,
             src=("sbuf", resid), dst=xln2, tag="ln1")
    b.barrier()

    wsb = pool.alloc("W_sb", [128, 8 * D], BF16)
    vr = pool.alloc("vr_sb", [128, 8 * 32], BF16)
    ct = pool.alloc("ct_sb", [32, D], BF16)
    bsb = pool.alloc("b_sb", [128, 8], F32)
    r1 = pool.alloc("r1_sb", [32, 4 * 512], BF16)
    b.dma("gpsimd", wsb[:].rearrange("p (c n) -> p c n", c=8),
          dr["WT"][:].rearrange("(c p) n -> p c n", p=128), "ldg")
    b.dma("gpsimd", vr[:].rearrange("p (c n) -> p c n", c=8),
          dr["Vr"][:].rearrange("(c p) n -> p c n", p=128), "ldg")
    b.dma("gpsimd", ct[:], dr["CT"][:], "ldg")
    for c in range(NC):
        b.dma("sync", bsb[:, c:c + 1], dr["bcol"][c * 128:(c + 1) * 128, :], "lds")
    b.wait("tensor", "ldg", b.val("ldg"))
    b.wait("vector", "lds", b.val("lds"))

    ts = None
    for t in range(NT):
        rb = bank[6]
        for dc in range(NC):
            b.run("tensor", lambda e, d=dc, tt=t: e.matmul(
                rb[0:32, :], vr[:, d * 32:(d + 1) * 32],
                xln2[:, d * TC + tt * 512: d * TC + (tt + 1) * 512],
                start=(d == 0), stop=(d == NC - 1)),
                inc=("tc" if dc == NC - 1 else None))
        tt1 = b.val("tc")
        b.wait("scalar", "tc", tt1)
        ts = b.run("scalar", lambda e, tt=t: e.copy(
            r1[:, tt * 512:(tt + 1) * 512], rb[0:32, :]), inc="sc")
        b.wait("tensor", "sc", ts)

    bring = Ring(b, 2)
    for ec in range(NC):
        bs = [bank[(ec % 2) * 4 + j] for j in range(4)]
        bring.acquire("tensor", ec)
        for t in range(NT):
            b.run("tensor", lambda e, c=ec, tt=t, bb=bs[t]: e.matmul(
                bb[:, :], ct[:, c * 128:(c + 1) * 128],
                r1[:, tt * 512:(tt + 1) * 512],
                start=True, stop=False))
        tcb = []
        for dc in range(NC):
            for t in range(NT):
                last = (dc == NC - 1)
                tgt = b.run("tensor", lambda e, d=dc, c=ec, tt2=t, bb=bs[t]:
                            e.matmul(
                    bb[:, :], wsb[:, d * D + c * 128: d * D + (c + 1) * 128],
                    xln2[:, d * TC + tt2 * 512: d * TC + (tt2 + 1) * 512],
                    start=False, stop=(d == NC - 1)),
                    inc=("tc" if last else None))
                if last:
                    tcb.append(tgt)
        tv = None
        for t in range(NT):
            b.wait("vector", "tc", tcb[t])
            tv = b.run("vector", lambda e, c=ec, tt4=t, bb=bs[t]:
                       e.scalar_tensor_tensor(
                resid[:, c * TC + tt4 * 512: c * TC + (tt4 + 1) * 512], bb[:, :],
                bsb[:, c:c + 1],
                resid[:, c * TC + tt4 * 512: c * TC + (tt4 + 1) * 512],
                op0=ALU.add, op1=ALU.add), inc="vc")
        bring.release(ec, "vc", tv)
    pool.free_all([xln2, wsb, vr, ct, bsb, r1])



def _emit_l1attn(b, dr, bank, pairs, resid, ones_col, ones_row, csel_sb,
                 carry_d, carry_all, pool, dbg=None):
    xln3 = pool.alloc("xln3", [128, NC * TC], BF16)
    _emit_ln(b, bank, ones_col, ones_row, pool,
             src=("sbuf", resid), dst=xln3, tag="ln2")
    b.barrier()

    bd = {n: pool.alloc(f"{n}_sb", [128, 8 * 128], BF16)
          for n in ("bdf", "bdi", "bdh")}
    F_sb = pool.alloc("F_sb", [128, NC * TC], F32)
    v_sb = pool.alloc("v_sb", [128, TC], F32)
    h_sb = pool.alloc("h_sb", [128, TC], F32)
    zero_sb = pool.alloc("zero_sb", [128, 1], F32)
    t0 = pool.alloc("g_t0", [128, 1024], F32)
    t1 = pool.alloc("g_t1", [128, 1024], F32)
    t2_ = pool.alloc("g_t2", [128, 1024], F32)
    t3 = pool.alloc("g_t3", [128, 1024], F32)
    mk = pool.alloc("g_mk", [128, 1024], U8)
    carry_sb = pool.alloc("carry_sb", [128, 8], F32)
    cg_sb = pool.alloc("cg_sb", [8, D], F32)
    hin = pool.alloc("hin_sb", [128, 8], F32)

    for n in ("bdf", "bdi", "bdh"):
        b.dma("gpsimd", bd[n][:].rearrange("p (c n2) -> p c n2", c=8),
              dr[n][:].rearrange("c p n2 -> p c n2"), "ldg")
    b.run("vector", lambda e: e.memset(zero_sb[:], 0.0))
    zero_bc = zero_sb[:].broadcast_to([128, TC])
    b.wait("tensor", "ldg", b.val("ldg"))

    gate_ring = Ring(b, 1)
    for c in range(NC):
        for t2 in range(2):
            gate_ring.acquire("tensor", 0)
            for gi, gname in enumerate(("bdf", "bdi", "bdh")):
                for sub in range(2):
                    b.run("tensor", lambda e, cc=c, tt=t2 * 2 + sub, gb=bd[gname],
                          pi_=pairs[gi], s2=sub: e.matmul(
                        pi_[:, s2 * 512:(s2 + 1) * 512],
                        gb[:, cc * 128:(cc + 1) * 128],
                        xln3[:, cc * TC + tt * 512: cc * TC + (tt + 1) * 512],
                        start=True, stop=True),
                        inc=("tc" if (gi == 2 and sub == 1) else None))
            tg = b.val("tc")
            b.wait("scalar", "tc", tg)
            b.wait("vector", "tc", tg)
            pf = pairs[0][:, 0:1024]
            pi = pairs[1][:, 0:1024]
            ph_ = pairs[2][:, 0:1024]
            sl = slice(t2 * 1024, (t2 + 1) * 1024)
            b.run("scalar", lambda e, p=pf: e.activation(t0[:], p, ACT.Exp, scale=-1.0))
            b.run("scalar", lambda e: e.activation(t0[:], t0[:], ACT.Ln, bias=1.0))
            b.run("scalar", lambda e, p=pi: e.activation(t1[:], p, ACT.Exp, scale=-1.0))
            ts1 = b.run("scalar", lambda e: e.activation(t1[:], t1[:], ACT.Ln, bias=1.0),
                        inc="sc")
            b.wait("vector", "sc", ts1)
            tv1 = b.run("vector", lambda e: e.tensor_sub(t2_[:], t0[:], t1[:]), inc="vc")
            b.wait("scalar", "vc", tv1)
            b.run("scalar", lambda e, cc=c, ss=sl: e.activation(
                F_sb[:, cc * TC + ss.start: cc * TC + ss.stop], t2_[:],
                ACT.Sigmoid, scale=-1.0))
            b.run("scalar", lambda e: e.activation(t2_[:], t2_[:], ACT.Sigmoid))
            ts3 = b.run("scalar", lambda e, p=ph_: e.activation(
                t0[:], p, ACT.Sigmoid), inc="sc")
            b.wait("vector", "sc", ts3)
            b.run("vector", lambda e, p=ph_: e.tensor_scalar(mk[:], p, 0.0, None,
                                                             op0=ALU.is_ge))
            b.run("vector", lambda e, p=ph_: e.tensor_scalar(t1[:], p, 0.5 + EPS,
                                                             None, op0=ALU.add))
            b.run("vector", lambda e: e.select(t3[:], mk[:], t1[:], t0[:]))
            tv3 = b.run("vector", lambda e, ss=sl: e.tensor_mul(
                v_sb[:, ss], t2_[:], t3[:]), inc="vc")
            gate_ring.release(0, "vc", tv3)
        tv4 = b.run("vector", lambda e: e.engine_nop(), inc="vc")
        b.wait("vector", "vc", tv4)
        b.run("vector", lambda e, cc=c: e.tensor_tensor_scan(
            h_sb[:], F_sb[:, cc * TC:(cc + 1) * TC], v_sb[:], 0.0,
            op0=ALU.mult, op1=ALU.add))
        b.run("vector", lambda e, cc=c: e.tensor_add(
            resid[:, cc * TC:(cc + 1) * TC], resid[:, cc * TC:(cc + 1) * TC], h_sb[:]))
        b.run("vector", lambda e, cc=c: e.tensor_copy(
            carry_sb[:, cc:cc + 1], h_sb[:, TC - 1:TC]), inc="vc")
    b.wait("sync", "vc", b.val("vc"))
    if dbg is not None and "F" in dbg:
        import kernel as _k
        if _k.DEV_DUMP:
            for cc in range(NC):
                b.dma("sync", dbg["F"][cc * 128:(cc + 1) * 128, :],
                      F_sb[:, cc * TC:(cc + 1) * TC], "sts")
                b.dma("gpsimd", dbg["x3"][cc * 128:(cc + 1) * 128, :],
                      xln3[:, cc * TC:(cc + 1) * TC], "ldg")
            b.wait("sync", "sts", b.val("sts"))
            b.wait("gpsimd", "ldg", b.val("ldg"))
    for c in range(NC):
        b.dma("sync", carry_d[c * 128:(c + 1) * 128], carry_sb[:, c:c + 1], "lds")
    b.wait("gpsimd", "lds", b.val("lds"))
    tcc = b.run("gpsimd", lambda e: e.collective_compute(
        "AllGather", ALU.bypass, replica_groups=[list(range(8))],
        ins=[carry_d[:]], outs=[carry_all[:]]), inc="cc")
    b.wait("sync", "cc", tcc)
    tcg = b.dma("sync", cg_sb[:], carry_all[:], "lds")
    b.wait("tensor", "lds", tcg)
    for c in range(NC):
        b.run("tensor", lambda e, cc=c: e.matmul(
            bank[5][:, cc:cc + 1], cg_sb[:, cc * 128:(cc + 1) * 128], csel_sb[:],
            start=True, stop=True), inc=("tc" if c == NC - 1 else None))
    tm = b.val("tc")
    b.wait("vector", "tc", tm)
    tvh = b.run("vector", lambda e: e.tensor_copy(hin[:], bank[5][:, 0:8]), inc="vc")
    b.wait("vector", "vc", tvh)
    for c in range(NC):
        b.run("vector", lambda e, cc=c: e.tensor_tensor_scan(
            h_sb[:], F_sb[:, cc * TC:(cc + 1) * TC], zero_bc, 1.0,
            op0=ALU.mult, op1=ALU.add))
        b.run("vector", lambda e, cc=c: e.scalar_tensor_tensor(
            resid[:, cc * TC:(cc + 1) * TC], h_sb[:], hin[:, cc:cc + 1],
            resid[:, cc * TC:(cc + 1) * TC], op0=ALU.mult, op1=ALU.add))
    pool.free_all([xln3, bd["bdf"], bd["bdi"], bd["bdh"], F_sb, v_sb, h_sb,
                   zero_sb, t0, t1, t2_, t3, mk, carry_sb, cg_sb, hin])



def _emit_ffn1(b, dr, bank, resid, ones_col, ones_row, prep_all, pool, dbg=None):
    xln4 = pool.alloc("xln4", [128, NC * TC], BF16)
    _emit_ln(b, bank, ones_col, ones_row, pool,
             src=("sbuf", resid), dst=xln4, tag="ln3")
    b.barrier()

    vt = pool.alloc("vt_sb", [128, 8 * RANK], BF16)
    fb = pool.alloc("fb_sb", [RANK, RANK], BF16)
    ut = pool.alloc("ut_sb", [RANK, HIDDEN], BF16)
    g16 = pool.alloc("g16_sb", [RANK, 1], F32)
    s1 = pool.alloc("s1_sb", [RANK, 512], BF16)
    s2 = pool.alloc("s2_sb", [RANK, 512], BF16)
    s1x = pool.alloc("s1x_sb", [RANK, 512], BF16)
    rsq = pool.alloc("rsq_sb", [128, 32 * 1024], BF16)
    relu_t = pool.alloc("relu_t", [128, 512], F32)
    wcol = [pool.alloc(f"wcol{i}", [128, 8 * 128], BF16) for i in range(2)]
    wpb = [pool.alloc(f"wpb{i}", [128, 32 * 128], BF16) for i in range(2)]

    b.dma("gpsimd", vt[:].rearrange("p (c n) -> p c n", c=8),
          dr["vT"][:].rearrange("(c p) n -> p c n", p=128), "ldg")
    b.dma("gpsimd", fb[:], dr["fbT"][:], "ldg")
    b.dma("gpsimd", ut[:], dr["uT"][:], "ldg")
    tg = b.dma("sync", g16[:], dr["gate16"][:], "lds")
    b.wait("tensor", "ldg", b.val("ldg"))
    b.wait("vector", "lds", tg)

    import kernel as _k
    if dbg is not None and _k.DEV_DUMP:
        vscr = pool.alloc("vscr", [128, TC], F32)
        for cc in range(NC):
            b.dma("gpsimd", dbg["x4"][cc * 128:(cc + 1) * 128, :],
                  xln4[:, cc * TC:(cc + 1) * TC], "ldg")
            tvv = b.run("vector", lambda e, c2=cc: e.tensor_copy(
                vscr[:], xln4[:, c2 * TC:(c2 + 1) * TC]), inc="vc")
            b.wait("sync", "vc", tvv)
            b.dma("sync", dbg["x3"][cc * 128:(cc + 1) * 128, :], vscr[:], "sts")
            b.wait("vector", "sts", b.val("sts"))
        b.wait("gpsimd", "ldg", b.val("ldg"))
        b.wait("sync", "sts", b.val("sts"))
        vscr.free()
    wcd = DmaRing(b, "f1wc", 2)
    wpd = DmaRing(b, "f1wp", 2)
    fc_ring = Ring(b, 2)
    pj_ring = Ring(b, 2)

    s2ab = [s1, s2]
    for qp in range(2):
        q0 = 2 * qp
        rb = bank[0]
        fc_ring.acquire("tensor", 0)
        fc_ring.acquire("tensor", 1)
        for qq in range(2):
            for dc in range(NC):
                b.run("tensor", lambda e, d=dc, q=q0 + qq: e.matmul(
                    rb[0:RANK, :], vt[:, d * RANK:(d + 1) * RANK],
                    xln4[:, d * TC + q * 512: d * TC + (q + 1) * 512],
                    start=(d == 0), stop=(d == NC - 1)),
                    inc=("tc" if dc == NC - 1 else None))
            tt = b.val("tc")
            b.wait("scalar", "tc", tt)
            ts = b.run("scalar", lambda e: e.copy(s1x[:], rb[0:RANK, :]), inc="sc")
            b.wait("tensor", "sc", ts)
            b.run("tensor", lambda e: e.matmul(rb[0:RANK, :], fb[:], s1x[:],
                                               start=True, stop=True), inc="tc")
            tt2 = b.val("tc")
            b.wait("vector", "tc", tt2)
            tv = b.run("vector", lambda e, s_=s2ab[qq]: e.tensor_scalar(
                s_[:], rb[0:RANK, :], g16[:], None, op0=ALU.mult), inc="vc")
            b.wait("tensor", "vc", tv)

        for hc in range(32):
            i = qp * 32 + hc
            r_ = (hc * 128) // 512
            lc = (hc * 128) % 512
            wsem, lw = wcd.load(
                "sync", wcol[i % 2][:].rearrange("p (c n) -> p c n", c=8),
                prep_all[r_, 0:D * 512].rearrange("(d n) -> d n", n=512)[
                    :, lc:lc + 128].rearrange("(c p) n -> p c n", p=128), i)
            fc_ring.acquire("tensor", hc)
            bkA = bank[(hc % 2) * 2]
            bkB = bank[(hc % 2) * 2 + 1]
            b.run("tensor", lambda e, h2=hc, bb=bkA: e.matmul(
                bb[:, :], ut[:, h2 * 128:(h2 + 1) * 128], s2ab[0][:],
                start=True, stop=False))
            b.run("tensor", lambda e, h2=hc, bb=bkB: e.matmul(
                bb[:, :], ut[:, h2 * 128:(h2 + 1) * 128], s2ab[1][:],
                start=True, stop=False))
            b.wait("tensor", wsem, lw)
            for dc in range(NC):
                for qq in range(2):
                    b.run("tensor",
                          lambda e, d=dc, q=q0 + qq, wi=i % 2,
                          bb=(bkA if qq == 0 else bkB): e.matmul(
                              bb[:, :], wcol[wi][:, d * 128:(d + 1) * 128],
                              xln4[:, d * TC + q * 512: d * TC + (q + 1) * 512],
                              start=False, stop=(d == NC - 1)),
                          inc=("tc" if dc == NC - 1 and qq == 1 else None))
            tt3 = b.val("tc")
            b.wait("scalar", "tc", tt3)
            for qq, bb in ((0, bkA), (1, bkB)):
                b.run("scalar", lambda e, bb=bb: e.activation(
                    relu_t[:], bb[:, :], ACT.Relu))
                tv2 = b.run("scalar", lambda e, h2=hc, q2=qq: e.activation(
                    rsq[:, h2 * 1024 + q2 * 512: h2 * 1024 + (q2 + 1) * 512],
                    relu_t[:], ACT.Square), inc="sc")
            fc_ring.release(hc, "sc", tv2)
            wcd.consumed(i, "sc", tv2)
        rsq_done = b.val("sc")

        for dco in range(NC):
            i = qp * NC + dco
            woff = D * 512
            psem, lw = wpd.load(
                "sync", wpb[i % 2][:].rearrange("p (c n) -> p c n", c=32),
                prep_all[dco, woff:woff + HIDDEN * 128].rearrange(
                    "(c p n) -> p c n", p=128, n=128), i)
            b.wait("tensor", psem, lw)
            b.wait("tensor", "sc", rsq_done)
            pj_ring.acquire("tensor", dco)
            bkA = bank[4 + (dco % 2) * 2]
            bkB = bank[5 + (dco % 2) * 2]
            for hc in range(32):
                for qq in range(2):
                    b.run("tensor",
                          lambda e, h2=hc, wi=i % 2, q2=qq,
                          bb=(bkA if qq == 0 else bkB): e.matmul(
                              bb[:, :], wpb[wi][:, h2 * 128:(h2 + 1) * 128],
                              rsq[:, h2 * 1024 + q2 * 512:
                                  h2 * 1024 + (q2 + 1) * 512],
                              start=(h2 == 0), stop=(h2 == 31)),
                          inc=("tc" if hc == 31 and qq == 1 else None))
            tt4 = b.val("tc")
            b.wait("vector", "tc", tt4)
            for qq, bb in ((0, bkA), (1, bkB)):
                tv3 = b.run("vector", lambda e, d=dco, q=q0 + qq, bb=bb:
                            e.tensor_add(
                    resid[:, d * TC + q * 512: d * TC + (q + 1) * 512],
                    resid[:, d * TC + q * 512: d * TC + (q + 1) * 512],
                    bb[:, :]), inc="vc")
            pj_ring.release(dco, "vc", tv3)
            wpd.consumed(i, "vc", tv3)
    pool.free_all([xln4, vt, fb, ut, g16, s1x, s1, s2, rsq, relu_t]
                  + wcol + wpb)



def kernel(**inputs):
    sh, per = stage_inputs(inputs)
    nc = build_program()
    in_maps = [{**sh, **per[c]} for c in range(8)]
    res = run_bass_kernel_spmd(nc, in_maps, core_ids=list(range(8)))
    out = np.empty((4, S, D), np.float32)
    for c in range(8):
        seq, half = c // 2, c % 2
        out[seq, half * TC:(half + 1) * TC, :] = res.results[c]["y"].T
    return out


# version 19
# speedup vs baseline: 1.0219x; 1.0075x over previous
import numpy as np
from contextlib import ExitStack

import concourse.bass as bass
import concourse.mybir as mybir
from concourse.bass_utils import run_bass_kernel_spmd

F32 = mybir.dt.float32
BF16 = mybir.dt.bfloat16
U8 = mybir.dt.uint8
F32R = mybir.dt.float32r
AX = mybir.AxisListType.X
ALU = mybir.AluOpType
ACT = mybir.ActivationFunctionType

D = 1024
TC = 2048
S = 4096
NH = 16
HD = 64
NC = 8
NT = TC // 512
EPS = 1e-8
NEG = -1.0e9
HIDDEN = 4096
RANK = 16
PREP_N = D * 512 + HIDDEN * 128

N_BLOCKS = 4
DEV_DUMP = False

ENGINES = ("sync", "tensor", "scalar", "vector", "gpsimd")


class Bld:
    def __init__(self, nc, stack):
        self.nc = nc
        self.stack = stack
        self.ops = {e: [] for e in ENGINES}
        self.sem = {}
        self.cnt = {}
        self.nbar = 0

    def new_sem(self, name):
        self.sem[name] = self.stack.enter_context(self.nc.semaphore(name))
        self.cnt[name] = 0

    def add(self, eng, fn):
        self.ops[eng].append(fn)

    def bump(self, name, k=1):
        self.cnt[name] += k
        return self.cnt[name]

    def val(self, name):
        return self.cnt[name]

    def dma(self, eng, out_ap, in_ap, sem):
        tgt = self.bump(sem, 16)
        h = self.sem[sem]
        self.add(eng, lambda e, o=out_ap, i=in_ap: e.dma_start(out=o, in_=i).then_inc(h, 16))
        return tgt

    def wait(self, eng, sem, tgt):
        if tgt <= 0:
            return
        h = self.sem[sem]
        self.add(eng, lambda e, t=tgt: e.wait_ge(h, t))

    def run(self, eng, fn, inc=None, k=1):
        if inc is None:
            self.add(eng, fn)
            return None
        tgt = self.bump(inc, k)
        h = self.sem[inc]
        self.add(eng, lambda e, f=fn, kk=k: f(e).then_inc(h, kk))
        return tgt

    def barrier(self):
        self.nbar += 1
        tgt = self.bump("ph", 5)
        h = self.sem["ph"]
        for eng in ENGINES:
            if eng in ("vector", "gpsimd"):
                self.add(eng, lambda e: e.engine_nop().then_inc(h, 1))
            elif eng == "scalar":
                d = self.dummy
                self.add(eng, lambda e: e.copy(d[0:1, 0:1], d[0:1, 1:2]).then_inc(h, 1))
            elif eng == "tensor":
                self.add(eng, lambda e: e.nop().then_inc(h, 1))
            else:
                self.add(eng, lambda e: e.sem_inc(h, 1))
            self.add(eng, lambda e, t=tgt: e.wait_ge(h, t))


class Ring:
    def __init__(self, bld, n):
        self.b = bld
        self.n = n
        self.last = [None] * n

    def acquire(self, eng, i):
        lu = self.last[i % self.n]
        if lu is not None:
            self.b.wait(eng, lu[0], lu[1])

    def release(self, i, sem, tgt):
        self.last[i % self.n] = (sem, tgt)


class DmaRing:

    _uid = [0]

    def __init__(self, bld, name, depth):
        self.b = bld
        self.depth = depth
        self.names = []
        DmaRing._uid[0] += 1
        for i in range(depth):
            nm = f"{name}{DmaRing._uid[0]}_{i}"
            bld.new_sem(nm)
            self.names.append(nm)
        self.last = [None] * depth
        self.uses = [0] * depth

    def load(self, eng, out_ap, in_ap, i):
        s = i % self.depth
        lu = self.last[s]
        if lu is not None:
            self.b.wait(eng, lu[0], lu[1])
        tgt = self.b.dma(eng, out_ap, in_ap, self.names[s])
        return (self.names[s], tgt)

    def consumed(self, i, sem, tgt):
        self.last[i % self.depth] = (sem, tgt)


class View:

    def __init__(self, pool, th, off, p, w):
        self.pool = pool
        self.th = th
        self.off = off
        self.p = p
        self.w = w

    def __getitem__(self, idx):
        if idx == slice(None):
            return self.th[0:self.p, self.off:self.off + self.w]
        if isinstance(idx, tuple):
            ps, cs = idx
        else:
            ps, cs = idx, slice(None)
        p0 = 0 if ps.start is None else ps.start
        p1 = self.p if ps.stop is None else ps.stop
        c0 = 0 if cs.start is None else cs.start
        c1 = self.w if cs.stop is None else cs.stop
        return self.th[p0:p1, self.off + c0:self.off + c1]

    def free(self):
        self.pool.free(self)


class Pool:

    def __init__(self, nc, nbytes):
        self.arena = nc.alloc_sbuf_tensor("arena", [128, nbytes], U8)
        self.views = {F32: self.arena.bitcast(F32), BF16: self.arena.bitcast(BF16),
                      U8: self.arena, F32R: self.arena.bitcast(F32R)}
        self.free_list = [(0, nbytes)]
        self.nbytes = nbytes
        self.hwm = 0

    verbose = False

    def alloc(self, name, shape, dt=F32):
        p, w = shape
        esz = {F32: 4, F32R: 4, BF16: 2, U8: 1}[dt]
        need = w * esz
        align = 32
        for i, (st, sz) in enumerate(self.free_list):
            a = (st + align - 1) // align * align
            if a + need <= st + sz:
                pre = a - st
                post = (st + sz) - (a + need)
                repl = []
                if pre:
                    repl.append((st, pre))
                if post:
                    repl.append((a + need, post))
                self.free_list[i:i + 1] = repl
                self.hwm = max(self.hwm, a + need)
                if Pool.verbose:
                    print(f"POOL {name}: [{a}, {a+need})")
                v = View(self, self.views[dt], a // esz, p, w)
                v._range = (a, need)
                return v
        raise RuntimeError(f"SBUF pool exhausted allocating {name} "
                           f"({need}B): free={self.free_list}")

    def free(self, v):
        st, sz = v._range
        self.free_list.append((st, sz))
        self.free_list.sort()
        merged = []
        for rng in self.free_list:
            if merged and merged[-1][0] + merged[-1][1] == rng[0]:
                merged[-1] = (merged[-1][0], merged[-1][1] + rng[1])
            else:
                merged.append(list(rng) if isinstance(rng, tuple) else rng)
        self.free_list = [tuple(m) for m in merged]

    def free_all(self, vs):
        for v in vs:
            self.free(v)



def _rope_tables():
    j = np.arange(0, HD, 2, dtype=np.float64)
    inv = 1.0 / (10000.0 ** (j / HD))
    pos = np.arange(S, dtype=np.float64)
    fr = pos[:, None] * inv[None, :]
    cos = np.cos(fr).T
    sin = np.sin(fr).T
    cosf = np.empty((HD, S), np.float64)
    sinS = np.empty((HD, S), np.float64)
    cosf[0::2] = cos
    cosf[1::2] = cos
    sinS[0::2] = -sin
    sinS[1::2] = sin
    return (np.ascontiguousarray(np.tile(cosf, (2, 1)).astype(np.float32)),
            np.ascontiguousarray(np.tile(sinS, (2, 1)).astype(np.float32)))


def _swap_pairs(n):
    idx = np.arange(n)
    idx[0::2] += 1
    idx[1::2] -= 1
    return idx


def _blockdiag(W):
    out = np.zeros((8, 128, 128), np.float32)
    for g in range(128):
        c, loc = g // 16, (g % 16) * 8
        out[c, loc:loc + 8, loc:loc + 8] = W[g].T
    return out


def _cmask4():
    cm = np.zeros((4, 128, 512), np.float32)
    i = np.arange(128)[:, None]
    jj = np.arange(512)[None, :]
    for r in range(4):
        cm[r] = np.where(i + r * 128 <= jj, 0.0, NEG)
    return cm


def stage_inputs(inp):
    x = np.asarray(inp["L0_attn_x"], np.float32)
    cos2, sin2 = _rope_tables()
    sw = _swap_pairs(D)

    sh = {}
    sh["wqT"] = np.ascontiguousarray(np.asarray(inp["L0_attn_wq"], np.float32).T)
    sh["wkT"] = np.ascontiguousarray(np.asarray(inp["L0_attn_wk"], np.float32).T)
    sh["wqswT"] = np.ascontiguousarray(sh["wqT"][:, sw])
    sh["wkswT"] = np.ascontiguousarray(sh["wkT"][:, sw])
    sh["wvT"] = np.ascontiguousarray(np.asarray(inp["L0_attn_wv"], np.float32).T)
    sh["woT"] = np.ascontiguousarray(np.asarray(inp["L0_attn_wo"], np.float32).T)
    sh["WT"] = np.ascontiguousarray(np.asarray(inp["L0_ffn_W"], np.float32).T)
    sh["bcol"] = np.ascontiguousarray(np.asarray(inp["L0_ffn_b"], np.float32).reshape(D, 1))
    sh["Vr"] = np.ascontiguousarray(np.asarray(inp["L0_ffn_V_r"], np.float32))
    sh["CT"] = np.ascontiguousarray(np.asarray(inp["L0_ffn_C"], np.float32).T)
    sh["bdf"] = _blockdiag(np.asarray(inp["L1_attn_Wf"], np.float32))
    sh["bdi"] = _blockdiag(np.asarray(inp["L1_attn_Wi"], np.float32))
    sh["bdh"] = _blockdiag(np.asarray(inp["L1_attn_Wh"], np.float32))
    sh["uT"] = np.ascontiguousarray(np.asarray(inp["L1_ffn_u"], np.float32).T)
    sh["vT"] = np.ascontiguousarray(np.asarray(inp["L1_ffn_v"], np.float32).T)
    sh["fbT"] = np.ascontiguousarray(np.asarray(inp["L1_ffn_fast_b"], np.float32).T)
    sh["gate16"] = np.full((16, 1), np.float32(np.asarray(inp["L1_ffn_gate"])), np.float32)
    sh["eye"] = np.eye(128, dtype=np.float32)
    sh["cmask4"] = _cmask4()
    sh["onescol"] = np.ones((128, 1), np.float32)

    wfc = np.asarray(inp["L1_ffn_w_fc"], np.float32)
    wproj = np.asarray(inp["L1_ffn_w_proj"], np.float32)

    per = []
    for c in range(8):
        seq, half = c // 2, c % 2
        own = slice(half * TC, half * TC + TC)
        par = slice((1 - half) * TC, (1 - half) * TC + TC)
        m = {}
        m["x_own"] = np.ascontiguousarray(x[seq, own].T)
        m["x_par"] = np.ascontiguousarray(x[seq, par].T)
        m["cos_own"] = np.ascontiguousarray(cos2[:, own])
        m["sin_own"] = np.ascontiguousarray(sin2[:, own])
        m["cos_par"] = np.ascontiguousarray(cos2[:, par])
        m["sin_par"] = np.ascontiguousarray(sin2[:, par])
        m["rmask"] = np.full((128, 1), 0.0 if half == 1 else NEG, np.float32)
        cs = np.zeros((8, 1), np.float32)
        if half == 1:
            cs[c - 1, 0] = 1.0
        m["carrysel"] = cs
        m["wfc_sl"] = np.ascontiguousarray(wfc[c * 512:(c + 1) * 512])
        m["wproj_sl"] = np.ascontiguousarray(wproj[c * 128:(c + 1) * 128])
        per.append(m)
    return sh, per



def build_program():
    nc = bass.Bass("TRN2", target_bir_lowering=False)
    dr = {}

    def din(name, shape):
        dr[name] = nc.dram_tensor(name, list(shape), F32, kind="ExternalInput")

    din("x_own", (D, TC)); din("x_par", (D, TC))
    din("cos_own", (128, TC)); din("sin_own", (128, TC))
    din("cos_par", (128, TC)); din("sin_par", (128, TC))
    din("rmask", (128, 1)); din("carrysel", (8, 1))
    for n in ("wqT", "wkT", "wqswT", "wkswT", "wvT", "woT", "WT"):
        din(n, (D, D))
    din("bcol", (D, 1)); din("Vr", (D, 32)); din("CT", (32, D))
    din("bdf", (8, 128, 128)); din("bdi", (8, 128, 128)); din("bdh", (8, 128, 128))
    din("uT", (RANK, HIDDEN)); din("vT", (D, RANK)); din("fbT", (RANK, RANK))
    din("gate16", (RANK, 1))
    din("eye", (128, 128)); din("cmask4", (4, 128, 512))
    din("onescol", (128, 1))
    din("wfc_sl", (512, D)); din("wproj_sl", (128, HIDDEN))

    out = nc.dram_tensor("y", [D, TC], F32, kind="ExternalOutput")
    dbg = {}
    if DEV_DUMP:
        dbg["xp"] = nc.dram_tensor("dbg_xp", [D, TC], F32, kind="ExternalOutput")
        dbg["xp2"] = nc.dram_tensor("dbg_xp2", [D, TC], F32, kind="ExternalOutput")
        dbg["xo"] = nc.dram_tensor("dbg_xo", [D, TC], F32, kind="ExternalOutput")
        dbg["K"] = nc.dram_tensor("dbg_K", [D, S], F32, kind="ExternalOutput")
        dbg["Q"] = nc.dram_tensor("dbg_Q", [D, TC], F32, kind="ExternalOutput")
        dbg["V"] = nc.dram_tensor("dbg_V", [32 * 128, NH * (HD + 1)], F32, kind="ExternalOutput")
        dbg["attn"] = nc.dram_tensor("dbg_attn", [D, TC], F32, kind="ExternalOutput")
        dbg["r2"] = nc.dram_tensor("dbg_r2", [D, TC], F32, kind="ExternalOutput")
        dbg["r3"] = nc.dram_tensor("dbg_r3", [D, TC], F32, kind="ExternalOutput")
        dbg["F"] = nc.dram_tensor("dbg_F", [D, TC], F32, kind="ExternalOutput")
        dbg["x3"] = nc.dram_tensor("dbg_x3", [D, TC], F32, kind="ExternalOutput")
        dbg["r4"] = nc.dram_tensor("dbg_r4", [D, TC], F32, kind="ExternalOutput")
        dbg["prep"] = nc.dram_tensor("dbg_prep", [8, PREP_N], F32, kind="ExternalOutput")
        dbg["prepb"] = nc.dram_tensor("dbg_prepb", [8, PREP_N], BF16, kind="ExternalOutput")
        dbg["x4"] = nc.dram_tensor("dbg_x4", [D, TC], F32, kind="ExternalOutput")
    prep_part = nc.dram_tensor("prep_part", [PREP_N], BF16)
    prep_all = nc.dram_tensor("prep_all", [8, PREP_N], BF16, addr_space="Shared")
    carry_d = nc.dram_tensor("carry_d", [D], F32)
    carry_all = nc.dram_tensor("carry_all", [8, D], F32, addr_space="Shared")
    ropew = nc.dram_tensor("ropew", [4, D, D], BF16)

    with ExitStack() as st0:
        b = Bld(nc, st0)
        for s in ("ph", "lds", "ldg", "lda", "tc", "sc", "vc", "cc", "sts"):
            b.new_sem(s)

        pairs = [st0.enter_context(nc.psum_tensor(f"pp{i}", [128, 1024], F32))
                 for i in range(4)]
        bank = [View(None, pairs[i // 2], (i % 2) * 512, 128, 512) for i in range(8)]
        ones_row_r = nc.alloc_sbuf_tensor("ones_row_r", [1, 128], F32R)
        drow_r = nc.alloc_sbuf_tensor("drow_r", [1, 512], F32R)
        pool = Pool(nc, (nc.sbuf_bytes_remaining - 64) // 32 * 32)

        b.dummy = pool.alloc("dummy", [1, 2], F32)
        I_bf = pool.alloc("I_bf", [128, 128], BF16)
        I_f32 = pool.alloc("I_f32", [128, 128], F32)
        ones_row = pool.alloc("ones_row", [128, 128], F32)
        smalls = pool.alloc("smalls", [128, 8], F32)
        csel_sb = pool.alloc("csel", [8, 1], F32)
        rmask_sb = View(pool, smalls.th, smalls.off + 0, 128, 1)
        ones_col = View(pool, smalls.th, smalls.off + 1, 128, 1)
        b.c_eps1e5 = View(pool, smalls.th, smalls.off + 2, 128, 1)
        b.c_eps = View(pool, smalls.th, smalls.off + 3, 128, 1)
        b.run("vector", lambda e: e.memset(b.c_eps1e5[:], 1e-5))
        b.run("vector", lambda e: e.memset(b.c_eps[:], EPS))


        b.dma("sync", rmask_sb[:], dr["rmask"][:], "lds")
        b.dma("sync", ones_col[:], dr["onescol"][:], "lds")
        b.dma("sync", csel_sb[:], dr["carrysel"][:], "lds")
        b.dma("sync", I_f32[:], dr["eye"][:], "lds")
        b.run("vector", lambda e: e.memset(ones_row[:], 1.0))
        b.wait("vector", "lds", b.val("lds"))
        b.run("vector", lambda e: e.tensor_copy(ones_row_r[:], ones_row[0:1, :]))
        b.ones_row_r = ones_row_r
        b.drow_r = drow_r
        b.dma("gpsimd", I_bf[:], dr["eye"][:], "ldg")
        b.wait("vector", "ldg", b.val("ldg"))

        _emit_precast(b, dr, ropew, pool)
        if N_BLOCKS >= 4:
            _emit_prep(b, dr, prep_part, prep_all, bank, I_f32, pool)
        I_f32.free()
        b.barrier()

        K_sb = pool.alloc("K", [128, NC * S], BF16)
        V_sb = pool.alloc("V", [128, 32 * NH * (HD + 1)], BF16)

        def dump(view, dst, rows, width):
            if not DEV_DUMP:
                return
            for cc in range(rows // 128):
                b.dma("gpsimd", dst[cc * 128:(cc + 1) * 128, :],
                      view[:, cc * width:(cc + 1) * width], "ldg")
            b.wait("gpsimd", "ldg", b.val("ldg"))
            b.barrier()

        xp = pool.alloc("xp", [128, NC * TC], BF16)
        _emit_ln(b, bank, ones_col, ones_row, pool,
                 src=("hbm", dr["x_par"]), dst=xp, tag="ln0p")
        b.barrier()
        dump(xp, dbg.get("xp2"), D, TC)
        _emit_vproj(b, dr, bank, V_sb, xp, 16, pool)
        b.barrier()
        _emit_rope_proj(b, dr, bank, K_sb, xp, S, TC, pool, ropew, 2, 3,
                        dr["cos_par"], dr["sin_par"], wring=2)
        b.barrier()
        dump(xp, dbg.get("xp"), D, TC)
        xp.free()

        xo = pool.alloc("xo", [128, NC * TC], BF16)
        _emit_ln(b, bank, ones_col, ones_row, pool,
                 src=("hbm", dr["x_own"]), dst=xo, tag="ln0o")
        b.barrier()
        _emit_vproj(b, dr, bank, V_sb, xo, 0, pool)
        b.barrier()
        _emit_rope_proj(b, dr, bank, K_sb, xo, S, 0, pool, ropew, 2, 3,
                        dr["cos_own"], dr["sin_own"], wring=2)
        b.barrier()
        Q_sb = pool.alloc("Q", [128, NC * TC], BF16)
        _emit_rope_proj(b, dr, bank, Q_sb, xo, TC, 0, pool, ropew, 0, 1,
                        dr["cos_own"], dr["sin_own"], wring=1, paired=True)
        b.barrier()
        dump(xo, dbg.get("xo"), D, TC)
        dump(K_sb, dbg.get("K"), D, S)
        dump(Q_sb, dbg.get("Q"), D, TC)
        dump(V_sb, dbg.get("V"), 32 * 128, NH * (HD + 1))
        xo.free()

        attn_sb = pool.alloc("attn", [128, NC * TC], BF16)
        _emit_attention(b, dr, bank, K_sb, V_sb, Q_sb, attn_sb, I_bf,
                        rmask_sb, ones_row, pool)
        b.barrier()
        dump(attn_sb, dbg.get("attn"), D, TC)
        pool.free_all([K_sb, V_sb, Q_sb])

        resid = pool.alloc("resid", [128, NC * TC], F32)
        _emit_oproj(b, dr, bank, attn_sb, resid, pool)
        b.barrier()
        attn_sb.free()

        if N_BLOCKS >= 2:
            _emit_ffn0(b, dr, bank, resid, ones_col, ones_row, pool)
            b.barrier()
        if N_BLOCKS >= 3:
            if DEV_DUMP:
                for cc in range(NC):
                    b.dma("sync", dbg["r2"][cc * 128:(cc + 1) * 128, :],
                          resid[:, cc * TC:(cc + 1) * TC], "sts")
                b.wait("sync", "sts", b.val("sts"))
                b.barrier()
            _emit_l1attn(b, dr, bank, pairs, resid, ones_col, ones_row, csel_sb,
                         carry_d, carry_all, pool, dbg)
            b.barrier()
            if DEV_DUMP:
                for cc in range(NC):
                    b.dma("sync", dbg["r3"][cc * 128:(cc + 1) * 128, :],
                          resid[:, cc * TC:(cc + 1) * TC], "sts")
                b.wait("sync", "sts", b.val("sts"))
                b.barrier()
        if N_BLOCKS >= 4:
            _emit_ffn1(b, dr, bank, resid, ones_col, ones_row, prep_all, pool)
            b.barrier()

        for c in range(NC):
            b.dma("sync", out[c * 128:(c + 1) * 128, :],
                  resid[:, c * TC:(c + 1) * TC], "sts")
        b.wait("sync", "sts", b.val("sts"))

        with nc.Block() as block:
            @block.sync
            def _(e):
                for f in b.ops["sync"]:
                    f(e)

            @block.tensor
            def _(e):
                for f in b.ops["tensor"]:
                    f(e)

            @block.scalar
            def _(e):
                for f in b.ops["scalar"]:
                    f(e)

            @block.vector
            def _(e):
                for f in b.ops["vector"]:
                    f(e)

            @block.gpsimd
            def _(e):
                for f in b.ops["gpsimd"]:
                    f(e)

    return nc



def _emit_precast(b, dr, ropew, pool):
    stg = pool.alloc("pc_stg", [128, 8 * D], BF16)
    names = ("wqT", "wqswT", "wkT", "wkswT")
    for i, nm in enumerate(names):
        lg = b.dma("gpsimd", stg[:].rearrange("p (c n) -> p c n", c=8),
                   dr[nm][:].rearrange("(c p) n -> p c n", p=128), "ldg")
        b.wait("sync", "ldg", lg)
        st = b.dma("sync", ropew[i].rearrange("(c p) n -> p c n", p=128),
                   stg[:].rearrange("p (c n) -> p c n", c=8), "lds")
        b.wait("gpsimd", "lds", st)
    b.wait("sync", "lds", b.val("lds"))
    stg.free()



def _emit_prep(b, dr, prep_part, prep_all, bank, I_f32, pool):
    wraw = pool.alloc("pr_raw", [128, HIDDEN], F32)
    scth = pool.alloc("pr_scth", [128, 3], F32)
    gt = pool.alloc("pr_gt", [128, HIDDEN], F32)
    lt = pool.alloc("pr_lt", [128, HIDDEN], F32)
    stg = [pool.alloc(f"pr_stg{i}", [128, 128], BF16) for i in range(2)]
    sc_ = View(pool, scth.th, scth.off + 0, 128, 1)
    th = View(pool, scth.th, scth.off + 1, 128, 1)
    thn = View(pool, scth.th, scth.off + 2, 128, 1)
    rawd = DmaRing(b, "prraw", 1)
    stgd = DmaRing(b, "prstg", 2)
    bank_ring = Ring(b, 2)

    fcv = prep_part[:].rearrange("(d n) -> d n", n=512)
    woff = D * 512
    step = [0]
    prev_tt = [0]

    def prep_tile(src_ap, width, dst_fn, nblk):
        rsem, t_ld = rawd.load("sync", wraw[:, 0:width], src_ap, 0)
        b.wait("vector", rsem, t_ld)
        b.wait("vector", "tc", prev_tt[0])
        tr_ = b.run("vector", lambda e, w=width: e.reduce_sum(
            sc_[:], wraw[:, 0:w], axis=AX, apply_absolute_value=True), inc="vc")
        b.wait("scalar", "vc", tr_)
        ts_ = b.run("scalar", lambda e, w=width: e.mul(sc_[:], sc_[:], 1.0 / w),
                    inc="sc")
        b.wait("vector", "sc", ts_)
        tm_ = b.run("vector", lambda e: e.tensor_max(sc_[:], sc_[:], b.c_eps1e5[:]),
                    inc="vc")
        b.wait("scalar", "vc", tm_)
        b.run("scalar", lambda e: e.mul(th[:], sc_[:], 0.5))
        tg_ = b.run("scalar", lambda e: e.mul(thn[:], sc_[:], -0.5), inc="sc")
        b.wait("vector", "sc", tg_)
        b.run("vector", lambda e, w=width: e.tensor_scalar(
            gt[:, 0:w], wraw[:, 0:w], th[:], None, op0=ALU.is_gt))
        b.run("vector", lambda e, w=width: e.tensor_scalar(
            lt[:, 0:w], wraw[:, 0:w], thn[:], None, op0=ALU.is_lt))
        b.run("vector", lambda e, w=width: e.tensor_sub(gt[:, 0:w], gt[:, 0:w], lt[:, 0:w]))
        tv = b.run("vector", lambda e, w=width: e.tensor_scalar(
            gt[:, 0:w], gt[:, 0:w], sc_[:], None, op0=ALU.mult), inc="vc")
        rawd.consumed(0, "vc", tv)
        b.wait("tensor", "vc", tv)
        for j in range(nblk):
            i = step[0]; step[0] += 1
            bank_ring.acquire("tensor", i)
            tt = b.run("tensor", lambda e, jj=j, bi=i % 2: e.transpose(
                bank[bi][:, 0:128], gt[:, jj * 128:(jj + 1) * 128], I_f32[:]),
                inc="tc")
            b.wait("scalar", "tc", tt)
            lu = stgd.last[i % 2]
            if lu is not None:
                b.wait("scalar", lu[0], lu[1])
            ts = b.run("scalar", lambda e, bi=i % 2, si=i % 2: e.copy(
                stg[si][:], bank[bi][:, 0:128]), inc="sc")
            bank_ring.release(i, "sc", ts)
            b.wait("sync", "sc", ts)
            t_st = b.dma("sync", dst_fn(j), stg[i % 2][:], stgd.names[i % 2])
            stgd.uses[i % 2] += 1
            stgd.last[i % 2] = (stgd.names[i % 2], t_st)
            prev_tt[0] = tt

    for t in range(4):
        def dst_fc(j, tt=t):
            return fcv[j * 128:(j + 1) * 128, tt * 128:(tt + 1) * 128]
        prep_tile(dr["wfc_sl"][t * 128:(t + 1) * 128, :], D, dst_fc, 8)

    def dst_pr(j):
        return prep_part[woff + j * 128 * 128: woff + (j + 1) * 128 * 128].rearrange(
            "(p n) -> p n", p=128)
    prep_tile(dr["wproj_sl"][:], HIDDEN, dst_pr, 32)

    b.wait("gpsimd", "lds", b.val("lds"))
    for nm in stgd.names:
        b.wait("gpsimd", nm, b.val(nm))
    b.run("gpsimd", lambda e: e.collective_compute(
        "AllGather", ALU.bypass, replica_groups=[list(range(8))],
        ins=[prep_part[:]], outs=[prep_all[:]]), inc="cc")
    pool.free_all([wraw, scth, gt, lt, stg[0], stg[1]])



def _emit_ln(b, bank, ones_col, ones_row, pool, src, dst, tag):
    kind, sh = src
    scratch = []

    def al(n, shp, dt=F32):
        v = pool.alloc(f"{tag}_{n}", shp, dt)
        scratch.append(v)
        return v

    xin = [al(f"x{i}", [128, 512]) for i in range(2)] if kind == "hbm" else None
    xdr = DmaRing(b, f"{tag}xd", 2) if kind == "hbm" else None
    xsq = al("q0", [128, 512])
    fin_m = al("m", [1, 512])
    fin_ms = al("ms", [1, 512])
    fin_v = al("v", [1, 512])
    fin_r = al("r", [1, 512])
    fin_mr = al("mr", [1, 512])
    tmp = al("t", [128, 512])
    qring = Ring(b, 1)

    SB, QB, RB, MB = bank[4], bank[5], bank[6], bank[7]

    def src_tile(c, t):
        if kind == "hbm":
            return None
        return sh[:, c * TC + t * 512: c * TC + (t + 1) * 512]

    for t in range(NT):
        for c in range(NC):
            if kind == "hbm":
                lsem, tl = xdr.load("sync", xin[c % 2][:],
                                    sh[c * 128:(c + 1) * 128, t * 512:(t + 1) * 512], c)
                b.wait("scalar", lsem, tl)
                b.wait("tensor", lsem, tl)
                src_ap = xin[c % 2][:]
            else:
                src_ap = src_tile(c, t)
            qring.acquire("scalar", c)
            ts = b.run("scalar", lambda e, a=src_ap: e.activation(
                xsq[:], a, ACT.Square), inc="sc")
            b.run("tensor", lambda e, a=src_ap, first=(c == 0): e.matmul(
                SB[0:1, :], ones_col[:], a, start=first, stop=(c == NC - 1)))
            b.wait("tensor", "sc", ts)
            tq = b.run("tensor", lambda e, first=(c == 0): e.matmul(
                QB[0:1, :], ones_col[:], xsq[:], start=first, stop=(c == NC - 1)),
                inc="tc")
            qring.release(c, "tc", tq)
            if kind == "hbm":
                xdr.consumed(c, "tc", tq)
        b.wait("vector", "tc", b.val("tc"))
        b.run("vector", lambda e: e.tensor_scalar(fin_m[:], SB[0:1, :], 1.0 / D,
                                                  None, op0=ALU.mult))
        b.run("vector", lambda e: e.tensor_mul(fin_ms[:], fin_m[:], fin_m[:]))
        tv = b.run("vector", lambda e: e.scalar_tensor_tensor(
            fin_v[:], QB[0:1, :], 1.0 / D, fin_ms[:], op0=ALU.mult,
            op1=ALU.subtract), inc="vc")
        b.wait("scalar", "vc", tv)
        b.run("scalar", lambda e: e.activation(fin_v[:], fin_v[:], ACT.Ln,
                                               bias=b.c_eps1e5[0:1, :]))
        ts2 = b.run("scalar", lambda e: e.activation(fin_r[:], fin_v[:], ACT.Exp,
                                                     scale=-0.5), inc="sc")
        b.wait("vector", "sc", ts2)
        tv2 = b.run("vector", lambda e: e.tensor_mul(fin_mr[:], fin_m[:], fin_r[:]),
                    inc="vc")
        b.wait("tensor", "vc", tv2)
        b.run("tensor", lambda e: e.matmul(RB[:, :], ones_row[0:1, :], fin_r[:],
                                           start=True, stop=True))
        tb = b.run("tensor", lambda e: e.matmul(MB[:, :], ones_row[0:1, :], fin_mr[:],
                                                start=True, stop=True), inc="tc")
        b.wait("vector", "tc", tb)
        for c in range(NC):
            if kind == "hbm":
                lsem, tl = xdr.load("sync", xin[c % 2][:],
                                    sh[c * 128:(c + 1) * 128, t * 512:(t + 1) * 512], c)
                b.wait("vector", lsem, tl)
                src_ap = xin[c % 2][:]
            else:
                src_ap = src_tile(c, t)
            b.run("vector", lambda e, a=src_ap: e.tensor_mul(tmp[:], a, RB[:, :]))
            tv3 = b.run("vector", lambda e, c2=c, t2=t: e.tensor_sub(
                dst[:, c2 * TC + t2 * 512: c2 * TC + (t2 + 1) * 512],
                tmp[:], MB[:, :]), inc="vc")
            if kind == "hbm":
                xdr.consumed(c, "vc", tv3)
        b.wait("tensor", "vc", b.val("vc"))
    pool.free_all(scratch)



def _emit_vproj(b, dr, bank, V_sb, xln, kt0, pool):
    wv = pool.alloc("wv_sb", [128, 8 * D], BF16)
    tl = b.dma("gpsimd", wv[:].rearrange("p (c n) -> p c n", c=8),
               dr["wvT"][:].rearrange("(c p) n -> p c n", p=128), "ldg")
    b.wait("tensor", "ldg", tl)
    for kt in range(kt0, kt0 + 16):
        b.run("vector", lambda e, t=kt: e.memset(
            V_sb[:, t * 1040:(t + 1) * 1040].rearrange(
                "p (h n) -> p h n", h=NH)[:, :, HD:HD + 1], 1.0))
    vdone = b.run("vector", lambda e: e.engine_nop(), inc="vc")
    b.wait("scalar", "vc", vdone)

    bring = Ring(b, 4)
    for ktl in range(16):
        kt = kt0 + ktl
        for eh in range(2):
            i = ktl * 2 + eh
            bk = bank[i % 4]
            bring.acquire("tensor", i)
            for dc in range(NC):
                b.run("tensor", lambda e, t=ktl, h=eh, d=dc, bb=bk: e.matmul(
                    bb[:, :], xln[:, d * TC + t * 128: d * TC + (t + 1) * 128],
                    wv[:, d * D + h * 512: d * D + (h + 1) * 512],
                    start=(d == 0), stop=(d == NC - 1)),
                    inc=("tc" if dc == NC - 1 else None))
            tt = b.val("tc")
            b.wait("scalar", "tc", tt)
            ts = b.run("scalar", lambda e, k=kt, h=eh, bb=bk: e.copy(
                V_sb[:, k * 1040:(k + 1) * 1040].rearrange(
                    "p (hh n) -> p hh n", hh=NH)[:, h * 8:(h + 1) * 8, 0:HD],
                bb[:].rearrange("p (hh n) -> p hh n", hh=8)), inc="sc")
            bring.release(i, "sc", ts)
    wv.free()



def _emit_rope_proj(b, dr, bank, dst_sb, xln, dst_span, coff, pool, ropew,
                    wi_a, wi_b, cosd, sind, wring, paired=None):
    if wring < 2 and not paired:
        _emit_rope_proj_single(b, dr, bank, dst_sb, xln, dst_span, coff, pool,
                               ropew, wi_a, wi_b, cosd, sind)
        return
    tdt = F32 if wring >= 2 else BF16
    teng = "sync" if wring >= 2 else "gpsimd"
    tab = pool.alloc("rp_tab", [128, 4 * 512], tdt)
    wsl = [pool.alloc(f"rp_w{i}", [128, 2 * 8 * 128], BF16) for i in range(wring)]
    t1 = pool.alloc("rp_t1", [128, 512], BF16)

    tdr = DmaRing(b, f"rt{wi_a}", 1)
    wdr = DmaRing(b, f"rw{wi_a}", wring)
    pair_ring = Ring(b, 2)
    step = [0]

    for tp in range(NT // 2):
        t0 = 2 * tp
        tdr.load(teng, tab[:, 0:512], cosd[:, t0 * 512:(t0 + 1) * 512], tp)
        ls = None
        for k, srcap in enumerate((sind[:, t0 * 512:(t0 + 1) * 512],
                                   cosd[:, (t0 + 1) * 512:(t0 + 2) * 512],
                                   sind[:, (t0 + 1) * 512:(t0 + 2) * 512])):
            ls = b.dma(teng, tab[:, (k + 1) * 512:(k + 2) * 512], srcap,
                       tdr.names[0])
        b.wait("vector", tdr.names[0], ls)
        last_tv = None
        for ec in range(NC):
            i = step[0]; step[0] += 1
            wv = wsl[i % wring]
            wdr.load("sync", wv[:, 0:1024].rearrange("p (c n) -> p c n", c=8),
                     ropew[wi_a][:, ec * 128:(ec + 1) * 128].rearrange(
                         "(c p) n -> p c n", p=128), i)
            lw = b.dma("sync", wv[:, 1024:2048].rearrange("p (c n) -> p c n", c=8),
                       ropew[wi_b][:, ec * 128:(ec + 1) * 128].rearrange(
                           "(c p) n -> p c n", p=128), wdr.names[i % wring])
            b.wait("tensor", wdr.names[i % wring], lw)
            bs = [bank[(i % 2) * 4 + j] for j in range(4)]
            pair_ring.acquire("tensor", i)
            for dc in range(NC):
                for q in range(2):
                    b.run("tensor", lambda e, d=dc, bb=bs[q], w=wv, tt=t0 + q:
                          e.matmul(
                        bb[:, :], w[:, d * 128:(d + 1) * 128],
                        xln[:, d * TC + tt * 512: d * TC + (tt + 1) * 512],
                        start=(d == 0), stop=(d == NC - 1)))
            tcb = []
            for dc in range(NC):
                for q in range(2):
                    last = (dc == NC - 1)
                    tgt = b.run("tensor", lambda e, d=dc, bb=bs[2 + q], w=wv,
                                tt=t0 + q: e.matmul(
                        bb[:, :], w[:, 1024 + d * 128: 1024 + (d + 1) * 128],
                        xln[:, d * TC + tt * 512: d * TC + (tt + 1) * 512],
                        start=(d == 0), stop=last),
                        inc=("tc" if last else None))
                    if last:
                        tcb.append(tgt)
            for q in range(2):
                b.wait("vector", "tc", tcb[q])
                b.run("vector", lambda e, bb=bs[2 + q], q2=q: e.tensor_mul(
                    t1[:], bb[:, :], tab[:, (2 * q2 + 1) * 512:(2 * q2 + 2) * 512]))
                b.run("vector", lambda e, bb=bs[q], q2=q: e.tensor_mul(
                    bb[:, :], bb[:, :], tab[:, 2 * q2 * 512:(2 * q2 + 1) * 512]))
                tv = b.run("vector", lambda e, c=ec, tt3=t0 + q, bb=bs[q],
                           co=coff: e.tensor_add(
                    dst_sb[:, c * dst_span + co + tt3 * 512:
                           c * dst_span + co + (tt3 + 1) * 512],
                    bb[:, :], t1[:]), inc="vc")
            pair_ring.release(i, "vc", tv)
            wdr.consumed(i, "vc", tv)
            last_tv = tv
        tdr.consumed(tp, "vc", last_tv)
    pool.free_all([tab, t1] + wsl)


def _emit_rope_proj_single(b, dr, bank, dst_sb, xln, dst_span, coff, pool,
                           ropew, wi_a, wi_b, cosd, sind):
    tab = pool.alloc("rp_tab0", [128, 1024], F32)
    wsl = [pool.alloc("rp_w0", [128, 2 * 8 * 128], BF16)]
    t1 = pool.alloc("rp_t1", [128, 512], F32)

    tdr = DmaRing(b, f"rt{wi_a}", 1)
    wdr = DmaRing(b, f"rw{wi_a}", 1)
    pair_ring = Ring(b, 2)
    step = [0]

    for t in range(NT):
        tdr.load("sync", tab[:, 0:512], cosd[:, t * 512:(t + 1) * 512], t)
        ls = b.dma("sync", tab[:, 512:1024], sind[:, t * 512:(t + 1) * 512],
                   tdr.names[0])
        b.wait("vector", tdr.names[0], ls)
        last_tv = None
        for ec in range(NC):
            i = step[0]; step[0] += 1
            wv = wsl[0]
            wdr.load("sync", wv[:, 0:1024].rearrange("p (c n) -> p c n", c=8),
                     ropew[wi_a][:, ec * 128:(ec + 1) * 128].rearrange(
                         "(c p) n -> p c n", p=128), i)
            lw = b.dma("sync", wv[:, 1024:2048].rearrange("p (c n) -> p c n", c=8),
                       ropew[wi_b][:, ec * 128:(ec + 1) * 128].rearrange(
                           "(c p) n -> p c n", p=128), wdr.names[0])
            b.wait("tensor", wdr.names[0], lw)
            bA = bank[(i % 2) * 2]
            bB = bank[(i % 2) * 2 + 1]
            pair_ring.acquire("tensor", i)
            for dc in range(NC):
                b.run("tensor", lambda e, d=dc, bb=bA, w=wv, tt=t: e.matmul(
                    bb[:, :], w[:, d * 128:(d + 1) * 128],
                    xln[:, d * TC + tt * 512: d * TC + (tt + 1) * 512],
                    start=(d == 0), stop=(d == NC - 1)))
            for dc in range(NC):
                b.run("tensor", lambda e, d=dc, bb=bB, w=wv, tt=t: e.matmul(
                    bb[:, :], w[:, 1024 + d * 128: 1024 + (d + 1) * 128],
                    xln[:, d * TC + tt * 512: d * TC + (tt + 1) * 512],
                    start=(d == 0), stop=(d == NC - 1)),
                    inc=("tc" if dc == NC - 1 else None))
            tt2 = b.val("tc")
            b.wait("vector", "tc", tt2)
            b.run("vector", lambda e, bb=bB: e.tensor_mul(
                t1[:], bb[:, :], tab[:, 512:1024]))
            b.run("vector", lambda e, bb=bA: e.tensor_mul(
                bb[:, :], bb[:, :], tab[:, 0:512]))
            tv = b.run("vector", lambda e, c=ec, tt3=t, bb=bA, co=coff: e.tensor_add(
                dst_sb[:, c * dst_span + co + tt3 * 512:
                       c * dst_span + co + (tt3 + 1) * 512],
                bb[:, :], t1[:]), inc="vc")
            pair_ring.release(i, "vc", tv)
            wdr.consumed(i, "vc", tv)
            last_tv = tv
        tdr.consumed(t, "vc", last_tv)
    pool.free_all([tab, t1] + wsl)



def _emit_attention(b, dr, bank, K_sb, V_sb, Q_sb, attn_sb, I_bf,
                    rmask_sb, ones_row, pool):
    cmask_bf = pool.alloc("cmask", [128, 4 * 512], BF16)
    expP = [pool.alloc(f"at_e{i}", [128, 512], BF16) for i in range(4)]
    drow = b.drow_r
    bc_sb = [pool.alloc("at_bc0", [64, 512], F32)]
    lg = b.dma("gpsimd", cmask_bf[:].rearrange("p (r n) -> p r n", r=4),
               dr["cmask4"][:].rearrange("r p n -> p r n"), "ldg")
    b.wait("tensor", "ldg", lg)

    sco = [bank[0], bank[1], bank[2], bank[3]]
    for i in range(120):
        b.run("tensor", lambda e, bb=sco[i % 4]: e.matmul(
            bb[:, :], I_bf[:], cmask_bf[:, 0:512], start=True, stop=True))

    avb = [bank[4], bank[6]]
    bcb = [bank[5], bank[7]]
    sco_ring = Ring(b, 4)
    exp_ring = Ring(b, 4)
    av_ring = Ring(b, 2)
    bc_ring = Ring(b, 2)
    bcs_ring = Ring(b, 1)

    hq = [0]
    tb_prev = [0]
    pending = [None]
    for qt in range(NT):
        for h in range(NH):
            ec, rh = h // 2, (h % 2) * 64
            i_h = hq[0]; hq[0] += 1
            av = avb[i_h % 2]
            kts = list(range(0, 4 * (qt + 1))) + list(range(16, 32))
            nk = len(kts)
            exp_t = {}

            def emit_score(j):
                kt = kts[j]
                bk = sco[j % 4]
                diag = (kt < 16 and kt >= 4 * qt)
                sco_ring.acquire("tensor", j)
                if diag:
                    rel = kt - 4 * qt
                    b.run("tensor", lambda e, bb=bk, r=rel: e.matmul(
                        bb[:, :], I_bf[:], cmask_bf[:, r * 512:(r + 1) * 512],
                        start=True, stop=False))
                tsc = b.run("tensor", lambda e, bb=bk, k=kt, c=ec, r=rh, q=qt,
                            dg=diag: e.matmul(
                    bb[:, :],
                    K_sb[r:r + 64, c * S + k * 128: c * S + (k + 1) * 128],
                    Q_sb[r:r + 64, c * TC + q * 512: c * TC + (q + 1) * 512],
                    start=(not dg), stop=True), inc="tc")
                b.wait("scalar", "tc", tsc)
                exp_ring.acquire("scalar", j)
                rect = kt >= 16
                te = b.run("scalar", lambda e, bb=bk, sl=j % 4, rc=rect: e.activation(
                    expP[sl][:], bb[:, :], ACT.Exp,
                    bias=(rmask_sb[:] if rc else 0.0), scale=0.125), inc="sc")
                exp_t[j] = te
                sco_ring.release(j, "sc", te)

            def emit_av(j):
                b.wait("tensor", "sc", exp_t[j])
                if j == 0:
                    av_ring.acquire("tensor", i_h)
                ta = b.run("tensor", lambda e, k=kts[j], hh=h, sl=j % 4,
                           aa=av: e.matmul(
                    aa[0:65, :],
                    V_sb[:, k * 1040 + hh * 65: k * 1040 + (hh + 1) * 65],
                    expP[sl][:],
                    start=(j == 0), stop=(j == nk - 1)), inc="tc")
                exp_ring.release(j, "tc", ta)
                return ta

            for j in range(4):
                emit_score(j)
            for j in range(4, nk):
                emit_av(j - 4)
                if j == 10 and pending[0] is not None:
                    pending[0]()
                    pending[0] = None
                emit_score(j)
            for j in range(nk - 4, nk - 1):
                emit_av(j)
            ta = emit_av(nk - 1)

            b.wait("vector", "tc", max(ta, tb_prev[0]))

            def _recip(e, aa):
                with b.nc.allow_low_precision(reason="float32r is fp32"):
                    return e.reciprocal(drow[0:1, :], aa[64:65, :])
            trc = b.run("vector", lambda e, aa=av, f=_recip: f(e, aa), inc="vc")

            def tail(ih=i_h, aa=av, c=ec, r=rh, q=qt, trc=trc):
                bc_ring.acquire("tensor", ih)
                b.wait("tensor", "vc", trc)
                tb = b.run("tensor", lambda e, bb=bcb[ih % 2]: e.matmul(
                    bb[0:64, :], b.ones_row_r[0:1, 0:64], drow[0:1, :],
                    start=True, stop=True), inc="tc")
                tb_prev[0] = tb
                bcs_ring.acquire("vector", ih)
                b.wait("vector", "tc", tb)
                tcp = b.run("vector", lambda e, bb=bcb[ih % 2], s2=bc_sb[0]:
                            e.tensor_copy(s2[:], bb[0:64, :]), inc="vc")
                bc_ring.release(ih, "vc", tcp)
                tn = b.run("vector", lambda e, s2=bc_sb[0]: e.tensor_mul(
                    attn_sb[r:r + 64, c * TC + q * 512: c * TC + (q + 1) * 512],
                    aa[0:64, :], s2[:]), inc="vc")
                av_ring.release(ih, "vc", tn)
                bcs_ring.release(ih, "vc", tn)

            pending[0] = tail
    pending[0]()
    pending[0] = None
    pool.free_all([cmask_bf] + expP + bc_sb)



def _emit_oproj(b, dr, bank, attn_sb, resid, pool):
    wo = pool.alloc("wo_sb", [128, 8 * D], BF16)
    xbuf = [pool.alloc(f"ox{i}", [128, 512], F32) for i in range(4)]
    tl = b.dma("gpsimd", wo[:].rearrange("p (c n) -> p c n", c=8),
               dr["woT"][:].rearrange("(c p) n -> p c n", p=128), "ldg")
    b.wait("tensor", "ldg", tl)
    bring = Ring(b, 2)
    xdr = DmaRing(b, "oprx", 4)
    for ec in range(NC):
        bs = [bank[(ec % 2) * 4 + j] for j in range(4)]
        xs = []
        for t in range(NT):
            xs.append(xdr.load("sync", xbuf[t][:],
                               dr["x_own"][ec * 128:(ec + 1) * 128,
                                           t * 512:(t + 1) * 512], ec * NT + t))
        bring.acquire("tensor", ec)
        tcb = []
        for dc in range(NC):
            for t in range(NT):
                last = (dc == NC - 1)
                tgt = b.run("tensor", lambda e, d=dc, c=ec, tt=t, bb=bs[t]:
                            e.matmul(
                    bb[:, :], wo[:, d * D + c * 128: d * D + (c + 1) * 128],
                    attn_sb[:, d * TC + tt * 512: d * TC + (tt + 1) * 512],
                    start=(d == 0), stop=(d == NC - 1)),
                    inc=("tc" if last else None))
                if last:
                    tcb.append(tgt)
        tv = None
        for t in range(NT):
            b.wait("vector", "tc", tcb[t])
            b.wait("vector", xs[t][0], xs[t][1])
            tv = b.run("vector", lambda e, c=ec, tt3=t, bb=bs[t], xb=xbuf[t]:
                       e.tensor_add(resid[:, c * TC + tt3 * 512:
                                          c * TC + (tt3 + 1) * 512],
                                    xb[:], bb[:, :]), inc="vc")
            xdr.consumed(ec * NT + t, "vc", tv)
        bring.release(ec, "vc", tv)
    pool.free_all([wo] + xbuf)



def _emit_ffn0(b, dr, bank, resid, ones_col, ones_row, pool):
    xln2 = pool.alloc("xln2", [128, NC * TC], BF16)
    _emit_ln(b, bank, ones_col, ones_row, pool,
             src=("sbuf", resid), dst=xln2, tag="ln1")
    b.barrier()

    wsb = pool.alloc("W_sb", [128, 8 * D], BF16)
    vr = pool.alloc("vr_sb", [128, 8 * 32], BF16)
    ct = pool.alloc("ct_sb", [32, D], BF16)
    bsb = pool.alloc("b_sb", [128, 8], F32)
    r1 = pool.alloc("r1_sb", [32, 4 * 512], BF16)
    b.dma("gpsimd", wsb[:].rearrange("p (c n) -> p c n", c=8),
          dr["WT"][:].rearrange("(c p) n -> p c n", p=128), "ldg")
    b.dma("gpsimd", vr[:].rearrange("p (c n) -> p c n", c=8),
          dr["Vr"][:].rearrange("(c p) n -> p c n", p=128), "ldg")
    b.dma("gpsimd", ct[:], dr["CT"][:], "ldg")
    for c in range(NC):
        b.dma("sync", bsb[:, c:c + 1], dr["bcol"][c * 128:(c + 1) * 128, :], "lds")
    b.wait("tensor", "ldg", b.val("ldg"))
    b.wait("vector", "lds", b.val("lds"))

    ts = None
    for t in range(NT):
        rb = bank[6]
        for dc in range(NC):
            b.run("tensor", lambda e, d=dc, tt=t: e.matmul(
                rb[0:32, :], vr[:, d * 32:(d + 1) * 32],
                xln2[:, d * TC + tt * 512: d * TC + (tt + 1) * 512],
                start=(d == 0), stop=(d == NC - 1)),
                inc=("tc" if dc == NC - 1 else None))
        tt1 = b.val("tc")
        b.wait("scalar", "tc", tt1)
        ts = b.run("scalar", lambda e, tt=t: e.copy(
            r1[:, tt * 512:(tt + 1) * 512], rb[0:32, :]), inc="sc")
        b.wait("tensor", "sc", ts)

    bring = Ring(b, 2)
    for ec in range(NC):
        bs = [bank[(ec % 2) * 4 + j] for j in range(4)]
        bring.acquire("tensor", ec)
        for t in range(NT):
            b.run("tensor", lambda e, c=ec, tt=t, bb=bs[t]: e.matmul(
                bb[:, :], ct[:, c * 128:(c + 1) * 128],
                r1[:, tt * 512:(tt + 1) * 512],
                start=True, stop=False))
        tcb = []
        for dc in range(NC):
            for t in range(NT):
                last = (dc == NC - 1)
                tgt = b.run("tensor", lambda e, d=dc, c=ec, tt2=t, bb=bs[t]:
                            e.matmul(
                    bb[:, :], wsb[:, d * D + c * 128: d * D + (c + 1) * 128],
                    xln2[:, d * TC + tt2 * 512: d * TC + (tt2 + 1) * 512],
                    start=False, stop=(d == NC - 1)),
                    inc=("tc" if last else None))
                if last:
                    tcb.append(tgt)
        tv = None
        for t in range(NT):
            b.wait("vector", "tc", tcb[t])
            tv = b.run("vector", lambda e, c=ec, tt4=t, bb=bs[t]:
                       e.scalar_tensor_tensor(
                resid[:, c * TC + tt4 * 512: c * TC + (tt4 + 1) * 512], bb[:, :],
                bsb[:, c:c + 1],
                resid[:, c * TC + tt4 * 512: c * TC + (tt4 + 1) * 512],
                op0=ALU.add, op1=ALU.add), inc="vc")
        bring.release(ec, "vc", tv)
    pool.free_all([xln2, wsb, vr, ct, bsb, r1])



def _emit_l1attn(b, dr, bank, pairs, resid, ones_col, ones_row, csel_sb,
                 carry_d, carry_all, pool, dbg=None):
    xln3 = pool.alloc("xln3", [128, NC * TC], BF16)
    _emit_ln(b, bank, ones_col, ones_row, pool,
             src=("sbuf", resid), dst=xln3, tag="ln2")
    b.barrier()

    bd = {n: pool.alloc(f"{n}_sb", [128, 8 * 128], BF16)
          for n in ("bdf", "bdi", "bdh")}
    F_sb = pool.alloc("F_sb", [128, NC * TC], F32)
    v_sb = pool.alloc("v_sb", [128, TC], F32)
    h_sb = pool.alloc("h_sb", [128, TC], F32)
    zero_sb = pool.alloc("zero_sb", [128, 1], F32)
    t0 = pool.alloc("g_t0", [128, 1024], F32)
    t1 = pool.alloc("g_t1", [128, 1024], F32)
    t2_ = pool.alloc("g_t2", [128, 1024], F32)
    t3 = pool.alloc("g_t3", [128, 1024], F32)
    mk = pool.alloc("g_mk", [128, 1024], U8)
    carry_sb = pool.alloc("carry_sb", [128, 8], F32)
    cg_sb = pool.alloc("cg_sb", [8, D], F32)
    hin = pool.alloc("hin_sb", [128, 8], F32)

    for n in ("bdf", "bdi", "bdh"):
        b.dma("gpsimd", bd[n][:].rearrange("p (c n2) -> p c n2", c=8),
              dr[n][:].rearrange("c p n2 -> p c n2"), "ldg")
    b.run("vector", lambda e: e.memset(zero_sb[:], 0.0))
    zero_bc = zero_sb[:].broadcast_to([128, TC])
    b.wait("tensor", "ldg", b.val("ldg"))

    gate_ring = Ring(b, 1)
    for c in range(NC):
        for t2 in range(2):
            gate_ring.acquire("tensor", 0)
            for gi, gname in enumerate(("bdf", "bdi", "bdh")):
                for sub in range(2):
                    b.run("tensor", lambda e, cc=c, tt=t2 * 2 + sub, gb=bd[gname],
                          pi_=pairs[gi], s2=sub: e.matmul(
                        pi_[:, s2 * 512:(s2 + 1) * 512],
                        gb[:, cc * 128:(cc + 1) * 128],
                        xln3[:, cc * TC + tt * 512: cc * TC + (tt + 1) * 512],
                        start=True, stop=True),
                        inc=("tc" if (gi == 2 and sub == 1) else None))
            tg = b.val("tc")
            b.wait("scalar", "tc", tg)
            b.wait("vector", "tc", tg)
            pf = pairs[0][:, 0:1024]
            pi = pairs[1][:, 0:1024]
            ph_ = pairs[2][:, 0:1024]
            sl = slice(t2 * 1024, (t2 + 1) * 1024)
            b.run("scalar", lambda e, p=pf: e.activation(t0[:], p, ACT.Exp, scale=-1.0))
            b.run("scalar", lambda e: e.activation(t0[:], t0[:], ACT.Ln, bias=1.0))
            b.run("scalar", lambda e, p=pi: e.activation(t1[:], p, ACT.Exp, scale=-1.0))
            ts1 = b.run("scalar", lambda e: e.activation(t1[:], t1[:], ACT.Ln, bias=1.0),
                        inc="sc")
            b.wait("vector", "sc", ts1)
            tv1 = b.run("vector", lambda e: e.tensor_sub(t2_[:], t0[:], t1[:]), inc="vc")
            b.wait("scalar", "vc", tv1)
            b.run("scalar", lambda e, cc=c, ss=sl: e.activation(
                F_sb[:, cc * TC + ss.start: cc * TC + ss.stop], t2_[:],
                ACT.Sigmoid, scale=-1.0))
            b.run("scalar", lambda e: e.activation(t2_[:], t2_[:], ACT.Sigmoid))
            ts3 = b.run("scalar", lambda e, p=ph_: e.activation(
                t0[:], p, ACT.Sigmoid), inc="sc")
            b.wait("vector", "sc", ts3)
            b.run("vector", lambda e, p=ph_: e.tensor_scalar(mk[:], p, 0.0, None,
                                                             op0=ALU.is_ge))
            b.run("vector", lambda e, p=ph_: e.tensor_scalar(t1[:], p, 0.5 + EPS,
                                                             None, op0=ALU.add))
            b.run("vector", lambda e: e.select(t3[:], mk[:], t1[:], t0[:]))
            tv3 = b.run("vector", lambda e, ss=sl: e.tensor_mul(
                v_sb[:, ss], t2_[:], t3[:]), inc="vc")
            gate_ring.release(0, "vc", tv3)
        tv4 = b.run("vector", lambda e: e.engine_nop(), inc="vc")
        b.wait("vector", "vc", tv4)
        b.run("vector", lambda e, cc=c: e.tensor_tensor_scan(
            h_sb[:], F_sb[:, cc * TC:(cc + 1) * TC], v_sb[:], 0.0,
            op0=ALU.mult, op1=ALU.add))
        b.run("vector", lambda e, cc=c: e.tensor_add(
            resid[:, cc * TC:(cc + 1) * TC], resid[:, cc * TC:(cc + 1) * TC], h_sb[:]))
        b.run("vector", lambda e, cc=c: e.tensor_copy(
            carry_sb[:, cc:cc + 1], h_sb[:, TC - 1:TC]), inc="vc")
    b.wait("sync", "vc", b.val("vc"))
    if dbg is not None and "F" in dbg:
        import kernel as _k
        if _k.DEV_DUMP:
            for cc in range(NC):
                b.dma("sync", dbg["F"][cc * 128:(cc + 1) * 128, :],
                      F_sb[:, cc * TC:(cc + 1) * TC], "sts")
                b.dma("gpsimd", dbg["x3"][cc * 128:(cc + 1) * 128, :],
                      xln3[:, cc * TC:(cc + 1) * TC], "ldg")
            b.wait("sync", "sts", b.val("sts"))
            b.wait("gpsimd", "ldg", b.val("ldg"))
    for c in range(NC):
        b.dma("sync", carry_d[c * 128:(c + 1) * 128], carry_sb[:, c:c + 1], "lds")
    b.wait("gpsimd", "lds", b.val("lds"))
    tcc = b.run("gpsimd", lambda e: e.collective_compute(
        "AllGather", ALU.bypass, replica_groups=[list(range(8))],
        ins=[carry_d[:]], outs=[carry_all[:]]), inc="cc")
    b.wait("sync", "cc", tcc)
    tcg = b.dma("sync", cg_sb[:], carry_all[:], "lds")
    b.wait("tensor", "lds", tcg)
    for c in range(NC):
        b.run("tensor", lambda e, cc=c: e.matmul(
            bank[5][:, cc:cc + 1], cg_sb[:, cc * 128:(cc + 1) * 128], csel_sb[:],
            start=True, stop=True), inc=("tc" if c == NC - 1 else None))
    tm = b.val("tc")
    b.wait("vector", "tc", tm)
    tvh = b.run("vector", lambda e: e.tensor_copy(hin[:], bank[5][:, 0:8]), inc="vc")
    b.wait("vector", "vc", tvh)
    for c in range(NC):
        b.run("vector", lambda e, cc=c: e.tensor_tensor_scan(
            h_sb[:], F_sb[:, cc * TC:(cc + 1) * TC], zero_bc, 1.0,
            op0=ALU.mult, op1=ALU.add))
        b.run("vector", lambda e, cc=c: e.scalar_tensor_tensor(
            resid[:, cc * TC:(cc + 1) * TC], h_sb[:], hin[:, cc:cc + 1],
            resid[:, cc * TC:(cc + 1) * TC], op0=ALU.mult, op1=ALU.add))
    pool.free_all([xln3, bd["bdf"], bd["bdi"], bd["bdh"], F_sb, v_sb, h_sb,
                   zero_sb, t0, t1, t2_, t3, mk, carry_sb, cg_sb, hin])



def _emit_ffn1(b, dr, bank, resid, ones_col, ones_row, prep_all, pool, dbg=None):
    xln4 = pool.alloc("xln4", [128, NC * TC], BF16)
    _emit_ln(b, bank, ones_col, ones_row, pool,
             src=("sbuf", resid), dst=xln4, tag="ln3")
    b.barrier()

    vt = pool.alloc("vt_sb", [128, 8 * RANK], BF16)
    fb = pool.alloc("fb_sb", [RANK, RANK], BF16)
    ut = pool.alloc("ut_sb", [RANK, HIDDEN], BF16)
    g16 = pool.alloc("g16_sb", [RANK, 1], F32)
    s1 = pool.alloc("s1_sb", [RANK, 512], BF16)
    s2 = pool.alloc("s2_sb", [RANK, 512], BF16)
    s1x = pool.alloc("s1x_sb", [RANK, 512], BF16)
    rsq = pool.alloc("rsq_sb", [128, 32 * 1024], BF16)
    relu_t = pool.alloc("relu_t", [128, 512], F32)
    wcol = [pool.alloc(f"wcol{i}", [128, 8 * 128], BF16) for i in range(2)]
    wpb = [pool.alloc(f"wpb{i}", [128, 32 * 128], BF16) for i in range(2)]

    b.dma("gpsimd", vt[:].rearrange("p (c n) -> p c n", c=8),
          dr["vT"][:].rearrange("(c p) n -> p c n", p=128), "ldg")
    b.dma("gpsimd", fb[:], dr["fbT"][:], "ldg")
    b.dma("gpsimd", ut[:], dr["uT"][:], "ldg")
    tg = b.dma("sync", g16[:], dr["gate16"][:], "lds")
    b.wait("tensor", "ldg", b.val("ldg"))
    b.wait("vector", "lds", tg)

    import kernel as _k
    if dbg is not None and _k.DEV_DUMP:
        vscr = pool.alloc("vscr", [128, TC], F32)
        for cc in range(NC):
            b.dma("gpsimd", dbg["x4"][cc * 128:(cc + 1) * 128, :],
                  xln4[:, cc * TC:(cc + 1) * TC], "ldg")
            tvv = b.run("vector", lambda e, c2=cc: e.tensor_copy(
                vscr[:], xln4[:, c2 * TC:(c2 + 1) * TC]), inc="vc")
            b.wait("sync", "vc", tvv)
            b.dma("sync", dbg["x3"][cc * 128:(cc + 1) * 128, :], vscr[:], "sts")
            b.wait("vector", "sts", b.val("sts"))
        b.wait("gpsimd", "ldg", b.val("ldg"))
        b.wait("sync", "sts", b.val("sts"))
        vscr.free()
    wcd = DmaRing(b, "f1wc", 2)
    wpd = DmaRing(b, "f1wp", 2)
    fc_ring = Ring(b, 2)
    pj_ring = Ring(b, 2)

    s2ab = [s1, s2]
    for qp in range(2):
        q0 = 2 * qp
        rb = bank[0]
        fc_ring.acquire("tensor", 0)
        fc_ring.acquire("tensor", 1)
        for qq in range(2):
            for dc in range(NC):
                b.run("tensor", lambda e, d=dc, q=q0 + qq: e.matmul(
                    rb[0:RANK, :], vt[:, d * RANK:(d + 1) * RANK],
                    xln4[:, d * TC + q * 512: d * TC + (q + 1) * 512],
                    start=(d == 0), stop=(d == NC - 1)),
                    inc=("tc" if dc == NC - 1 else None))
            tt = b.val("tc")
            b.wait("scalar", "tc", tt)
            ts = b.run("scalar", lambda e: e.copy(s1x[:], rb[0:RANK, :]), inc="sc")
            b.wait("tensor", "sc", ts)
            b.run("tensor", lambda e: e.matmul(rb[0:RANK, :], fb[:], s1x[:],
                                               start=True, stop=True), inc="tc")
            tt2 = b.val("tc")
            b.wait("vector", "tc", tt2)
            tv = b.run("vector", lambda e, s_=s2ab[qq]: e.tensor_scalar(
                s_[:], rb[0:RANK, :], g16[:], None, op0=ALU.mult), inc="vc")
            b.wait("tensor", "vc", tv)

        for hc in range(32):
            i = qp * 32 + hc
            r_ = (hc * 128) // 512
            lc = (hc * 128) % 512
            wsem, lw = wcd.load(
                "sync", wcol[i % 2][:].rearrange("p (c n) -> p c n", c=8),
                prep_all[r_, 0:D * 512].rearrange("(d n) -> d n", n=512)[
                    :, lc:lc + 128].rearrange("(c p) n -> p c n", p=128), i)
            fc_ring.acquire("tensor", hc)
            bkA = bank[(hc % 2) * 2]
            bkB = bank[(hc % 2) * 2 + 1]
            b.run("tensor", lambda e, h2=hc, bb=bkA: e.matmul(
                bb[:, :], ut[:, h2 * 128:(h2 + 1) * 128], s2ab[0][:],
                start=True, stop=False))
            b.run("tensor", lambda e, h2=hc, bb=bkB: e.matmul(
                bb[:, :], ut[:, h2 * 128:(h2 + 1) * 128], s2ab[1][:],
                start=True, stop=False))
            b.wait("tensor", wsem, lw)
            for dc in range(NC):
                for qq in range(2):
                    b.run("tensor",
                          lambda e, d=dc, q=q0 + qq, wi=i % 2,
                          bb=(bkA if qq == 0 else bkB): e.matmul(
                              bb[:, :], wcol[wi][:, d * 128:(d + 1) * 128],
                              xln4[:, d * TC + q * 512: d * TC + (q + 1) * 512],
                              start=False, stop=(d == NC - 1)),
                          inc=("tc" if dc == NC - 1 and qq == 1 else None))
            tt3 = b.val("tc")
            b.wait("scalar", "tc", tt3)
            for qq, bb in ((0, bkA), (1, bkB)):
                b.run("scalar", lambda e, bb=bb: e.activation(
                    relu_t[:], bb[:, :], ACT.Relu))
                tv2 = b.run("scalar", lambda e, h2=hc, q2=qq: e.activation(
                    rsq[:, h2 * 1024 + q2 * 512: h2 * 1024 + (q2 + 1) * 512],
                    relu_t[:], ACT.Square), inc="sc")
            fc_ring.release(hc, "sc", tv2)
            wcd.consumed(i, "sc", tv2)
        rsq_done = b.val("sc")

        for dco in range(NC):
            i = qp * NC + dco
            woff = D * 512
            psem, lw = wpd.load(
                "sync", wpb[i % 2][:].rearrange("p (c n) -> p c n", c=32),
                prep_all[dco, woff:woff + HIDDEN * 128].rearrange(
                    "(c p n) -> p c n", p=128, n=128), i)
            b.wait("tensor", psem, lw)
            b.wait("tensor", "sc", rsq_done)
            pj_ring.acquire("tensor", dco)
            bkA = bank[4 + (dco % 2) * 2]
            bkB = bank[5 + (dco % 2) * 2]
            for hc in range(32):
                for qq in range(2):
                    b.run("tensor",
                          lambda e, h2=hc, wi=i % 2, q2=qq,
                          bb=(bkA if qq == 0 else bkB): e.matmul(
                              bb[:, :], wpb[wi][:, h2 * 128:(h2 + 1) * 128],
                              rsq[:, h2 * 1024 + q2 * 512:
                                  h2 * 1024 + (q2 + 1) * 512],
                              start=(h2 == 0), stop=(h2 == 31)),
                          inc=("tc" if hc == 31 and qq == 1 else None))
            tt4 = b.val("tc")
            b.wait("vector", "tc", tt4)
            for qq, bb in ((0, bkA), (1, bkB)):
                tv3 = b.run("vector", lambda e, d=dco, q=q0 + qq, bb=bb:
                            e.tensor_add(
                    resid[:, d * TC + q * 512: d * TC + (q + 1) * 512],
                    resid[:, d * TC + q * 512: d * TC + (q + 1) * 512],
                    bb[:, :]), inc="vc")
            pj_ring.release(dco, "vc", tv3)
            wpd.consumed(i, "vc", tv3)
    pool.free_all([xln4, vt, fb, ut, g16, s1x, s1, s2, rsq, relu_t]
                  + wcol + wpb)



def kernel(**inputs):
    sh, per = stage_inputs(inputs)
    nc = build_program()
    in_maps = [{**sh, **per[c]} for c in range(8)]
    res = run_bass_kernel_spmd(nc, in_maps, core_ids=list(range(8)))
    out = np.empty((4, S, D), np.float32)
    for c in range(8):
        seq, half = c // 2, c % 2
        out[seq, half * TC:(half + 1) * TC, :] = res.results[c]["y"].T
    return out


# version 20
# speedup vs baseline: 1.0250x; 1.0030x over previous
import numpy as np
from contextlib import ExitStack

import concourse.bass as bass
import concourse.mybir as mybir
from concourse.bass_utils import run_bass_kernel_spmd

F32 = mybir.dt.float32
BF16 = mybir.dt.bfloat16
U8 = mybir.dt.uint8
F32R = mybir.dt.float32r
AX = mybir.AxisListType.X
ALU = mybir.AluOpType
ACT = mybir.ActivationFunctionType

D = 1024
TC = 2048
S = 4096
NH = 16
HD = 64
NC = 8
NT = TC // 512
EPS = 1e-8
NEG = -1.0e9
HIDDEN = 4096
RANK = 16
PREP_N = D * 512 + HIDDEN * 128

N_BLOCKS = 4
DEV_DUMP = False

ENGINES = ("sync", "tensor", "scalar", "vector", "gpsimd")


class Bld:
    def __init__(self, nc, stack):
        self.nc = nc
        self.stack = stack
        self.ops = {e: [] for e in ENGINES}
        self.sem = {}
        self.cnt = {}
        self.nbar = 0

    def new_sem(self, name):
        self.sem[name] = self.stack.enter_context(self.nc.semaphore(name))
        self.cnt[name] = 0

    def add(self, eng, fn):
        self.ops[eng].append(fn)

    def bump(self, name, k=1):
        self.cnt[name] += k
        return self.cnt[name]

    def val(self, name):
        return self.cnt[name]

    def dma(self, eng, out_ap, in_ap, sem):
        tgt = self.bump(sem, 16)
        h = self.sem[sem]
        self.add(eng, lambda e, o=out_ap, i=in_ap: e.dma_start(out=o, in_=i).then_inc(h, 16))
        return tgt

    def wait(self, eng, sem, tgt):
        if tgt <= 0:
            return
        h = self.sem[sem]
        self.add(eng, lambda e, t=tgt: e.wait_ge(h, t))

    def run(self, eng, fn, inc=None, k=1):
        if inc is None:
            self.add(eng, fn)
            return None
        tgt = self.bump(inc, k)
        h = self.sem[inc]
        self.add(eng, lambda e, f=fn, kk=k: f(e).then_inc(h, kk))
        return tgt

    def barrier(self):
        self.nbar += 1
        tgt = self.bump("ph", 5)
        h = self.sem["ph"]
        for eng in ENGINES:
            if eng in ("vector", "gpsimd"):
                self.add(eng, lambda e: e.engine_nop().then_inc(h, 1))
            elif eng == "scalar":
                d = self.dummy
                self.add(eng, lambda e: e.copy(d[0:1, 0:1], d[0:1, 1:2]).then_inc(h, 1))
            elif eng == "tensor":
                self.add(eng, lambda e: e.nop().then_inc(h, 1))
            else:
                self.add(eng, lambda e: e.sem_inc(h, 1))
            self.add(eng, lambda e, t=tgt: e.wait_ge(h, t))


class Ring:
    def __init__(self, bld, n):
        self.b = bld
        self.n = n
        self.last = [None] * n

    def acquire(self, eng, i):
        lu = self.last[i % self.n]
        if lu is not None:
            self.b.wait(eng, lu[0], lu[1])

    def release(self, i, sem, tgt):
        self.last[i % self.n] = (sem, tgt)


class DmaRing:

    _uid = [0]

    def __init__(self, bld, name, depth):
        self.b = bld
        self.depth = depth
        self.names = []
        DmaRing._uid[0] += 1
        for i in range(depth):
            nm = f"{name}{DmaRing._uid[0]}_{i}"
            bld.new_sem(nm)
            self.names.append(nm)
        self.last = [None] * depth
        self.uses = [0] * depth

    def load(self, eng, out_ap, in_ap, i):
        s = i % self.depth
        lu = self.last[s]
        if lu is not None:
            self.b.wait(eng, lu[0], lu[1])
        tgt = self.b.dma(eng, out_ap, in_ap, self.names[s])
        return (self.names[s], tgt)

    def consumed(self, i, sem, tgt):
        self.last[i % self.depth] = (sem, tgt)


class View:

    def __init__(self, pool, th, off, p, w):
        self.pool = pool
        self.th = th
        self.off = off
        self.p = p
        self.w = w

    def __getitem__(self, idx):
        if idx == slice(None):
            return self.th[0:self.p, self.off:self.off + self.w]
        if isinstance(idx, tuple):
            ps, cs = idx
        else:
            ps, cs = idx, slice(None)
        p0 = 0 if ps.start is None else ps.start
        p1 = self.p if ps.stop is None else ps.stop
        c0 = 0 if cs.start is None else cs.start
        c1 = self.w if cs.stop is None else cs.stop
        return self.th[p0:p1, self.off + c0:self.off + c1]

    def free(self):
        self.pool.free(self)


class Pool:

    def __init__(self, nc, nbytes):
        self.arena = nc.alloc_sbuf_tensor("arena", [128, nbytes], U8)
        self.views = {F32: self.arena.bitcast(F32), BF16: self.arena.bitcast(BF16),
                      U8: self.arena, F32R: self.arena.bitcast(F32R)}
        self.free_list = [(0, nbytes)]
        self.nbytes = nbytes
        self.hwm = 0

    verbose = False

    def alloc(self, name, shape, dt=F32):
        p, w = shape
        esz = {F32: 4, F32R: 4, BF16: 2, U8: 1}[dt]
        need = w * esz
        align = 32
        for i, (st, sz) in enumerate(self.free_list):
            a = (st + align - 1) // align * align
            if a + need <= st + sz:
                pre = a - st
                post = (st + sz) - (a + need)
                repl = []
                if pre:
                    repl.append((st, pre))
                if post:
                    repl.append((a + need, post))
                self.free_list[i:i + 1] = repl
                self.hwm = max(self.hwm, a + need)
                if Pool.verbose:
                    print(f"POOL {name}: [{a}, {a+need})")
                v = View(self, self.views[dt], a // esz, p, w)
                v._range = (a, need)
                return v
        raise RuntimeError(f"SBUF pool exhausted allocating {name} "
                           f"({need}B): free={self.free_list}")

    def free(self, v):
        st, sz = v._range
        self.free_list.append((st, sz))
        self.free_list.sort()
        merged = []
        for rng in self.free_list:
            if merged and merged[-1][0] + merged[-1][1] == rng[0]:
                merged[-1] = (merged[-1][0], merged[-1][1] + rng[1])
            else:
                merged.append(list(rng) if isinstance(rng, tuple) else rng)
        self.free_list = [tuple(m) for m in merged]

    def free_all(self, vs):
        for v in vs:
            self.free(v)



def _rope_tables():
    j = np.arange(0, HD, 2, dtype=np.float64)
    inv = 1.0 / (10000.0 ** (j / HD))
    pos = np.arange(S, dtype=np.float64)
    fr = pos[:, None] * inv[None, :]
    cos = np.cos(fr).T
    sin = np.sin(fr).T
    cosf = np.empty((HD, S), np.float64)
    sinS = np.empty((HD, S), np.float64)
    cosf[0::2] = cos
    cosf[1::2] = cos
    sinS[0::2] = -sin
    sinS[1::2] = sin
    return (np.ascontiguousarray(np.tile(cosf, (2, 1)).astype(np.float32)),
            np.ascontiguousarray(np.tile(sinS, (2, 1)).astype(np.float32)))


def _swap_pairs(n):
    idx = np.arange(n)
    idx[0::2] += 1
    idx[1::2] -= 1
    return idx


def _blockdiag(W):
    out = np.zeros((8, 128, 128), np.float32)
    for g in range(128):
        c, loc = g // 16, (g % 16) * 8
        out[c, loc:loc + 8, loc:loc + 8] = W[g].T
    return out


def _cmask4():
    cm = np.zeros((4, 128, 512), np.float32)
    i = np.arange(128)[:, None]
    jj = np.arange(512)[None, :]
    for r in range(4):
        cm[r] = np.where(i + r * 128 <= jj, 0.0, NEG)
    return cm


def stage_inputs(inp):
    x = np.asarray(inp["L0_attn_x"], np.float32)
    cos2, sin2 = _rope_tables()
    sw = _swap_pairs(D)

    sh = {}
    sh["wqT"] = np.ascontiguousarray(np.asarray(inp["L0_attn_wq"], np.float32).T)
    sh["wkT"] = np.ascontiguousarray(np.asarray(inp["L0_attn_wk"], np.float32).T)
    sh["wqswT"] = np.ascontiguousarray(sh["wqT"][:, sw])
    sh["wkswT"] = np.ascontiguousarray(sh["wkT"][:, sw])
    sh["wvT"] = np.ascontiguousarray(np.asarray(inp["L0_attn_wv"], np.float32).T)
    sh["woT"] = np.ascontiguousarray(np.asarray(inp["L0_attn_wo"], np.float32).T)
    sh["WT"] = np.ascontiguousarray(np.asarray(inp["L0_ffn_W"], np.float32).T)
    sh["bcol"] = np.ascontiguousarray(np.asarray(inp["L0_ffn_b"], np.float32).reshape(D, 1))
    sh["Vr"] = np.ascontiguousarray(np.asarray(inp["L0_ffn_V_r"], np.float32))
    sh["CT"] = np.ascontiguousarray(np.asarray(inp["L0_ffn_C"], np.float32).T)
    sh["bdf"] = _blockdiag(np.asarray(inp["L1_attn_Wf"], np.float32))
    sh["bdi"] = _blockdiag(np.asarray(inp["L1_attn_Wi"], np.float32))
    sh["bdh"] = _blockdiag(np.asarray(inp["L1_attn_Wh"], np.float32))
    sh["uT"] = np.ascontiguousarray(np.asarray(inp["L1_ffn_u"], np.float32).T)
    sh["vT"] = np.ascontiguousarray(np.asarray(inp["L1_ffn_v"], np.float32).T)
    sh["fbT"] = np.ascontiguousarray(np.asarray(inp["L1_ffn_fast_b"], np.float32).T)
    sh["gate16"] = np.full((16, 1), np.float32(np.asarray(inp["L1_ffn_gate"])), np.float32)
    sh["eye"] = np.eye(128, dtype=np.float32)
    sh["cmask4"] = _cmask4()
    sh["onescol"] = np.ones((128, 1), np.float32)

    wfc = np.asarray(inp["L1_ffn_w_fc"], np.float32)
    wproj = np.asarray(inp["L1_ffn_w_proj"], np.float32)

    per = []
    for c in range(8):
        seq, half = c // 2, c % 2
        own = slice(half * TC, half * TC + TC)
        par = slice((1 - half) * TC, (1 - half) * TC + TC)
        m = {}
        m["x_own"] = np.ascontiguousarray(x[seq, own].T)
        m["x_par"] = np.ascontiguousarray(x[seq, par].T)
        m["cos_own"] = np.ascontiguousarray(cos2[:, own])
        m["sin_own"] = np.ascontiguousarray(sin2[:, own])
        m["cos_par"] = np.ascontiguousarray(cos2[:, par])
        m["sin_par"] = np.ascontiguousarray(sin2[:, par])
        m["rmask"] = np.full((128, 1), 0.0 if half == 1 else NEG, np.float32)
        cs = np.zeros((8, 1), np.float32)
        if half == 1:
            cs[c - 1, 0] = 1.0
        m["carrysel"] = cs
        m["wfc_sl"] = np.ascontiguousarray(wfc[c * 512:(c + 1) * 512])
        m["wproj_sl"] = np.ascontiguousarray(wproj[c * 128:(c + 1) * 128])
        per.append(m)
    return sh, per



def build_program():
    nc = bass.Bass("TRN2", target_bir_lowering=False)
    dr = {}

    def din(name, shape):
        dr[name] = nc.dram_tensor(name, list(shape), F32, kind="ExternalInput")

    din("x_own", (D, TC)); din("x_par", (D, TC))
    din("cos_own", (128, TC)); din("sin_own", (128, TC))
    din("cos_par", (128, TC)); din("sin_par", (128, TC))
    din("rmask", (128, 1)); din("carrysel", (8, 1))
    for n in ("wqT", "wkT", "wqswT", "wkswT", "wvT", "woT", "WT"):
        din(n, (D, D))
    din("bcol", (D, 1)); din("Vr", (D, 32)); din("CT", (32, D))
    din("bdf", (8, 128, 128)); din("bdi", (8, 128, 128)); din("bdh", (8, 128, 128))
    din("uT", (RANK, HIDDEN)); din("vT", (D, RANK)); din("fbT", (RANK, RANK))
    din("gate16", (RANK, 1))
    din("eye", (128, 128)); din("cmask4", (4, 128, 512))
    din("onescol", (128, 1))
    din("wfc_sl", (512, D)); din("wproj_sl", (128, HIDDEN))

    out = nc.dram_tensor("y", [D, TC], F32, kind="ExternalOutput")
    dbg = {}
    if DEV_DUMP:
        dbg["xp"] = nc.dram_tensor("dbg_xp", [D, TC], F32, kind="ExternalOutput")
        dbg["xp2"] = nc.dram_tensor("dbg_xp2", [D, TC], F32, kind="ExternalOutput")
        dbg["xo"] = nc.dram_tensor("dbg_xo", [D, TC], F32, kind="ExternalOutput")
        dbg["K"] = nc.dram_tensor("dbg_K", [D, S], F32, kind="ExternalOutput")
        dbg["Q"] = nc.dram_tensor("dbg_Q", [D, TC], F32, kind="ExternalOutput")
        dbg["V"] = nc.dram_tensor("dbg_V", [32 * 128, NH * (HD + 1)], F32, kind="ExternalOutput")
        dbg["attn"] = nc.dram_tensor("dbg_attn", [D, TC], F32, kind="ExternalOutput")
        dbg["r2"] = nc.dram_tensor("dbg_r2", [D, TC], F32, kind="ExternalOutput")
        dbg["r3"] = nc.dram_tensor("dbg_r3", [D, TC], F32, kind="ExternalOutput")
        dbg["F"] = nc.dram_tensor("dbg_F", [D, TC], F32, kind="ExternalOutput")
        dbg["x3"] = nc.dram_tensor("dbg_x3", [D, TC], F32, kind="ExternalOutput")
        dbg["r4"] = nc.dram_tensor("dbg_r4", [D, TC], F32, kind="ExternalOutput")
        dbg["prep"] = nc.dram_tensor("dbg_prep", [8, PREP_N], F32, kind="ExternalOutput")
        dbg["prepb"] = nc.dram_tensor("dbg_prepb", [8, PREP_N], BF16, kind="ExternalOutput")
        dbg["x4"] = nc.dram_tensor("dbg_x4", [D, TC], F32, kind="ExternalOutput")
    prep_part = nc.dram_tensor("prep_part", [PREP_N], BF16)
    prep_all = nc.dram_tensor("prep_all", [8, PREP_N], BF16, addr_space="Shared")
    carry_d = nc.dram_tensor("carry_d", [D], F32)
    carry_all = nc.dram_tensor("carry_all", [8, D], F32, addr_space="Shared")
    ropew = nc.dram_tensor("ropew", [4, D, D], BF16)

    with ExitStack() as st0:
        b = Bld(nc, st0)
        for s in ("ph", "lds", "ldg", "lda", "tc", "sc", "vc", "cc", "sts"):
            b.new_sem(s)

        pairs = [st0.enter_context(nc.psum_tensor(f"pp{i}", [128, 1024], F32))
                 for i in range(4)]
        bank = [View(None, pairs[i // 2], (i % 2) * 512, 128, 512) for i in range(8)]
        ones_row_r = nc.alloc_sbuf_tensor("ones_row_r", [1, 128], F32R)
        drow_r = nc.alloc_sbuf_tensor("drow_r", [1, 512], F32R)
        pool = Pool(nc, (nc.sbuf_bytes_remaining - 64) // 32 * 32)

        b.dummy = pool.alloc("dummy", [1, 2], F32)
        I_bf = pool.alloc("I_bf", [128, 128], BF16)
        I_f32 = pool.alloc("I_f32", [128, 128], F32)
        ones_row = pool.alloc("ones_row", [128, 128], F32)
        smalls = pool.alloc("smalls", [128, 8], F32)
        csel_sb = pool.alloc("csel", [8, 1], F32)
        rmask_sb = View(pool, smalls.th, smalls.off + 0, 128, 1)
        ones_col = View(pool, smalls.th, smalls.off + 1, 128, 1)
        b.c_eps1e5 = View(pool, smalls.th, smalls.off + 2, 128, 1)
        b.c_eps = View(pool, smalls.th, smalls.off + 3, 128, 1)
        b.run("vector", lambda e: e.memset(b.c_eps1e5[:], 1e-5))
        b.run("vector", lambda e: e.memset(b.c_eps[:], EPS))


        b.dma("sync", rmask_sb[:], dr["rmask"][:], "lds")
        b.dma("sync", ones_col[:], dr["onescol"][:], "lds")
        b.dma("sync", csel_sb[:], dr["carrysel"][:], "lds")
        b.dma("sync", I_f32[:], dr["eye"][:], "lds")
        tms = b.run("vector", lambda e: e.memset(ones_row[:], 1.0), inc="vc")
        b.wait("vector", "lds", b.val("lds"))
        b.wait("vector", "vc", tms)
        b.run("vector", lambda e: e.tensor_copy(ones_row_r[:], ones_row[0:1, :]))
        b.ones_row_r = ones_row_r
        b.drow_r = drow_r
        b.dma("gpsimd", I_bf[:], dr["eye"][:], "ldg")
        b.wait("vector", "ldg", b.val("ldg"))

        _emit_precast(b, dr, ropew, pool)
        if N_BLOCKS >= 4:
            _emit_prep(b, dr, prep_part, prep_all, bank, I_f32, pool)
        I_f32.free()
        b.barrier()

        K_sb = pool.alloc("K", [128, NC * S], BF16)
        V_sb = pool.alloc("V", [128, 32 * NH * (HD + 1)], BF16)

        def dump(view, dst, rows, width):
            if not DEV_DUMP:
                return
            for cc in range(rows // 128):
                b.dma("gpsimd", dst[cc * 128:(cc + 1) * 128, :],
                      view[:, cc * width:(cc + 1) * width], "ldg")
            b.wait("gpsimd", "ldg", b.val("ldg"))
            b.barrier()

        xp = pool.alloc("xp", [128, NC * TC], BF16)
        _emit_ln(b, bank, ones_col, ones_row, pool,
                 src=("hbm", dr["x_par"]), dst=xp, tag="ln0p")
        b.barrier()
        dump(xp, dbg.get("xp2"), D, TC)
        _emit_vproj(b, dr, bank, V_sb, xp, 16, pool)
        b.barrier()
        _emit_rope_proj(b, dr, bank, K_sb, xp, S, TC, pool, ropew, 2, 3,
                        dr["cos_par"], dr["sin_par"], wring=2)
        b.barrier()
        dump(xp, dbg.get("xp"), D, TC)
        xp.free()

        xo = pool.alloc("xo", [128, NC * TC], BF16)
        _emit_ln(b, bank, ones_col, ones_row, pool,
                 src=("hbm", dr["x_own"]), dst=xo, tag="ln0o")
        b.barrier()
        _emit_vproj(b, dr, bank, V_sb, xo, 0, pool)
        b.barrier()
        _emit_rope_proj(b, dr, bank, K_sb, xo, S, 0, pool, ropew, 2, 3,
                        dr["cos_own"], dr["sin_own"], wring=2)
        b.barrier()
        Q_sb = pool.alloc("Q", [128, NC * TC], BF16)
        _emit_rope_proj(b, dr, bank, Q_sb, xo, TC, 0, pool, ropew, 0, 1,
                        dr["cos_own"], dr["sin_own"], wring=1, paired=True)
        b.barrier()
        dump(xo, dbg.get("xo"), D, TC)
        dump(K_sb, dbg.get("K"), D, S)
        dump(Q_sb, dbg.get("Q"), D, TC)
        dump(V_sb, dbg.get("V"), 32 * 128, NH * (HD + 1))
        xo.free()

        attn_sb = pool.alloc("attn", [128, NC * TC], BF16)
        _emit_attention(b, dr, bank, K_sb, V_sb, Q_sb, attn_sb, I_bf,
                        rmask_sb, ones_row, pool)
        b.barrier()
        dump(attn_sb, dbg.get("attn"), D, TC)
        pool.free_all([K_sb, V_sb, Q_sb])

        resid = pool.alloc("resid", [128, NC * TC], F32)
        _emit_oproj(b, dr, bank, attn_sb, resid, pool)
        b.barrier()
        attn_sb.free()

        if N_BLOCKS >= 2:
            _emit_ffn0(b, dr, bank, resid, ones_col, ones_row, pool)
            b.barrier()
        if N_BLOCKS >= 3:
            if DEV_DUMP:
                for cc in range(NC):
                    b.dma("sync", dbg["r2"][cc * 128:(cc + 1) * 128, :],
                          resid[:, cc * TC:(cc + 1) * TC], "sts")
                b.wait("sync", "sts", b.val("sts"))
                b.barrier()
            _emit_l1attn(b, dr, bank, pairs, resid, ones_col, ones_row, csel_sb,
                         carry_d, carry_all, pool, dbg)
            b.barrier()
            if DEV_DUMP:
                for cc in range(NC):
                    b.dma("sync", dbg["r3"][cc * 128:(cc + 1) * 128, :],
                          resid[:, cc * TC:(cc + 1) * TC], "sts")
                b.wait("sync", "sts", b.val("sts"))
                b.barrier()
        if N_BLOCKS >= 4:
            _emit_ffn1(b, dr, bank, resid, ones_col, ones_row, prep_all, pool)
            b.barrier()

        for c in range(NC):
            b.dma("sync", out[c * 128:(c + 1) * 128, :],
                  resid[:, c * TC:(c + 1) * TC], "sts")
        b.wait("sync", "sts", b.val("sts"))

        with nc.Block() as block:
            @block.sync
            def _(e):
                for f in b.ops["sync"]:
                    f(e)

            @block.tensor
            def _(e):
                for f in b.ops["tensor"]:
                    f(e)

            @block.scalar
            def _(e):
                for f in b.ops["scalar"]:
                    f(e)

            @block.vector
            def _(e):
                for f in b.ops["vector"]:
                    f(e)

            @block.gpsimd
            def _(e):
                for f in b.ops["gpsimd"]:
                    f(e)

    return nc



def _emit_precast(b, dr, ropew, pool):
    stg = pool.alloc("pc_stg", [128, 8 * D], BF16)
    names = ("wqT", "wqswT", "wkT", "wkswT")
    for i, nm in enumerate(names):
        lg = b.dma("gpsimd", stg[:].rearrange("p (c n) -> p c n", c=8),
                   dr[nm][:].rearrange("(c p) n -> p c n", p=128), "ldg")
        b.wait("sync", "ldg", lg)
        st = b.dma("sync", ropew[i].rearrange("(c p) n -> p c n", p=128),
                   stg[:].rearrange("p (c n) -> p c n", c=8), "lds")
        b.wait("gpsimd", "lds", st)
    b.wait("sync", "lds", b.val("lds"))
    stg.free()



def _emit_prep(b, dr, prep_part, prep_all, bank, I_f32, pool):
    wraw = pool.alloc("pr_raw", [128, HIDDEN], F32)
    scth = pool.alloc("pr_scth", [128, 3], F32)
    gt = pool.alloc("pr_gt", [128, HIDDEN], F32)
    lt = pool.alloc("pr_lt", [128, HIDDEN], F32)
    stg = [pool.alloc(f"pr_stg{i}", [128, 128], BF16) for i in range(2)]
    sc_ = View(pool, scth.th, scth.off + 0, 128, 1)
    th = View(pool, scth.th, scth.off + 1, 128, 1)
    thn = View(pool, scth.th, scth.off + 2, 128, 1)
    rawd = DmaRing(b, "prraw", 1)
    stgd = DmaRing(b, "prstg", 2)
    bank_ring = Ring(b, 2)

    fcv = prep_part[:].rearrange("(d n) -> d n", n=512)
    woff = D * 512
    step = [0]
    prev_tt = [0]

    def prep_tile(src_ap, width, dst_fn, nblk):
        rsem, t_ld = rawd.load("sync", wraw[:, 0:width], src_ap, 0)
        b.wait("vector", rsem, t_ld)
        b.wait("vector", "tc", prev_tt[0])
        tr_ = b.run("vector", lambda e, w=width: e.reduce_sum(
            sc_[:], wraw[:, 0:w], axis=AX, apply_absolute_value=True), inc="vc")
        b.wait("scalar", "vc", tr_)
        ts_ = b.run("scalar", lambda e, w=width: e.mul(sc_[:], sc_[:], 1.0 / w),
                    inc="sc")
        b.wait("vector", "sc", ts_)
        tm_ = b.run("vector", lambda e: e.tensor_max(sc_[:], sc_[:], b.c_eps1e5[:]),
                    inc="vc")
        b.wait("scalar", "vc", tm_)
        b.run("scalar", lambda e: e.mul(th[:], sc_[:], 0.5))
        tg_ = b.run("scalar", lambda e: e.mul(thn[:], sc_[:], -0.5), inc="sc")
        b.wait("vector", "sc", tg_)
        b.run("vector", lambda e, w=width: e.tensor_scalar(
            gt[:, 0:w], wraw[:, 0:w], th[:], None, op0=ALU.is_gt))
        b.run("vector", lambda e, w=width: e.tensor_scalar(
            lt[:, 0:w], wraw[:, 0:w], thn[:], None, op0=ALU.is_lt))
        b.run("vector", lambda e, w=width: e.tensor_sub(gt[:, 0:w], gt[:, 0:w], lt[:, 0:w]))
        tv = b.run("vector", lambda e, w=width: e.tensor_scalar(
            gt[:, 0:w], gt[:, 0:w], sc_[:], None, op0=ALU.mult), inc="vc")
        rawd.consumed(0, "vc", tv)
        b.wait("tensor", "vc", tv)
        for j in range(nblk):
            i = step[0]; step[0] += 1
            bank_ring.acquire("tensor", i)
            tt = b.run("tensor", lambda e, jj=j, bi=i % 2: e.transpose(
                bank[bi][:, 0:128], gt[:, jj * 128:(jj + 1) * 128], I_f32[:]),
                inc="tc")
            b.wait("scalar", "tc", tt)
            lu = stgd.last[i % 2]
            if lu is not None:
                b.wait("scalar", lu[0], lu[1])
            ts = b.run("scalar", lambda e, bi=i % 2, si=i % 2: e.copy(
                stg[si][:], bank[bi][:, 0:128]), inc="sc")
            bank_ring.release(i, "sc", ts)
            b.wait("sync", "sc", ts)
            t_st = b.dma("sync", dst_fn(j), stg[i % 2][:], stgd.names[i % 2])
            stgd.uses[i % 2] += 1
            stgd.last[i % 2] = (stgd.names[i % 2], t_st)
            prev_tt[0] = tt

    for t in range(4):
        def dst_fc(j, tt=t):
            return fcv[j * 128:(j + 1) * 128, tt * 128:(tt + 1) * 128]
        prep_tile(dr["wfc_sl"][t * 128:(t + 1) * 128, :], D, dst_fc, 8)

    def dst_pr(j):
        return prep_part[woff + j * 128 * 128: woff + (j + 1) * 128 * 128].rearrange(
            "(p n) -> p n", p=128)
    prep_tile(dr["wproj_sl"][:], HIDDEN, dst_pr, 32)

    b.wait("gpsimd", "lds", b.val("lds"))
    for nm in stgd.names:
        b.wait("gpsimd", nm, b.val(nm))
    b.run("gpsimd", lambda e: e.collective_compute(
        "AllGather", ALU.bypass, replica_groups=[list(range(8))],
        ins=[prep_part[:]], outs=[prep_all[:]]), inc="cc")
    pool.free_all([wraw, scth, gt, lt, stg[0], stg[1]])



def _emit_ln(b, bank, ones_col, ones_row, pool, src, dst, tag):
    kind, sh = src
    scratch = []

    def al(n, shp, dt=F32):
        v = pool.alloc(f"{tag}_{n}", shp, dt)
        scratch.append(v)
        return v

    xin = [al(f"x{i}", [128, 512]) for i in range(2)] if kind == "hbm" else None
    xdr = DmaRing(b, f"{tag}xd", 2) if kind == "hbm" else None
    xsq = al("q0", [128, 512])
    fin_m = al("m", [1, 512])
    fin_ms = al("ms", [1, 512])
    fin_v = al("v", [1, 512])
    fin_r = al("r", [1, 512])
    fin_mr = al("mr", [1, 512])
    tmp = al("t", [128, 512])
    qring = Ring(b, 1)

    SB, QB, RB, MB = bank[4], bank[5], bank[6], bank[7]

    def src_tile(c, t):
        if kind == "hbm":
            return None
        return sh[:, c * TC + t * 512: c * TC + (t + 1) * 512]

    for t in range(NT):
        for c in range(NC):
            if kind == "hbm":
                lsem, tl = xdr.load("sync", xin[c % 2][:],
                                    sh[c * 128:(c + 1) * 128, t * 512:(t + 1) * 512], c)
                b.wait("scalar", lsem, tl)
                b.wait("tensor", lsem, tl)
                src_ap = xin[c % 2][:]
            else:
                src_ap = src_tile(c, t)
            qring.acquire("scalar", c)
            ts = b.run("scalar", lambda e, a=src_ap: e.activation(
                xsq[:], a, ACT.Square), inc="sc")
            b.run("tensor", lambda e, a=src_ap, first=(c == 0): e.matmul(
                SB[0:1, :], ones_col[:], a, start=first, stop=(c == NC - 1)))
            b.wait("tensor", "sc", ts)
            tq = b.run("tensor", lambda e, first=(c == 0): e.matmul(
                QB[0:1, :], ones_col[:], xsq[:], start=first, stop=(c == NC - 1)),
                inc="tc")
            qring.release(c, "tc", tq)
            if kind == "hbm":
                xdr.consumed(c, "tc", tq)
        b.wait("vector", "tc", b.val("tc"))
        b.run("vector", lambda e: e.tensor_scalar(fin_m[:], SB[0:1, :], 1.0 / D,
                                                  None, op0=ALU.mult))
        b.run("vector", lambda e: e.tensor_mul(fin_ms[:], fin_m[:], fin_m[:]))
        tv = b.run("vector", lambda e: e.scalar_tensor_tensor(
            fin_v[:], QB[0:1, :], 1.0 / D, fin_ms[:], op0=ALU.mult,
            op1=ALU.subtract), inc="vc")
        b.wait("scalar", "vc", tv)
        b.run("scalar", lambda e: e.activation(fin_v[:], fin_v[:], ACT.Ln,
                                               bias=b.c_eps1e5[0:1, :]))
        ts2 = b.run("scalar", lambda e: e.activation(fin_r[:], fin_v[:], ACT.Exp,
                                                     scale=-0.5), inc="sc")
        b.wait("vector", "sc", ts2)
        tv2 = b.run("vector", lambda e: e.tensor_mul(fin_mr[:], fin_m[:], fin_r[:]),
                    inc="vc")
        b.wait("tensor", "vc", tv2)
        b.run("tensor", lambda e: e.matmul(RB[:, :], ones_row[0:1, :], fin_r[:],
                                           start=True, stop=True))
        tb = b.run("tensor", lambda e: e.matmul(MB[:, :], ones_row[0:1, :], fin_mr[:],
                                                start=True, stop=True), inc="tc")
        b.wait("vector", "tc", tb)
        for c in range(NC):
            if kind == "hbm":
                lsem, tl = xdr.load("sync", xin[c % 2][:],
                                    sh[c * 128:(c + 1) * 128, t * 512:(t + 1) * 512], c)
                b.wait("vector", lsem, tl)
                src_ap = xin[c % 2][:]
            else:
                src_ap = src_tile(c, t)
            b.run("vector", lambda e, a=src_ap: e.tensor_mul(tmp[:], a, RB[:, :]))
            tv3 = b.run("vector", lambda e, c2=c, t2=t: e.tensor_sub(
                dst[:, c2 * TC + t2 * 512: c2 * TC + (t2 + 1) * 512],
                tmp[:], MB[:, :]), inc="vc")
            if kind == "hbm":
                xdr.consumed(c, "vc", tv3)
        b.wait("tensor", "vc", b.val("vc"))
    pool.free_all(scratch)



def _emit_vproj(b, dr, bank, V_sb, xln, kt0, pool):
    wv = pool.alloc("wv_sb", [128, 8 * D], BF16)
    tl = b.dma("gpsimd", wv[:].rearrange("p (c n) -> p c n", c=8),
               dr["wvT"][:].rearrange("(c p) n -> p c n", p=128), "ldg")
    b.wait("tensor", "ldg", tl)
    for kt in range(kt0, kt0 + 16):
        b.run("vector", lambda e, t=kt: e.memset(
            V_sb[:, t * 1040:(t + 1) * 1040].rearrange(
                "p (h n) -> p h n", h=NH)[:, :, HD:HD + 1], 1.0))
    vdone = b.run("vector", lambda e: e.engine_nop(), inc="vc")
    b.wait("scalar", "vc", vdone)

    bring = Ring(b, 4)
    for ktl in range(16):
        kt = kt0 + ktl
        for eh in range(2):
            i = ktl * 2 + eh
            bk = bank[i % 4]
            bring.acquire("tensor", i)
            for dc in range(NC):
                b.run("tensor", lambda e, t=ktl, h=eh, d=dc, bb=bk: e.matmul(
                    bb[:, :], xln[:, d * TC + t * 128: d * TC + (t + 1) * 128],
                    wv[:, d * D + h * 512: d * D + (h + 1) * 512],
                    start=(d == 0), stop=(d == NC - 1)),
                    inc=("tc" if dc == NC - 1 else None))
            tt = b.val("tc")
            b.wait("scalar", "tc", tt)
            ts = b.run("scalar", lambda e, k=kt, h=eh, bb=bk: e.copy(
                V_sb[:, k * 1040:(k + 1) * 1040].rearrange(
                    "p (hh n) -> p hh n", hh=NH)[:, h * 8:(h + 1) * 8, 0:HD],
                bb[:].rearrange("p (hh n) -> p hh n", hh=8)), inc="sc")
            bring.release(i, "sc", ts)
    wv.free()



def _emit_rope_proj(b, dr, bank, dst_sb, xln, dst_span, coff, pool, ropew,
                    wi_a, wi_b, cosd, sind, wring, paired=None):
    if wring < 2 and not paired:
        _emit_rope_proj_single(b, dr, bank, dst_sb, xln, dst_span, coff, pool,
                               ropew, wi_a, wi_b, cosd, sind)
        return
    tdt = F32 if wring >= 2 else BF16
    teng = "sync" if wring >= 2 else "gpsimd"
    tab = pool.alloc("rp_tab", [128, 4 * 512], tdt)
    wsl = [pool.alloc(f"rp_w{i}", [128, 2 * 8 * 128], BF16) for i in range(wring)]
    t1 = pool.alloc("rp_t1", [128, 512], BF16)

    tdr = DmaRing(b, f"rt{wi_a}", 1)
    wdr = DmaRing(b, f"rw{wi_a}", wring)
    pair_ring = Ring(b, 2)
    step = [0]

    for tp in range(NT // 2):
        t0 = 2 * tp
        tdr.load(teng, tab[:, 0:512], cosd[:, t0 * 512:(t0 + 1) * 512], tp)
        ls = None
        for k, srcap in enumerate((sind[:, t0 * 512:(t0 + 1) * 512],
                                   cosd[:, (t0 + 1) * 512:(t0 + 2) * 512],
                                   sind[:, (t0 + 1) * 512:(t0 + 2) * 512])):
            ls = b.dma(teng, tab[:, (k + 1) * 512:(k + 2) * 512], srcap,
                       tdr.names[0])
        b.wait("vector", tdr.names[0], ls)
        last_tv = None
        for ec in range(NC):
            i = step[0]; step[0] += 1
            wv = wsl[i % wring]
            wdr.load("sync", wv[:, 0:1024].rearrange("p (c n) -> p c n", c=8),
                     ropew[wi_a][:, ec * 128:(ec + 1) * 128].rearrange(
                         "(c p) n -> p c n", p=128), i)
            lw = b.dma("sync", wv[:, 1024:2048].rearrange("p (c n) -> p c n", c=8),
                       ropew[wi_b][:, ec * 128:(ec + 1) * 128].rearrange(
                           "(c p) n -> p c n", p=128), wdr.names[i % wring])
            b.wait("tensor", wdr.names[i % wring], lw)
            bs = [bank[(i % 2) * 4 + j] for j in range(4)]
            pair_ring.acquire("tensor", i)
            for dc in range(NC):
                for q in range(2):
                    b.run("tensor", lambda e, d=dc, bb=bs[q], w=wv, tt=t0 + q:
                          e.matmul(
                        bb[:, :], w[:, d * 128:(d + 1) * 128],
                        xln[:, d * TC + tt * 512: d * TC + (tt + 1) * 512],
                        start=(d == 0), stop=(d == NC - 1)))
            tcb = []
            for dc in range(NC):
                for q in range(2):
                    last = (dc == NC - 1)
                    tgt = b.run("tensor", lambda e, d=dc, bb=bs[2 + q], w=wv,
                                tt=t0 + q: e.matmul(
                        bb[:, :], w[:, 1024 + d * 128: 1024 + (d + 1) * 128],
                        xln[:, d * TC + tt * 512: d * TC + (tt + 1) * 512],
                        start=(d == 0), stop=last),
                        inc=("tc" if last else None))
                    if last:
                        tcb.append(tgt)
            for q in range(2):
                b.wait("vector", "tc", tcb[q])
                b.run("vector", lambda e, bb=bs[2 + q], q2=q: e.tensor_mul(
                    t1[:], bb[:, :], tab[:, (2 * q2 + 1) * 512:(2 * q2 + 2) * 512]))
                b.run("vector", lambda e, bb=bs[q], q2=q: e.tensor_mul(
                    bb[:, :], bb[:, :], tab[:, 2 * q2 * 512:(2 * q2 + 1) * 512]))
                tv = b.run("vector", lambda e, c=ec, tt3=t0 + q, bb=bs[q],
                           co=coff: e.tensor_add(
                    dst_sb[:, c * dst_span + co + tt3 * 512:
                           c * dst_span + co + (tt3 + 1) * 512],
                    bb[:, :], t1[:]), inc="vc")
            pair_ring.release(i, "vc", tv)
            wdr.consumed(i, "vc", tv)
            last_tv = tv
        tdr.consumed(tp, "vc", last_tv)
    pool.free_all([tab, t1] + wsl)


def _emit_rope_proj_single(b, dr, bank, dst_sb, xln, dst_span, coff, pool,
                           ropew, wi_a, wi_b, cosd, sind):
    tab = pool.alloc("rp_tab0", [128, 1024], F32)
    wsl = [pool.alloc("rp_w0", [128, 2 * 8 * 128], BF16)]
    t1 = pool.alloc("rp_t1", [128, 512], F32)

    tdr = DmaRing(b, f"rt{wi_a}", 1)
    wdr = DmaRing(b, f"rw{wi_a}", 1)
    pair_ring = Ring(b, 2)
    step = [0]

    for t in range(NT):
        tdr.load("sync", tab[:, 0:512], cosd[:, t * 512:(t + 1) * 512], t)
        ls = b.dma("sync", tab[:, 512:1024], sind[:, t * 512:(t + 1) * 512],
                   tdr.names[0])
        b.wait("vector", tdr.names[0], ls)
        last_tv = None
        for ec in range(NC):
            i = step[0]; step[0] += 1
            wv = wsl[0]
            wdr.load("sync", wv[:, 0:1024].rearrange("p (c n) -> p c n", c=8),
                     ropew[wi_a][:, ec * 128:(ec + 1) * 128].rearrange(
                         "(c p) n -> p c n", p=128), i)
            lw = b.dma("sync", wv[:, 1024:2048].rearrange("p (c n) -> p c n", c=8),
                       ropew[wi_b][:, ec * 128:(ec + 1) * 128].rearrange(
                           "(c p) n -> p c n", p=128), wdr.names[0])
            b.wait("tensor", wdr.names[0], lw)
            bA = bank[(i % 2) * 2]
            bB = bank[(i % 2) * 2 + 1]
            pair_ring.acquire("tensor", i)
            for dc in range(NC):
                b.run("tensor", lambda e, d=dc, bb=bA, w=wv, tt=t: e.matmul(
                    bb[:, :], w[:, d * 128:(d + 1) * 128],
                    xln[:, d * TC + tt * 512: d * TC + (tt + 1) * 512],
                    start=(d == 0), stop=(d == NC - 1)))
            for dc in range(NC):
                b.run("tensor", lambda e, d=dc, bb=bB, w=wv, tt=t: e.matmul(
                    bb[:, :], w[:, 1024 + d * 128: 1024 + (d + 1) * 128],
                    xln[:, d * TC + tt * 512: d * TC + (tt + 1) * 512],
                    start=(d == 0), stop=(d == NC - 1)),
                    inc=("tc" if dc == NC - 1 else None))
            tt2 = b.val("tc")
            b.wait("vector", "tc", tt2)
            b.run("vector", lambda e, bb=bB: e.tensor_mul(
                t1[:], bb[:, :], tab[:, 512:1024]))
            b.run("vector", lambda e, bb=bA: e.tensor_mul(
                bb[:, :], bb[:, :], tab[:, 0:512]))
            tv = b.run("vector", lambda e, c=ec, tt3=t, bb=bA, co=coff: e.tensor_add(
                dst_sb[:, c * dst_span + co + tt3 * 512:
                       c * dst_span + co + (tt3 + 1) * 512],
                bb[:, :], t1[:]), inc="vc")
            pair_ring.release(i, "vc", tv)
            wdr.consumed(i, "vc", tv)
            last_tv = tv
        tdr.consumed(t, "vc", last_tv)
    pool.free_all([tab, t1] + wsl)



def _emit_attention(b, dr, bank, K_sb, V_sb, Q_sb, attn_sb, I_bf,
                    rmask_sb, ones_row, pool):
    cmask_bf = pool.alloc("cmask", [128, 4 * 512], BF16)
    expP = [pool.alloc(f"at_e{i}", [128, 512], BF16) for i in range(4)]
    drow = b.drow_r
    bc_sb = [pool.alloc("at_bc0", [64, 512], F32)]
    lg = b.dma("gpsimd", cmask_bf[:].rearrange("p (r n) -> p r n", r=4),
               dr["cmask4"][:].rearrange("r p n -> p r n"), "ldg")
    b.wait("tensor", "ldg", lg)

    sco = [bank[0], bank[1], bank[2], bank[3]]
    for i in range(120):
        b.run("tensor", lambda e, bb=sco[i % 4]: e.matmul(
            bb[:, :], I_bf[:], cmask_bf[:, 0:512], start=True, stop=True))

    avb = [bank[4], bank[6]]
    bcb = [bank[5], bank[7]]
    sco_ring = Ring(b, 4)
    exp_ring = Ring(b, 4)
    av_ring = Ring(b, 2)
    bc_ring = Ring(b, 2)
    bcs_ring = Ring(b, 1)

    hq = [0]
    tb_prev = [0]
    pending = [None]
    for qt in range(NT):
        for h in range(NH):
            ec, rh = h // 2, (h % 2) * 64
            i_h = hq[0]; hq[0] += 1
            av = avb[i_h % 2]
            kts = list(range(0, 4 * (qt + 1))) + list(range(16, 32))
            nk = len(kts)
            exp_t = {}

            def emit_score(j):
                kt = kts[j]
                bk = sco[j % 4]
                diag = (kt < 16 and kt >= 4 * qt)
                sco_ring.acquire("tensor", j)
                if diag:
                    rel = kt - 4 * qt
                    b.run("tensor", lambda e, bb=bk, r=rel: e.matmul(
                        bb[:, :], I_bf[:], cmask_bf[:, r * 512:(r + 1) * 512],
                        start=True, stop=False))
                tsc = b.run("tensor", lambda e, bb=bk, k=kt, c=ec, r=rh, q=qt,
                            dg=diag: e.matmul(
                    bb[:, :],
                    K_sb[r:r + 64, c * S + k * 128: c * S + (k + 1) * 128],
                    Q_sb[r:r + 64, c * TC + q * 512: c * TC + (q + 1) * 512],
                    start=(not dg), stop=True), inc="tc")
                b.wait("scalar", "tc", tsc)
                exp_ring.acquire("scalar", j)
                rect = kt >= 16
                te = b.run("scalar", lambda e, bb=bk, sl=j % 4, rc=rect: e.activation(
                    expP[sl][:], bb[:, :], ACT.Exp,
                    bias=(rmask_sb[:] if rc else 0.0), scale=0.125), inc="sc")
                exp_t[j] = te
                sco_ring.release(j, "sc", te)

            def emit_av(j):
                b.wait("tensor", "sc", exp_t[j])
                if j == 0:
                    av_ring.acquire("tensor", i_h)
                ta = b.run("tensor", lambda e, k=kts[j], hh=h, sl=j % 4,
                           aa=av: e.matmul(
                    aa[0:65, :],
                    V_sb[:, k * 1040 + hh * 65: k * 1040 + (hh + 1) * 65],
                    expP[sl][:],
                    start=(j == 0), stop=(j == nk - 1)), inc="tc")
                exp_ring.release(j, "tc", ta)
                return ta

            for j in range(4):
                emit_score(j)
            for j in range(4, nk):
                emit_av(j - 4)
                if j == 10 and pending[0] is not None:
                    pending[0]()
                    pending[0] = None
                emit_score(j)
            for j in range(nk - 4, nk - 1):
                emit_av(j)
            ta = emit_av(nk - 1)

            b.wait("vector", "tc", max(ta, tb_prev[0]))

            def _recip(e, aa):
                with b.nc.allow_low_precision(reason="float32r is fp32"):
                    return e.reciprocal(drow[0:1, :], aa[64:65, :])
            trc = b.run("vector", lambda e, aa=av, f=_recip: f(e, aa), inc="vc")

            def tail(ih=i_h, aa=av, c=ec, r=rh, q=qt, trc=trc):
                bc_ring.acquire("tensor", ih)
                b.wait("tensor", "vc", trc)
                tb = b.run("tensor", lambda e, bb=bcb[ih % 2]: e.matmul(
                    bb[0:64, :], b.ones_row_r[0:1, 0:64], drow[0:1, :],
                    start=True, stop=True), inc="tc")
                tb_prev[0] = tb
                bcs_ring.acquire("vector", ih)
                b.wait("vector", "tc", tb)
                tcp = b.run("vector", lambda e, bb=bcb[ih % 2], s2=bc_sb[0]:
                            e.tensor_copy(s2[:], bb[0:64, :]), inc="vc")
                bc_ring.release(ih, "vc", tcp)
                tn = b.run("vector", lambda e, s2=bc_sb[0]: e.tensor_mul(
                    attn_sb[r:r + 64, c * TC + q * 512: c * TC + (q + 1) * 512],
                    aa[0:64, :], s2[:]), inc="vc")
                av_ring.release(ih, "vc", tn)
                bcs_ring.release(ih, "vc", tn)

            pending[0] = tail
    pending[0]()
    pending[0] = None
    pool.free_all([cmask_bf] + expP + bc_sb)



def _emit_oproj(b, dr, bank, attn_sb, resid, pool):
    wo = pool.alloc("wo_sb", [128, 8 * D], BF16)
    xbuf = [pool.alloc(f"ox{i}", [128, 512], F32) for i in range(4)]
    tl = b.dma("gpsimd", wo[:].rearrange("p (c n) -> p c n", c=8),
               dr["woT"][:].rearrange("(c p) n -> p c n", p=128), "ldg")
    b.wait("tensor", "ldg", tl)
    bring = Ring(b, 2)
    xdr = DmaRing(b, "oprx", 4)
    for ec in range(NC):
        bs = [bank[(ec % 2) * 4 + j] for j in range(4)]
        xs = []
        for t in range(NT):
            xs.append(xdr.load("sync", xbuf[t][:],
                               dr["x_own"][ec * 128:(ec + 1) * 128,
                                           t * 512:(t + 1) * 512], ec * NT + t))
        bring.acquire("tensor", ec)
        tcb = []
        for dc in range(NC):
            for t in range(NT):
                last = (dc == NC - 1)
                tgt = b.run("tensor", lambda e, d=dc, c=ec, tt=t, bb=bs[t]:
                            e.matmul(
                    bb[:, :], wo[:, d * D + c * 128: d * D + (c + 1) * 128],
                    attn_sb[:, d * TC + tt * 512: d * TC + (tt + 1) * 512],
                    start=(d == 0), stop=(d == NC - 1)),
                    inc=("tc" if last else None))
                if last:
                    tcb.append(tgt)
        tv = None
        for t in range(NT):
            b.wait("vector", "tc", tcb[t])
            b.wait("vector", xs[t][0], xs[t][1])
            tv = b.run("vector", lambda e, c=ec, tt3=t, bb=bs[t], xb=xbuf[t]:
                       e.tensor_add(resid[:, c * TC + tt3 * 512:
                                          c * TC + (tt3 + 1) * 512],
                                    xb[:], bb[:, :]), inc="vc")
            xdr.consumed(ec * NT + t, "vc", tv)
        bring.release(ec, "vc", tv)
    pool.free_all([wo] + xbuf)



def _emit_ffn0(b, dr, bank, resid, ones_col, ones_row, pool):
    xln2 = pool.alloc("xln2", [128, NC * TC], BF16)
    _emit_ln(b, bank, ones_col, ones_row, pool,
             src=("sbuf", resid), dst=xln2, tag="ln1")
    b.barrier()

    wsb = pool.alloc("W_sb", [128, 8 * D], BF16)
    vr = pool.alloc("vr_sb", [128, 8 * 32], BF16)
    ct = pool.alloc("ct_sb", [32, D], BF16)
    bsb = pool.alloc("b_sb", [128, 8], F32)
    r1 = pool.alloc("r1_sb", [32, 4 * 512], BF16)
    b.dma("gpsimd", wsb[:].rearrange("p (c n) -> p c n", c=8),
          dr["WT"][:].rearrange("(c p) n -> p c n", p=128), "ldg")
    b.dma("gpsimd", vr[:].rearrange("p (c n) -> p c n", c=8),
          dr["Vr"][:].rearrange("(c p) n -> p c n", p=128), "ldg")
    b.dma("gpsimd", ct[:], dr["CT"][:], "ldg")
    for c in range(NC):
        b.dma("sync", bsb[:, c:c + 1], dr["bcol"][c * 128:(c + 1) * 128, :], "lds")
    b.wait("tensor", "ldg", b.val("ldg"))
    b.wait("vector", "lds", b.val("lds"))

    ts = None
    for t in range(NT):
        rb = bank[6]
        for dc in range(NC):
            b.run("tensor", lambda e, d=dc, tt=t: e.matmul(
                rb[0:32, :], vr[:, d * 32:(d + 1) * 32],
                xln2[:, d * TC + tt * 512: d * TC + (tt + 1) * 512],
                start=(d == 0), stop=(d == NC - 1)),
                inc=("tc" if dc == NC - 1 else None))
        tt1 = b.val("tc")
        b.wait("scalar", "tc", tt1)
        ts = b.run("scalar", lambda e, tt=t: e.copy(
            r1[:, tt * 512:(tt + 1) * 512], rb[0:32, :]), inc="sc")
        b.wait("tensor", "sc", ts)

    bring = Ring(b, 2)
    for ec in range(NC):
        bs = [bank[(ec % 2) * 4 + j] for j in range(4)]
        bring.acquire("tensor", ec)
        for t in range(NT):
            b.run("tensor", lambda e, c=ec, tt=t, bb=bs[t]: e.matmul(
                bb[:, :], ct[:, c * 128:(c + 1) * 128],
                r1[:, tt * 512:(tt + 1) * 512],
                start=True, stop=False))
        tcb = []
        for dc in range(NC):
            for t in range(NT):
                last = (dc == NC - 1)
                tgt = b.run("tensor", lambda e, d=dc, c=ec, tt2=t, bb=bs[t]:
                            e.matmul(
                    bb[:, :], wsb[:, d * D + c * 128: d * D + (c + 1) * 128],
                    xln2[:, d * TC + tt2 * 512: d * TC + (tt2 + 1) * 512],
                    start=False, stop=(d == NC - 1)),
                    inc=("tc" if last else None))
                if last:
                    tcb.append(tgt)
        tv = None
        for t in range(NT):
            b.wait("vector", "tc", tcb[t])
            tv = b.run("vector", lambda e, c=ec, tt4=t, bb=bs[t]:
                       e.scalar_tensor_tensor(
                resid[:, c * TC + tt4 * 512: c * TC + (tt4 + 1) * 512], bb[:, :],
                bsb[:, c:c + 1],
                resid[:, c * TC + tt4 * 512: c * TC + (tt4 + 1) * 512],
                op0=ALU.add, op1=ALU.add), inc="vc")
        bring.release(ec, "vc", tv)
    pool.free_all([xln2, wsb, vr, ct, bsb, r1])



def _emit_l1attn(b, dr, bank, pairs, resid, ones_col, ones_row, csel_sb,
                 carry_d, carry_all, pool, dbg=None):
    xln3 = pool.alloc("xln3", [128, NC * TC], BF16)
    _emit_ln(b, bank, ones_col, ones_row, pool,
             src=("sbuf", resid), dst=xln3, tag="ln2")
    b.barrier()

    bd = {n: pool.alloc(f"{n}_sb", [128, 8 * 128], BF16)
          for n in ("bdf", "bdi", "bdh")}
    F_sb = pool.alloc("F_sb", [128, NC * TC], F32)
    v_sb = pool.alloc("v_sb", [128, TC], F32)
    h_sb = pool.alloc("h_sb", [128, TC], F32)
    zero_sb = pool.alloc("zero_sb", [128, 1], F32)
    t0 = pool.alloc("g_t0", [128, 1024], F32)
    t1 = pool.alloc("g_t1", [128, 1024], F32)
    t2_ = pool.alloc("g_t2", [128, 1024], F32)
    t3 = pool.alloc("g_t3", [128, 1024], F32)
    mk = pool.alloc("g_mk", [128, 1024], U8)
    carry_sb = pool.alloc("carry_sb", [128, 8], F32)
    cg_sb = pool.alloc("cg_sb", [8, D], F32)
    hin = pool.alloc("hin_sb", [128, 8], F32)

    for n in ("bdf", "bdi", "bdh"):
        b.dma("gpsimd", bd[n][:].rearrange("p (c n2) -> p c n2", c=8),
              dr[n][:].rearrange("c p n2 -> p c n2"), "ldg")
    b.run("vector", lambda e: e.memset(zero_sb[:], 0.0))
    zero_bc = zero_sb[:].broadcast_to([128, TC])
    b.wait("tensor", "ldg", b.val("ldg"))

    gate_ring = Ring(b, 1)
    for c in range(NC):
        for t2 in range(2):
            gate_ring.acquire("tensor", 0)
            for gi, gname in enumerate(("bdf", "bdi", "bdh")):
                for sub in range(2):
                    b.run("tensor", lambda e, cc=c, tt=t2 * 2 + sub, gb=bd[gname],
                          pi_=pairs[gi], s2=sub: e.matmul(
                        pi_[:, s2 * 512:(s2 + 1) * 512],
                        gb[:, cc * 128:(cc + 1) * 128],
                        xln3[:, cc * TC + tt * 512: cc * TC + (tt + 1) * 512],
                        start=True, stop=True),
                        inc=("tc" if (gi == 2 and sub == 1) else None))
            tg = b.val("tc")
            b.wait("scalar", "tc", tg)
            b.wait("vector", "tc", tg)
            pf = pairs[0][:, 0:1024]
            pi = pairs[1][:, 0:1024]
            ph_ = pairs[2][:, 0:1024]
            sl = slice(t2 * 1024, (t2 + 1) * 1024)
            b.run("scalar", lambda e, p=pf: e.activation(t0[:], p, ACT.Exp, scale=-1.0))
            b.run("scalar", lambda e: e.activation(t0[:], t0[:], ACT.Ln, bias=1.0))
            b.run("scalar", lambda e, p=pi: e.activation(t1[:], p, ACT.Exp, scale=-1.0))
            ts1 = b.run("scalar", lambda e: e.activation(t1[:], t1[:], ACT.Ln, bias=1.0),
                        inc="sc")
            b.wait("vector", "sc", ts1)
            tv1 = b.run("vector", lambda e: e.tensor_sub(t2_[:], t0[:], t1[:]), inc="vc")
            b.wait("scalar", "vc", tv1)
            b.run("scalar", lambda e, cc=c, ss=sl: e.activation(
                F_sb[:, cc * TC + ss.start: cc * TC + ss.stop], t2_[:],
                ACT.Sigmoid, scale=-1.0))
            b.run("scalar", lambda e: e.activation(t2_[:], t2_[:], ACT.Sigmoid))
            ts3 = b.run("scalar", lambda e, p=ph_: e.activation(
                t0[:], p, ACT.Sigmoid), inc="sc")
            b.wait("vector", "sc", ts3)
            b.run("vector", lambda e, p=ph_: e.tensor_scalar(mk[:], p, 0.0, None,
                                                             op0=ALU.is_ge))
            b.run("vector", lambda e, p=ph_: e.tensor_scalar(t1[:], p, 0.5 + EPS,
                                                             None, op0=ALU.add))
            b.run("vector", lambda e: e.select(t3[:], mk[:], t1[:], t0[:]))
            tv3 = b.run("vector", lambda e, ss=sl: e.tensor_mul(
                v_sb[:, ss], t2_[:], t3[:]), inc="vc")
            gate_ring.release(0, "vc", tv3)
        tv4 = b.run("vector", lambda e: e.engine_nop(), inc="vc")
        b.wait("vector", "vc", tv4)
        b.run("vector", lambda e, cc=c: e.tensor_tensor_scan(
            h_sb[:], F_sb[:, cc * TC:(cc + 1) * TC], v_sb[:], 0.0,
            op0=ALU.mult, op1=ALU.add))
        b.run("vector", lambda e, cc=c: e.tensor_add(
            resid[:, cc * TC:(cc + 1) * TC], resid[:, cc * TC:(cc + 1) * TC], h_sb[:]))
        b.run("vector", lambda e, cc=c: e.tensor_copy(
            carry_sb[:, cc:cc + 1], h_sb[:, TC - 1:TC]), inc="vc")
    b.wait("sync", "vc", b.val("vc"))
    if dbg is not None and "F" in dbg:
        import kernel as _k
        if _k.DEV_DUMP:
            for cc in range(NC):
                b.dma("sync", dbg["F"][cc * 128:(cc + 1) * 128, :],
                      F_sb[:, cc * TC:(cc + 1) * TC], "sts")
                b.dma("gpsimd", dbg["x3"][cc * 128:(cc + 1) * 128, :],
                      xln3[:, cc * TC:(cc + 1) * TC], "ldg")
            b.wait("sync", "sts", b.val("sts"))
            b.wait("gpsimd", "ldg", b.val("ldg"))
    for c in range(NC):
        b.dma("sync", carry_d[c * 128:(c + 1) * 128], carry_sb[:, c:c + 1], "lds")
    b.wait("gpsimd", "lds", b.val("lds"))
    tcc = b.run("gpsimd", lambda e: e.collective_compute(
        "AllGather", ALU.bypass, replica_groups=[list(range(8))],
        ins=[carry_d[:]], outs=[carry_all[:]]), inc="cc")
    b.wait("sync", "cc", tcc)
    tcg = b.dma("sync", cg_sb[:], carry_all[:], "lds")
    b.wait("tensor", "lds", tcg)
    for c in range(NC):
        b.run("tensor", lambda e, cc=c: e.matmul(
            bank[5][:, cc:cc + 1], cg_sb[:, cc * 128:(cc + 1) * 128], csel_sb[:],
            start=True, stop=True), inc=("tc" if c == NC - 1 else None))
    tm = b.val("tc")
    b.wait("vector", "tc", tm)
    tvh = b.run("vector", lambda e: e.tensor_copy(hin[:], bank[5][:, 0:8]), inc="vc")
    b.wait("vector", "vc", tvh)
    for c in range(NC):
        b.run("vector", lambda e, cc=c: e.tensor_tensor_scan(
            h_sb[:], F_sb[:, cc * TC:(cc + 1) * TC], zero_bc, 1.0,
            op0=ALU.mult, op1=ALU.add))
        b.run("vector", lambda e, cc=c: e.scalar_tensor_tensor(
            resid[:, cc * TC:(cc + 1) * TC], h_sb[:], hin[:, cc:cc + 1],
            resid[:, cc * TC:(cc + 1) * TC], op0=ALU.mult, op1=ALU.add))
    pool.free_all([xln3, bd["bdf"], bd["bdi"], bd["bdh"], F_sb, v_sb, h_sb,
                   zero_sb, t0, t1, t2_, t3, mk, carry_sb, cg_sb, hin])



def _emit_ffn1(b, dr, bank, resid, ones_col, ones_row, prep_all, pool, dbg=None):
    xln4 = pool.alloc("xln4", [128, NC * TC], BF16)
    _emit_ln(b, bank, ones_col, ones_row, pool,
             src=("sbuf", resid), dst=xln4, tag="ln3")
    b.barrier()

    vt = pool.alloc("vt_sb", [128, 8 * RANK], BF16)
    fb = pool.alloc("fb_sb", [RANK, RANK], BF16)
    ut = pool.alloc("ut_sb", [RANK, HIDDEN], BF16)
    g16 = pool.alloc("g16_sb", [RANK, 1], F32)
    s1 = pool.alloc("s1_sb", [RANK, 512], BF16)
    s2 = pool.alloc("s2_sb", [RANK, 512], BF16)
    s1x = pool.alloc("s1x_sb", [RANK, 512], BF16)
    rsq = pool.alloc("rsq_sb", [128, 32 * 1024], BF16)
    relu_t = pool.alloc("relu_t", [128, 512], F32)
    wcol = [pool.alloc(f"wcol{i}", [128, 8 * 128], BF16) for i in range(2)]
    wpb = [pool.alloc(f"wpb{i}", [128, 32 * 128], BF16) for i in range(2)]

    b.dma("gpsimd", vt[:].rearrange("p (c n) -> p c n", c=8),
          dr["vT"][:].rearrange("(c p) n -> p c n", p=128), "ldg")
    b.dma("gpsimd", fb[:], dr["fbT"][:], "ldg")
    b.dma("gpsimd", ut[:], dr["uT"][:], "ldg")
    tg = b.dma("sync", g16[:], dr["gate16"][:], "lds")
    b.wait("tensor", "ldg", b.val("ldg"))
    b.wait("vector", "lds", tg)

    import kernel as _k
    if dbg is not None and _k.DEV_DUMP:
        vscr = pool.alloc("vscr", [128, TC], F32)
        for cc in range(NC):
            b.dma("gpsimd", dbg["x4"][cc * 128:(cc + 1) * 128, :],
                  xln4[:, cc * TC:(cc + 1) * TC], "ldg")
            tvv = b.run("vector", lambda e, c2=cc: e.tensor_copy(
                vscr[:], xln4[:, c2 * TC:(c2 + 1) * TC]), inc="vc")
            b.wait("sync", "vc", tvv)
            b.dma("sync", dbg["x3"][cc * 128:(cc + 1) * 128, :], vscr[:], "sts")
            b.wait("vector", "sts", b.val("sts"))
        b.wait("gpsimd", "ldg", b.val("ldg"))
        b.wait("sync", "sts", b.val("sts"))
        vscr.free()
    wcd = DmaRing(b, "f1wc", 2)
    wpd = DmaRing(b, "f1wp", 2)
    fc_ring = Ring(b, 2)
    pj_ring = Ring(b, 2)

    s2ab = [s1, s2]
    for qp in range(2):
        q0 = 2 * qp
        rb = bank[0]
        fc_ring.acquire("tensor", 0)
        fc_ring.acquire("tensor", 1)
        for qq in range(2):
            for dc in range(NC):
                b.run("tensor", lambda e, d=dc, q=q0 + qq: e.matmul(
                    rb[0:RANK, :], vt[:, d * RANK:(d + 1) * RANK],
                    xln4[:, d * TC + q * 512: d * TC + (q + 1) * 512],
                    start=(d == 0), stop=(d == NC - 1)),
                    inc=("tc" if dc == NC - 1 else None))
            tt = b.val("tc")
            b.wait("scalar", "tc", tt)
            ts = b.run("scalar", lambda e: e.copy(s1x[:], rb[0:RANK, :]), inc="sc")
            b.wait("tensor", "sc", ts)
            b.run("tensor", lambda e: e.matmul(rb[0:RANK, :], fb[:], s1x[:],
                                               start=True, stop=True), inc="tc")
            tt2 = b.val("tc")
            b.wait("vector", "tc", tt2)
            tv = b.run("vector", lambda e, s_=s2ab[qq]: e.tensor_scalar(
                s_[:], rb[0:RANK, :], g16[:], None, op0=ALU.mult), inc="vc")
            b.wait("tensor", "vc", tv)

        for hc in range(32):
            i = qp * 32 + hc
            r_ = (hc * 128) // 512
            lc = (hc * 128) % 512
            wsem, lw = wcd.load(
                "sync", wcol[i % 2][:].rearrange("p (c n) -> p c n", c=8),
                prep_all[r_, 0:D * 512].rearrange("(d n) -> d n", n=512)[
                    :, lc:lc + 128].rearrange("(c p) n -> p c n", p=128), i)
            fc_ring.acquire("tensor", hc)
            bkA = bank[(hc % 2) * 2]
            bkB = bank[(hc % 2) * 2 + 1]
            b.run("tensor", lambda e, h2=hc, bb=bkA: e.matmul(
                bb[:, :], ut[:, h2 * 128:(h2 + 1) * 128], s2ab[0][:],
                start=True, stop=False))
            b.run("tensor", lambda e, h2=hc, bb=bkB: e.matmul(
                bb[:, :], ut[:, h2 * 128:(h2 + 1) * 128], s2ab[1][:],
                start=True, stop=False))
            b.wait("tensor", wsem, lw)
            for dc in range(NC):
                for qq in range(2):
                    b.run("tensor",
                          lambda e, d=dc, q=q0 + qq, wi=i % 2,
                          bb=(bkA if qq == 0 else bkB): e.matmul(
                              bb[:, :], wcol[wi][:, d * 128:(d + 1) * 128],
                              xln4[:, d * TC + q * 512: d * TC + (q + 1) * 512],
                              start=False, stop=(d == NC - 1)),
                          inc=("tc" if dc == NC - 1 and qq == 1 else None))
            tt3 = b.val("tc")
            b.wait("scalar", "tc", tt3)
            for qq, bb in ((0, bkA), (1, bkB)):
                b.run("scalar", lambda e, bb=bb: e.activation(
                    relu_t[:], bb[:, :], ACT.Relu))
                tv2 = b.run("scalar", lambda e, h2=hc, q2=qq: e.activation(
                    rsq[:, h2 * 1024 + q2 * 512: h2 * 1024 + (q2 + 1) * 512],
                    relu_t[:], ACT.Square), inc="sc")
            fc_ring.release(hc, "sc", tv2)
            wcd.consumed(i, "sc", tv2)
        rsq_done = b.val("sc")

        for dco in range(NC):
            i = qp * NC + dco
            woff = D * 512
            psem, lw = wpd.load(
                "sync", wpb[i % 2][:].rearrange("p (c n) -> p c n", c=32),
                prep_all[dco, woff:woff + HIDDEN * 128].rearrange(
                    "(c p n) -> p c n", p=128, n=128), i)
            b.wait("tensor", psem, lw)
            b.wait("tensor", "sc", rsq_done)
            pj_ring.acquire("tensor", dco)
            bkA = bank[4 + (dco % 2) * 2]
            bkB = bank[5 + (dco % 2) * 2]
            for hc in range(32):
                for qq in range(2):
                    b.run("tensor",
                          lambda e, h2=hc, wi=i % 2, q2=qq,
                          bb=(bkA if qq == 0 else bkB): e.matmul(
                              bb[:, :], wpb[wi][:, h2 * 128:(h2 + 1) * 128],
                              rsq[:, h2 * 1024 + q2 * 512:
                                  h2 * 1024 + (q2 + 1) * 512],
                              start=(h2 == 0), stop=(h2 == 31)),
                          inc=("tc" if hc == 31 and qq == 1 else None))
            tt4 = b.val("tc")
            b.wait("vector", "tc", tt4)
            for qq, bb in ((0, bkA), (1, bkB)):
                tv3 = b.run("vector", lambda e, d=dco, q=q0 + qq, bb=bb:
                            e.tensor_add(
                    resid[:, d * TC + q * 512: d * TC + (q + 1) * 512],
                    resid[:, d * TC + q * 512: d * TC + (q + 1) * 512],
                    bb[:, :]), inc="vc")
            pj_ring.release(dco, "vc", tv3)
            wpd.consumed(i, "vc", tv3)
    pool.free_all([xln4, vt, fb, ut, g16, s1x, s1, s2, rsq, relu_t]
                  + wcol + wpb)



def kernel(**inputs):
    sh, per = stage_inputs(inputs)
    nc = build_program()
    in_maps = [{**sh, **per[c]} for c in range(8)]
    res = run_bass_kernel_spmd(nc, in_maps, core_ids=list(range(8)))
    out = np.empty((4, S, D), np.float32)
    for c in range(8):
        seq, half = c // 2, c % 2
        out[seq, half * TC:(half + 1) * TC, :] = res.results[c]["y"].T
    return out
